# version 09
# baseline (speedup 1.0000x reference)
import math
import numpy as np
import ml_dtypes
from contextlib import ExitStack

import concourse.bass as bass
import concourse.mybir as mybir
from concourse.bass_utils import run_bass_kernel_spmd

F32 = mybir.dt.float32
BF16 = mybir.dt.bfloat16
AF = mybir.ActivationFunctionType
ALU = mybir.AluOpType
AX = mybir.AxisListType

D = 2048
ND = D // 128
NEG = -1e30
INV_SQRT_DK = 1.0 / math.sqrt(128.0)


class Ctx:
    def __init__(self, nc, es, T):
        self.nc = nc
        self.es = es
        self.scopes = [es]
        self.T = T
        self.NT = T // 128
        self.NB = T // 512
        self._sems = {}
        self._cnt = {}
        self._uid = 0
        self.const = {}
        self._cc_tile = es.enter_context(nc.sbuf_tensor("constcols", [128, 16], F32))
        self._cc_n = 0

    def push_scope(self):
        st = ExitStack()
        self.scopes.append(st)
        return st

    def pop_scope(self):
        self.scopes.pop().close()

    def uid(self):
        self._uid += 1
        return self._uid

    def sem(self, role):
        if role not in self._sems:
            self._sems[role] = self.es.enter_context(self.nc.semaphore(role))
            self._cnt[role] = 0
        return self._sems[role]

    def base(self, role):
        self.sem(role)
        return self._cnt[role]

    def bump(self, role, k):
        self.sem(role)
        self._cnt[role] += k
        return self._cnt[role]

    def fence(self, e_, ins, role):
        n = self.bump(role, 1)
        ins.then_inc(self.sem(role), 1)
        e_.wait_ge(self.sem(role), n)

    def sb(self, name, shape, dtype):
        return self.scopes[-1].enter_context(self.nc.sbuf_tensor(name, shape, dtype))

    def psum(self, name, shape, dtype=F32):
        return self.scopes[-1].enter_context(self.nc.psum_tensor(name, shape, dtype))

    def inline(self, name, arr):
        if name not in self.const:
            self.const[name] = self.nc.inline_tensor(np.ascontiguousarray(arr), name)
        return self.const[name]

    def constcol(self, value):
        key = ("cc", float(value))
        if key not in self.const:
            idx = self._cc_n
            self._cc_n += 1
            assert idx < 16
            with self.nc.Block() as blk:
                @blk.gpsimd
                def _(e_):
                    e_.memset(self._cc_tile[:, idx:idx + 1], float(value))
            self.const[key] = self._cc_tile[:, idx:idx + 1]
        return self.const[key]


def chunked(hbm_2d, p=128):
    return hbm_2d.rearrange("(c p) t -> p c t", p=p)


def ceil_div(a, b):
    return (a + b - 1) // b


def _scoped(fn):
    def wrap(C, *a, **kw):
        C.push_scope()
        try:
            return fn(C, *a, **kw)
        finally:
            C.pop_scope()
    return wrap


@_scoped
def ph_layernorm(C, x_hbm, ln_sbuf, post_tb_hook=None):
    nc, T = C.nc, C.T
    u = C.uid()
    xs = [C.sb(f"ln_xs{u}_0", [128, ND, 512], F32)]
    xcb = C.sb(f"ln_xcb{u}", [128, ND, 512], BF16)
    xsq = C.sb(f"ln_xsq{u}", [128, ND, 512], BF16)
    ones = C.sb(f"ln_one{u}", [128, 1], BF16)
    onesf = C.sb(f"ln_onef{u}", [1, 128], F32)
    rows = C.sb(f"ln_rows{u}", [1, 4, 512], F32)
    nmrow = C.sb(f"ln_nm{u}", [1, 512], F32)
    rb = C.sb(f"ln_rb{u}", [128, 512], F32)
    nmb = C.sb(f"ln_nmb{u}", [128, 512], F32)
    tmp = C.sb(f"ln_tmp{u}", [128, 512], F32)
    pstat = C.psum(f"ln_ps{u}", [128, 1024], F32)
    pbc = C.psum(f"ln_pb{u}", [128, 1024], F32)

    eps_ap = C.constcol(1e-5)
    with nc.Block() as blk:
        @blk.gpsimd
        def _(e_):
            e_.memset(ones[:, :], 1.0)
            e_.memset(onesf[:, :], 1.0)

    for tb in range(C.NB):
        buf = xs[0]
        l0b = C.base("l0")
        with nc.Block() as blk:
            @blk.sync
            def _(e_):
                e_.dma_start(buf[:, :, :],
                             chunked(x_hbm[:, tb * 512:(tb + 1) * 512]),
                             ).then_inc(C.sem("l0"), 16)
            @blk.vector
            def _(e_):
                e_.wait_ge(C.sem("l0"), l0b + 16)
                e_.tensor_copy(xcb[:, :, :], buf[:, :, :])
            @blk.scalar
            def _(e_):
                e_.wait_ge(C.sem("l0"), l0b + 16)
                e_.activation(xsq[:, :, :], buf[:, :, :], AF.Square, bias=0.0, scale=1.0)
        C.bump("l0", 16)
        b_mm = C.base("mm")
        b_x1 = C.base("x1")
        b_ev = C.base("ev")
        with nc.Block() as blk:
            @blk.tensor
            def _(e_):
                for c in range(ND):
                    ins = e_.matmul(pstat[0:1, 0:512], ones[:, :], xcb[:, c, :],
                                    start=(c == 0), stop=(c == ND - 1))
                ins.then_inc(C.sem("mm"), 1)
                for c in range(ND):
                    ins = e_.matmul(pstat[0:1, 512:1024], ones[:, :], xsq[:, c, :],
                                    start=(c == 0), stop=(c == ND - 1))
                ins.then_inc(C.sem("mm"), 1)
                e_.wait_ge(C.sem("x1"), b_x1 + 2)
                e_.matmul(pbc[:, 0:512], onesf[:, :], rows[:, 3, :], start=True,
                          stop=True).then_inc(C.sem("mm"), 1)
                e_.matmul(pbc[:, 512:1024], onesf[:, :], nmrow[:, :], start=True,
                          stop=True).then_inc(C.sem("mm"), 1)
            @blk.scalar
            def _(e_):
                e_.wait_ge(C.sem("x1"), b_x1 + 1)
                e_.activation(rows[:, 1, :], rows[:, 2, :], AF.Sqrt,
                              bias=eps_ap[0:1, :], scale=1.0).then_inc(C.sem("ev"), 1)
            @blk.vector
            def _(e_):
                e_.wait_ge(C.sem("mm"), b_mm + 2)
                C.fence(e_, e_.tensor_scalar(rows[:, 0, :], pstat[0:1, 0:512],
                                             1.0 / D, None, ALU.mult), "fv")
                C.fence(e_, e_.tensor_scalar(rows[:, 1, :], pstat[0:1, 512:1024],
                                             1.0 / D, None, ALU.mult), "fv")
                C.fence(e_, e_.tensor_mul(rows[:, 2, :], rows[:, 0, :],
                                          rows[:, 0, :]), "fv")
                e_.tensor_sub(rows[:, 2, :], rows[:, 1, :], rows[:, 2, :]).then_inc(
                    C.sem("x1"), 1)
                e_.wait_ge(C.sem("ev"), b_ev + 1)
                C.fence(e_, e_.reciprocal(rows[:, 3, :], rows[:, 1, :]), "fv")
                e_.tensor_mul(nmrow[:, :], rows[:, 0, :], rows[:, 3, :]).then_inc(
                    C.sem("x1"), 1)
                e_.wait_ge(C.sem("mm"), b_mm + 4)
                C.fence(e_, e_.tensor_copy(rb[:, :], pbc[:, 0:512]), "fv")
                C.fence(e_, e_.tensor_copy(nmb[:, :], pbc[:, 512:1024]), "fv")
                for c in range(ND):
                    C.fence(e_, e_.tensor_mul(tmp[:, :], buf[:, c, :], rb[:, :]), "fv")
                    C.fence(e_, e_.tensor_sub(
                        ln_sbuf[:, c, tb * 512:(tb + 1) * 512], tmp[:, :], nmb[:, :]), "fv")
        C.bump("mm", 4)
        C.bump("x1", 2)
        C.bump("ev", 1)
        if post_tb_hook is not None:
            post_tb_hook(tb)


@_scoped
def ph_proj(C, src, w_hbm, out_hbm, *, bias_hbm=None, func=AF.Identity,
            scale=1.0, bias_scale=None, w_hook=None, src_in_hbm=False,
            out_row0=0, bias_row=None):
    nc, T = C.nc, C.T
    Din = w_hbm.shape[0]
    NDin = Din // 128
    E = w_hbm.shape[1]
    NE = E // 128
    u = C.uid()

    if src_in_hbm:
        ln = C.sb(f"pj_in{u}", [128, NDin, T], BF16)
        ld0 = C.base("ld")
        with nc.Block() as blk:
            @blk.sync
            def _(e_):
                e_.dma_start(ln[:, :, :], chunked(src)).then_inc(C.sem("ld"), 16)
            @blk.vector
            def _(e_):
                e_.wait_ge(C.sem("ld"), ld0 + 16)
        C.bump("ld", 16)
    else:
        ln = src

    wstage = [C.sb(f"pj_ws{u}_{i}", [128, NDin, 128], F32) for i in range(2)]
    wb = [C.sb(f"pj_wb{u}_{i}", [128, NDin, 128], BF16) for i in range(2)]
    ob = [C.sb(f"pj_ob{u}_{i}", [128, 512], BF16) for i in range(2)]
    bcols = C.sb(f"pj_bc{u}", [128, max(NE, 1)], F32)
    ps = C.psum(f"pj_ps{u}", [128, 1024], F32)

    if bias_hbm is not None:
        ld0 = C.base("ld")
        with nc.Block() as blk:
            @blk.sync
            def _(e_):
                e_.dma_start(bcols[:, 0:NE], bias_hbm.rearrange("(c p) -> p c", p=128),
                             allow_slow_non_contiguous=True).then_inc(C.sem("ld"), 16)
            @blk.vector
            def _(e_):
                e_.wait_ge(C.sem("ld"), ld0 + 16)
        C.bump("ld", 16)
        if bias_scale is not None:
            with nc.Block() as blk:
                @blk.vector
                def _(e_):
                    e_.tensor_scalar(bcols[:, 0:NE], bcols[:, 0:NE], bias_scale, None,
                                     ALU.mult)

    b_ld = C.base("ld")
    b_ca = C.base("ca")
    b_mm = C.base("mm")
    b_ev = C.base("ev")
    b_s = [C.base("s0"), C.base("s1")]
    nunits = NE * C.NB
    with nc.Block() as blk:
        @blk.sync
        def _(e_):
            for ec in range(NE):
                b = ec % 2
                if ec >= 1:
                    e_.wait_ge(C.sem("ld"), b_ld + 16 * ec)
                if ec >= 2:
                    e_.wait_ge(C.sem("ca"), b_ca + ec - 1)
                e_.dma_start(
                    wstage[b][:, :, :],
                    w_hbm[:, ec * 128:(ec + 1) * 128].rearrange("(c p) e -> p c e", p=128),
                ).then_inc(C.sem("ld"), 16)
        @blk.vector
        def _(e_):
            for ec in range(NE):
                b = ec % 2
                e_.wait_ge(C.sem("ld"), b_ld + 16 * (ec + 1))
                if ec >= 2:
                    e_.wait_ge(C.sem("mm"), b_mm + (ec - 1) * C.NB)
                ins = e_.tensor_copy(wb[b][:, :, :], wstage[b][:, :, :])
                if w_hook is not None:
                    C.fence(e_, ins, "fv")
                    ins = w_hook(e_, wb[b], ec)
                ins.then_inc(C.sem("ca"), 1)
        @blk.tensor
        def _(e_):
            for ec in range(NE):
                b = ec % 2
                e_.wait_ge(C.sem("ca"), b_ca + ec + 1)
                for tb in range(C.NB):
                    i = ec * C.NB + tb
                    pb = i % 2
                    if i >= 2:
                        e_.wait_ge(C.sem("ev"), b_ev + i - 1)
                    last = NDin - 1 if bias_row is None else -1
                    for c in range(NDin):
                        ins = e_.matmul(ps[:, pb * 512:(pb + 1) * 512],
                                        wb[b][:, c, :], ln[:, c, tb * 512:(tb + 1) * 512],
                                        start=(c == 0), stop=(c == last))
                    if bias_row is not None:
                        brs, onr = bias_row
                        ins = e_.matmul(ps[:, pb * 512:(pb + 1) * 512],
                                        brs[0:1, ec * 128:(ec + 1) * 128],
                                        onr[0:1, 0:512], start=False, stop=True)
                    ins.then_inc(C.sem("mm"), 1)
        @blk.scalar
        def _(e_):
            for ec in range(NE):
                for tb in range(C.NB):
                    i = ec * C.NB + tb
                    pb = i % 2
                    e_.wait_ge(C.sem("mm"), b_mm + i + 1)
                    if i >= 2:
                        e_.wait_ge(C.sem(f"s{pb}"), b_s[pb] + 16 * (i // 2))
                    bias_ap = bcols[:, ec:ec + 1] if bias_hbm is not None else 0.0
                    ins = e_.activation(ob[pb][:, :], ps[:, pb * 512:(pb + 1) * 512],
                                        func, bias=bias_ap, scale=scale)
                    ins.then_inc(C.sem("ev"), 1)
                    e_.wait_ge(C.sem("ev"), b_ev + i + 1)
                    e_.dma_start(
                        out_hbm[out_row0 + ec * 128:out_row0 + (ec + 1) * 128,
                                tb * 512:(tb + 1) * 512],
                        ob[pb][:, :]).then_inc(C.sem(f"s{pb}"), 16)
    C.bump("ld", 16 * NE)
    C.bump("ca", NE)
    C.bump("mm", nunits)
    C.bump("ev", nunits)
    C.bump("s0", 16 * ceil_div(nunits, 2))
    C.bump("s1", 16 * (nunits // 2))
    e0, e1 = C.base("s0"), C.base("s1")
    with nc.Block() as blk:
        @blk.sync
        def _(e_):
            e_.wait_ge(C.sem("s0"), e0)
            e_.wait_ge(C.sem("s1"), e1)


@_scoped
def ph_proj_tm(C, ln_sbuf, w_hbm, out_hbm, *, bias_hbm=None):
    nc, T = C.nc, C.T
    Din = w_hbm.shape[0]
    NDin = Din // 128
    E = w_hbm.shape[1]
    NEB = ceil_div(E, 512)
    u = C.uid()
    wstage = [C.sb(f"pt_ws{u}_{i}", [128, NDin, 512], F32) for i in range(2)]
    wb = [C.sb(f"pt_wb{u}_{i}", [128, NDin, 512], BF16) for i in range(2)]
    ob = [C.sb(f"pt_ob{u}_{i}", [128, 512], BF16) for i in range(2)]
    brow_f = C.sb(f"pt_bf{u}", [1, E], F32)
    brow = C.sb(f"pt_br{u}", [1, E], BF16)
    onest = C.sb(f"pt_on{u}", [1, 128], BF16)
    ps = C.psum(f"pt_ps{u}", [128, 1024], F32)

    ld0 = C.base("ld")
    with nc.Block() as blk:
        @blk.gpsimd
        def _(e_):
            e_.memset(onest[:, :], 1.0)
            if bias_hbm is None:
                e_.memset(brow[:, :], 0.0)
        if bias_hbm is not None:
            @blk.sync
            def _(e_):
                e_.dma_start(brow_f[0:1, :],
                             bias_hbm.rearrange("(o e) -> o e", o=1)).then_inc(
                    C.sem("ld"), 16)
            @blk.vector
            def _(e_):
                e_.wait_ge(C.sem("ld"), ld0 + 16)
    if bias_hbm is not None:
        C.bump("ld", 16)
        with nc.Block() as blk:
            @blk.vector
            def _(e_):
                e_.tensor_copy(brow[0:1, :], brow_f[0:1, :])

    b_ld = C.base("ld")
    b_ca = C.base("ca")
    b_mm = C.base("mm")
    b_ev = C.base("ev")
    b_s = [C.base("s0"), C.base("s1")]
    with nc.Block() as blk:
        @blk.sync
        def _(e_):
            for eb in range(NEB):
                ew = min(512, E - eb * 512)
                b = eb % 2
                if eb >= 1:
                    e_.wait_ge(C.sem("ld"), b_ld + 16 * eb)
                if eb >= 2:
                    e_.wait_ge(C.sem("ca"), b_ca + eb - 1)
                e_.dma_start(
                    wstage[b][:, :, 0:ew],
                    w_hbm[:, eb * 512:eb * 512 + ew].rearrange("(c p) e -> p c e", p=128),
                ).then_inc(C.sem("ld"), 16)
        @blk.vector
        def _(e_):
            for eb in range(NEB):
                ew = min(512, E - eb * 512)
                b = eb % 2
                e_.wait_ge(C.sem("ld"), b_ld + 16 * (eb + 1))
                if eb >= 2:
                    e_.wait_ge(C.sem("mm"), b_mm + (eb - 1) * C.NT)
                e_.tensor_copy(wb[b][:, :, 0:ew], wstage[b][:, :, 0:ew]).then_inc(
                    C.sem("ca"), 1)
        @blk.tensor
        def _(e_):
            for eb in range(NEB):
                ew = min(512, E - eb * 512)
                b = eb % 2
                e_.wait_ge(C.sem("ca"), b_ca + eb + 1)
                for tt in range(C.NT):
                    i = eb * C.NT + tt
                    pb = i % 2
                    if i >= 2:
                        e_.wait_ge(C.sem("ev"), b_ev + i - 1)
                    for c in range(NDin):
                        e_.matmul(ps[:, pb * 512:pb * 512 + ew],
                                  ln_sbuf[:, c, tt * 128:(tt + 1) * 128],
                                  wb[b][:, c, 0:ew],
                                  start=(c == 0), stop=False)
                    e_.matmul(ps[:, pb * 512:pb * 512 + ew], onest[:, :],
                              brow[0:1, eb * 512:eb * 512 + ew],
                              start=False, stop=True).then_inc(C.sem("mm"), 1)
        @blk.scalar
        def _(e_):
            for eb in range(NEB):
                ew = min(512, E - eb * 512)
                for tt in range(C.NT):
                    i = eb * C.NT + tt
                    pb = i % 2
                    e_.wait_ge(C.sem("mm"), b_mm + i + 1)
                    if i >= 2:
                        e_.wait_ge(C.sem(f"s{pb}"), b_s[pb] + 16 * (i // 2))
                    ins = e_.activation(ob[pb][:, 0:ew], ps[:, pb * 512:pb * 512 + ew],
                                        AF.Identity, bias=0.0, scale=1.0)
                    ins.then_inc(C.sem("ev"), 1)
                    e_.wait_ge(C.sem("ev"), b_ev + i + 1)
                    e_.dma_start(out_hbm[tt * 128:(tt + 1) * 128, eb * 512:eb * 512 + ew],
                                 ob[pb][:, 0:ew]).then_inc(C.sem(f"s{pb}"), 16)
    n = NEB * C.NT
    C.bump("ld", 16 * NEB)
    C.bump("ca", NEB)
    C.bump("mm", n)
    C.bump("ev", n)
    C.bump("s0", 16 * ceil_div(n, 2))
    C.bump("s1", 16 * (n // 2))
    e0, e1 = C.base("s0"), C.base("s1")
    with nc.Block() as blk:
        @blk.sync
        def _(e_):
            e_.wait_ge(C.sem("s0"), e0)
            e_.wait_ge(C.sem("s1"), e1)


@_scoped
def ph_out_proj_add(C, src_hbm, w_hbm, bias_hbm, r_src, r_dst):
    nc, T = C.nc, C.T
    Din = w_hbm.shape[0]
    NDin = Din // 128
    E = w_hbm.shape[1]
    NE = E // 128
    u = C.uid()
    ln = C.sb(f"op_in{u}", [128, NDin, T], BF16)
    wstage = [C.sb(f"op_ws{u}_{i}", [128, NDin, 128], F32) for i in range(2)]
    wb = [C.sb(f"op_wb{u}_{i}", [128, NDin, 128], BF16) for i in range(2)]
    yb = [C.sb(f"op_yb{u}_{i}", [128, 512], F32) for i in range(2)]
    rtile = [C.sb(f"op_rt{u}_{i}", [128, 512], F32) for i in range(2)]
    ob = [C.sb(f"op_ob{u}_{i}", [128, 512], F32) for i in range(2)]
    bcols = C.sb(f"op_bc{u}", [128, max(NE, 1)], F32)
    ps = C.psum(f"op_ps{u}", [128, 1024], F32)

    ld0 = C.base("ld")
    nld = 1 + (1 if bias_hbm is not None else 0)
    with nc.Block() as blk:
        @blk.sync
        def _(e_):
            e_.dma_start(ln[:, :, :], chunked(src_hbm)).then_inc(C.sem("ld"), 16)
            if bias_hbm is not None:
                e_.dma_start(bcols[:, 0:NE], bias_hbm.rearrange("(c p) -> p c", p=128),
                             allow_slow_non_contiguous=True).then_inc(C.sem("ld"), 16)
        @blk.vector
        def _(e_):
            e_.wait_ge(C.sem("ld"), ld0 + 16 * nld)
    C.bump("ld", 16 * nld)

    b_ld = C.base("ld")
    b_ca = C.base("ca")
    b_mm = C.base("mm")
    b_ev = C.base("ev")
    b_x2 = C.base("x2")
    b_r = [C.base("r0"), C.base("r1")]
    b_s = [C.base("s0"), C.base("s1")]
    nunits = NE * C.NB
    with nc.Block() as blk:
        @blk.sync
        def _(e_):
            for ec in range(NE):
                b = ec % 2
                if ec >= 1:
                    e_.wait_ge(C.sem("ld"), b_ld + 16 * ec)
                if ec >= 2:
                    e_.wait_ge(C.sem("ca"), b_ca + ec - 1)
                e_.dma_start(
                    wstage[b][:, :, :],
                    w_hbm[:, ec * 128:(ec + 1) * 128].rearrange("(c p) e -> p c e", p=128),
                ).then_inc(C.sem("ld"), 16)
                for tb in range(C.NB):
                    i = ec * C.NB + tb
                    if i >= 2:
                        e_.wait_ge(C.sem("x2"), b_x2 + i - 1)
                    e_.dma_start(rtile[i % 2][:, :],
                                 r_src[ec * 128:(ec + 1) * 128, tb * 512:(tb + 1) * 512]
                                 ).then_inc(C.sem(f"r{i % 2}"), 16)
        @blk.vector
        def _(e_):
            for ec in range(NE + 1):
                if ec >= 1:
                    for tb in range(C.NB):
                        i = (ec - 1) * C.NB + tb
                        pb = i % 2
                        e_.wait_ge(C.sem("mm"), b_mm + i + 1)
                        e_.wait_ge(C.sem(f"r{pb}"), b_r[pb] + 16 * (i // 2 + 1))
                        if i >= 2:
                            e_.wait_ge(C.sem(f"s{pb}"), b_s[pb] + 16 * (i // 2))
                        bias_ap = (bcols[:, ec - 1:ec] if bias_hbm is not None else 0.0)
                        e_.scalar_tensor_tensor(
                            ob[pb][:, :], ps[:, pb * 512:(pb + 1) * 512], bias_ap,
                            rtile[pb][:, :], ALU.add, ALU.add).then_inc(C.sem("x2"), 1)
                        e_.wait_ge(C.sem("x2"), b_x2 + i + 1)
                if ec < NE:
                    b = ec % 2
                    e_.wait_ge(C.sem("ld"), b_ld + 16 * (ec + 1))
                    if ec >= 2:
                        e_.wait_ge(C.sem("mm"), b_mm + (ec - 1) * C.NB)
                    e_.tensor_copy(wb[b][:, :, :], wstage[b][:, :, :]).then_inc(
                        C.sem("ca"), 1)
        @blk.tensor
        def _(e_):
            for ec in range(NE):
                b = ec % 2
                e_.wait_ge(C.sem("ca"), b_ca + ec + 1)
                for tb in range(C.NB):
                    i = ec * C.NB + tb
                    pb = i % 2
                    if i >= 2:
                        e_.wait_ge(C.sem("x2"), b_x2 + i - 1)
                    for c in range(NDin):
                        ins = e_.matmul(ps[:, pb * 512:(pb + 1) * 512],
                                        wb[b][:, c, :], ln[:, c, tb * 512:(tb + 1) * 512],
                                        start=(c == 0), stop=(c == NDin - 1))
                    ins.then_inc(C.sem("mm"), 1)
        @blk.scalar
        def _(e_):
            for ec in range(NE):
                for tb in range(C.NB):
                    i = ec * C.NB + tb
                    pb = i % 2
                    e_.wait_ge(C.sem("x2"), b_x2 + i + 1)
                    e_.dma_start(r_dst[ec * 128:(ec + 1) * 128, tb * 512:(tb + 1) * 512],
                                 ob[pb][:, :]).then_inc(C.sem(f"s{pb}"), 16)
    C.bump("ld", 16 * NE)
    C.bump("ca", NE)
    C.bump("mm", nunits)
    C.bump("x2", nunits)
    C.bump("r0", 16 * ceil_div(nunits, 2))
    C.bump("r1", 16 * (nunits // 2))
    C.bump("s0", 16 * ceil_div(nunits, 2))
    C.bump("s1", 16 * (nunits // 2))
    e0, e1 = C.base("s0"), C.base("s1")
    with nc.Block() as blk:
        @blk.sync
        def _(e_):
            e_.wait_ge(C.sem("s0"), e0)
            e_.wait_ge(C.sem("s1"), e1)


@_scoped
def ph_attention(C, q_hbm, k_hbm, v_hbm, o_hbm, *, n_heads=16, n_kv=16,
                 mask_kind="causal", window_chunks=8):
    nc, T = C.nc, C.T
    NT = C.NT
    u = C.uid()
    EK = n_kv * 128

    ksb = C.sb(f"at_k{u}", [128, n_kv, T], BF16)
    vsb = C.sb(f"at_v{u}", [128, NT, EK], BF16)
    qsb = [C.sb(f"at_q{u}_{i}", [128, n_heads, 512], BF16) for i in range(2)]
    p2 = C.sb(f"at_p2{u}", [128, T], BF16)
    pt = C.sb(f"at_ptb{u}", [128, NT, 128], BF16)
    osb = [C.sb(f"at_o{u}_{i}", [128, 128], BF16) for i in range(2)]
    stat = C.sb(f"at_st{u}", [128, 4], F32)
    ident = C.sb(f"at_id{u}", [128, 128], BF16)
    mneg = C.sb(f"at_mn{u}", [128, 128], F32)
    mwin = C.sb(f"at_mw{u}", [128, 128], F32)
    mbias = C.sb(f"at_mb{u}", [128, 128], F32)

    pS = C.psum(f"at_ps{u}", [128, 2048], F32)
    pT = C.psum(f"at_pt{u}", [128, 2048], BF16)
    pO = C.psum(f"at_po{u}", [128, 1024], F32)

    id_h = C.inline("c_ident", np.eye(128, dtype=ml_dtypes.bfloat16))
    mneg_h = C.inline("c_mneg", np.triu(np.full((128, 128), NEG, dtype=np.float32), 1))
    mwin_h = C.inline("c_mwin", np.tril(np.full((128, 128), NEG, dtype=np.float32), -1))
    mbias_h = C.inline("c_mbias", np.tril(np.ones((128, 128), dtype=np.float32)))

    ld0 = C.base("ld")
    with nc.Block() as blk:
        @blk.sync
        def _(e_):
            e_.dma_start(ksb[:, :, :], chunked(k_hbm[0:EK, :])).then_inc(C.sem("ld"), 16)
            e_.dma_start(vsb[:, :, :],
                         v_hbm[:, 0:EK].rearrange("(c p) t -> p c t", p=128)
                         ).then_inc(C.sem("ld"), 16)
            e_.dma_start(ident[:, :], id_h[:, :]).then_inc(C.sem("ld"), 16)
            e_.dma_start(mneg[:, :], mneg_h[:, :]).then_inc(C.sem("ld"), 16)
            e_.dma_start(mwin[:, :], mwin_h[:, :]).then_inc(C.sem("ld"), 16)
            e_.dma_start(mbias[:, :], mbias_h[:, :]).then_inc(C.sem("ld"), 16)
        @blk.vector
        def _(e_):
            e_.wait_ge(C.sem("ld"), ld0 + 16 * 6)
        @blk.tensor
        def _(e_):
            e_.wait_ge(C.sem("ld"), ld0 + 16 * 6)
    C.bump("ld", 16 * 6)

    b_mm = C.base("mm")
    b_ev = C.base("ev")
    b_x1 = C.base("x1")
    b_q = [C.base("q0"), C.base("q1")]
    b_s = [C.base("s0"), C.base("s1")]

    units = []
    for qb in range(C.NB):
        for h in range(n_heads):
            for qt in range(4):
                units.append((qb, h, qb * 4 + qt))
    nu = len(units)

    def kparams(qt_abs):
        k_lo = max(0, qt_abs - window_chunks) if mask_kind == "window" else 0
        k_hi = NT if mask_kind == "trilbias" else qt_abs + 1
        return k_lo, k_hi

    with nc.Block() as blk:
        @blk.sync
        def _(e_):
            for qb in range(C.NB):
                if qb >= 2:
                    done_units = len([x for x in units if x[0] <= qb - 2])
                    e_.wait_ge(C.sem("mm"), b_mm + 3 * done_units)
                e_.dma_start(qsb[qb % 2][:, :, :],
                             chunked(q_hbm[:, qb * 512:(qb + 1) * 512])).then_inc(
                                 C.sem(f"q{qb % 2}"), 16)

        @blk.tensor
        def _(e_):
            for ui, (qb, h, qt_abs) in enumerate(units):
                k_lo, k_hi = kparams(qt_abs)
                nk = k_hi - k_lo
                kv = h // (n_heads // n_kv)
                qcol = (qt_abs - qb * 4) * 128
                e_.wait_ge(C.sem(f"q{qb % 2}"), b_q[qb % 2] + 16 * (qb // 2 + 1))
                if ui >= 1:
                    e_.wait_ge(C.sem("ev"), b_ev + 3 * (ui - 1) + 1)
                ins = None
                for kc in range(k_lo, k_hi, 4):
                    kw = min(4, k_hi - kc)
                    ins = e_.matmul(pS[:, (kc - k_lo) * 128:(kc - k_lo) * 128 + kw * 128],
                                    qsb[qb % 2][:, h, qcol:qcol + 128],
                                    ksb[:, kv, kc * 128:(kc + kw) * 128],
                                    start=True, stop=True)
                ins.then_inc(C.sem("mm"), 1)
                e_.wait_ge(C.sem("x1"), b_x1 + 3 * ui + 3)
                for j in range(nk):
                    ins = e_.transpose(pT[:, j * 128:(j + 1) * 128],
                                       p2[:, j * 128:(j + 1) * 128], ident[:, :])
                ins.then_inc(C.sem("mm"), 1)
                e_.wait_ge(C.sem("ev"), b_ev + 3 * ui + 2)
                if ui >= 2:
                    e_.wait_ge(C.sem("ev"), b_ev + 3 * (ui - 2) + 3)
                for j in range(nk):
                    ins = e_.matmul(pO[:, (ui % 2) * 512:(ui % 2) * 512 + 128],
                                    vsb[:, k_lo + j, kv * 128:(kv + 1) * 128],
                                    pt[:, j, :],
                                    start=(j == 0), stop=(j == nk - 1))
                ins.then_inc(C.sem("mm"), 1)

        @blk.vector
        def _(e_):
            for ui, (qb, h, qt_abs) in enumerate(units):
                k_lo, k_hi = kparams(qt_abs)
                nk = k_hi - k_lo
                e_.wait_ge(C.sem("mm"), b_mm + 3 * ui + 1)
                if mask_kind in ("causal", "window"):
                    dcol = (qt_abs - k_lo) * 128
                    C.fence(e_, e_.tensor_add(pS[:, dcol:dcol + 128],
                                              pS[:, dcol:dcol + 128], mneg[:, :]), "fv")
                    if mask_kind == "window" and qt_abs >= window_chunks:
                        C.fence(e_, e_.tensor_add(pS[:, 0:128], pS[:, 0:128],
                                                  mwin[:, :]), "fv")
                else:
                    if qt_abs > 0:
                        C.fence(e_, e_.tensor_scalar(pS[:, 0:qt_abs * 128],
                                                     pS[:, 0:qt_abs * 128],
                                                     1.0, None, ALU.add), "fv")
                    C.fence(e_, e_.tensor_add(pS[:, qt_abs * 128:(qt_abs + 1) * 128],
                                              pS[:, qt_abs * 128:(qt_abs + 1) * 128],
                                              mbias[:, :]), "fv")
                e_.reduce_max(stat[:, 0:1], pS[:, 0:nk * 128], axis=AX.X,
                              negate=True).then_inc(C.sem("x1"), 1)
                e_.wait_ge(C.sem("ev"), b_ev + 3 * ui + 1)
                e_.reciprocal(stat[:, 2:3], stat[:, 1:2]).then_inc(C.sem("x1"), 1)
                e_.wait_ge(C.sem("x1"), b_x1 + 3 * ui + 2)
                e_.tensor_scalar(p2[:, 0:nk * 128], p2[:, 0:nk * 128], stat[:, 2:3],
                                 None, ALU.mult).then_inc(C.sem("x1"), 1)

        @blk.scalar
        def _(e_):
            for ui, (qb, h, qt_abs) in enumerate(units):
                k_lo, k_hi = kparams(qt_abs)
                nk = k_hi - k_lo
                e_.wait_ge(C.sem("x1"), b_x1 + 3 * ui + 1)
                if ui >= 1:
                    e_.wait_ge(C.sem("mm"), b_mm + 3 * (ui - 1) + 2)
                e_.activation(p2[:, 0:nk * 128], pS[:, 0:nk * 128], AF.Exp,
                              bias=stat[:, 0:1], scale=1.0,
                              accum_out=stat[:, 1:2]).then_inc(C.sem("ev"), 1)
                e_.wait_ge(C.sem("mm"), b_mm + 3 * ui + 2)
                ins = None
                for j in range(nk):
                    ins = e_.activation(pt[:, j, :], pT[:, j * 128:(j + 1) * 128],
                                        AF.Identity, bias=0.0, scale=1.0)
                ins.then_inc(C.sem("ev"), 1)
                e_.wait_ge(C.sem("mm"), b_mm + 3 * ui + 3)
                if ui >= 2:
                    e_.wait_ge(C.sem(f"s{ui % 2}"), b_s[ui % 2] + 16 * (ui // 2))
                ins = e_.activation(osb[ui % 2][:, :],
                                    pO[:, (ui % 2) * 512:(ui % 2) * 512 + 128],
                                    AF.Identity, bias=0.0, scale=1.0)
                ins.then_inc(C.sem("ev"), 1)
                e_.wait_ge(C.sem("ev"), b_ev + 3 * ui + 3)
                e_.dma_start(o_hbm[h * 128:(h + 1) * 128, qt_abs * 128:(qt_abs + 1) * 128],
                             osb[ui % 2][:, :]).then_inc(C.sem(f"s{ui % 2}"), 16)

    C.bump("q0", 16 * ceil_div(C.NB, 2))
    C.bump("q1", 16 * (C.NB // 2))
    C.bump("mm", 3 * nu)
    C.bump("x1", 3 * nu)
    C.bump("ev", 3 * nu)
    C.bump("s0", 16 * ceil_div(nu, 2))
    C.bump("s1", 16 * (nu // 2))
    e0, e1 = C.base("s0"), C.base("s1")
    with nc.Block() as blk:
        @blk.sync
        def _(e_):
            e_.wait_ge(C.sem("s0"), e0)
            e_.wait_ge(C.sem("s1"), e1)


def sublayer_L0_attn_a(C, W, sc, r_src, r_dst):
    C.push_scope()
    ln = C.sb(f"s1_ln{C.uid()}", [128, ND, C.T], BF16)
    ph_layernorm(C, r_src, ln)
    ph_proj(C, ln, W["L0aa_wq"], sc["q"], bias_hbm=W["L0aa_bq"],
            scale=INV_SQRT_DK, bias_scale=INV_SQRT_DK)
    ph_proj(C, ln, W["L0aa_wk"], sc["k"], bias_hbm=W["L0aa_bk"])
    ph_proj_tm(C, ln, W["L0aa_wv"], sc["v"], bias_hbm=W["L0aa_bv"])
    C.pop_scope()
    ph_attention4(C, sc["q"], sc["k"], sc["v"], sc["o"],
                   n_heads=16, n_kv=16, mask_kind="causal")
    ph_out_proj_add(C, sc["o"], W["L0aa_wo"], W["L0aa_bo"], r_src, r_dst)


WEIGHT_SPECS = {
    "L0aa_wq": ("L0_attn_a_w_q", "T", (D, D)),
    "L0aa_bq": ("L0_attn_a_b_q", None, (D,)),
    "L0aa_wk": ("L0_attn_a_w_k", "T", (D, D)),
    "L0aa_bk": ("L0_attn_a_b_k", None, (D,)),
    "L0aa_wv": ("L0_attn_a_w_v", "T", (D, D)),
    "L0aa_bv": ("L0_attn_a_b_v", None, (D,)),
    "L0aa_wo": ("L0_attn_a_w_o", "T", (D, D)),
    "L0aa_bo": ("L0_attn_a_b_o", None, (D,)),
    "L0f_wr": ("L0_ffn_router_weight", "T", (D, 8)),
    "L0f_g": ("L0_ffn_sh_gate", "T", (D, 8192)),
    "L0f_u": ("L0_ffn_sh_up", "T", (D, 8192)),
    "L0f_d": ("L0_ffn_sh_down", "T", (8192, D)),
    "L0f_rg": ("L0_ffn_r_gate", "T", (D, 1024)),
    "L0f_ru": ("L0_ffn_r_up", "T", (D, 1024)),
    "L0f_rd": ("L0_ffn_r_down", "T", (1024, D)),
    "L0ab_wq": ("L0_attn_b_wq", "T", (D, D)),
    "L0ab_bq": ("L0_attn_b_bq", None, (D,)),
    "L0ab_wk": ("L0_attn_b_wk", "T", (D, 512)),
    "L0ab_bk": ("L0_attn_b_bk", None, (512,)),
    "L0ab_wv": ("L0_attn_b_wv", "T", (D, 512)),
    "L0ab_bv": ("L0_attn_b_bv", None, (512,)),
    "L0ab_wo": ("L0_attn_b_wo", "T", (D, D)),
    "L0ab_bo": ("L0_attn_b_bo", None, (D,)),
    "L1aa_wq": ("L1_attn_a_wq", "T", (D, D)),
    "L1aa_wk": ("L1_attn_a_wk", "T", (D, 512)),
    "L1aa_wv": ("L1_attn_a_wv", "T", (D, 512)),
    "L1aa_wo": ("L1_attn_a_wo", "T", (D, D)),
    "L1aa_gq": ("L1_attn_a_gamma_q", None, (128,)),
    "L1aa_gk": ("L1_attn_a_gamma_k", None, (128,)),
    "L1f_wr": ("L1_ffn_gate_w", "T", (D, 8)),
    "L1f_wd": ("L1_ffn_wd", "L1WD", (D, 4608)),
    "L1f_bd": ("L1_ffn_bd", "L1BD", (4608,)),
    "L1f_wu": ("L1_ffn_wu", "L1WU", (4608, D)),
    "L1f_bu": ("L1_ffn_bu", None, (8, D)),
    "L1f_bsu": ("L1_ffn_bs_u", None, (D,)),
    "L1ab_wq": ("L1_attn_b_Wq", "T", (D, D)),
    "L1ab_bq": ("L1_attn_b_bq", None, (D,)),
    "L1ab_wk": ("L1_attn_b_Wk", "T", (D, D)),
    "L1ab_bk": ("L1_attn_b_bk", None, (D,)),
    "L1ab_wv": ("L1_attn_b_Wv", "T", (D, D)),
    "L1ab_bv": ("L1_attn_b_bv", None, (D,)),
    "L1ab_wo": ("L1_attn_b_Wo", "T", (D, D)),
    "L1ab_bo": ("L1_attn_b_bo", None, (D,)),
}


def build_program(T=2048, upto=1):
    nc = bass.Bass("TRN2", target_bir_lowering=False)
    x = nc.dram_tensor("x", [D, T], F32, kind="ExternalInput")
    out = nc.dram_tensor("out", [D, T], F32, kind="ExternalOutput")
    W = {}
    for name, (_, _, shape) in WEIGHT_SPECS.items():
        W[name] = nc.dram_tensor(name, list(shape), F32, kind="ExternalInput")
    r1 = nc.dram_tensor("r1", [D, T], F32)
    r2 = nc.dram_tensor("r2", [D, T], F32)
    sc = {
        "q": nc.dram_tensor("sc_q", [D, T], BF16),
        "k": nc.dram_tensor("sc_k", [D, T], BF16),
        "v": nc.dram_tensor("sc_v", [T, D], BF16),
        "o": nc.dram_tensor("sc_o", [D, T], BF16),
        "h": nc.dram_tensor("sc_h", [9216, T], BF16),
        "qt": nc.dram_tensor("sc_qt", [T, D], BF16),
        "kt": nc.dram_tensor("sc_kt", [T, 512], BF16),
        "g1": nc.dram_tensor("sc_g1", [D, T], BF16),
        "g2": nc.dram_tensor("sc_g2", [D, T], BF16),
        "g3": nc.dram_tensor("sc_g3", [D, T], BF16),
    }
    sublayers = [sublayer_L0_attn_a, sublayer_L0_ffn, sublayer_L0_attn_b, sublayer_L0_ffn, sublayer_L1_attn_a, sublayer_L1_ffn, sublayer_L1_attn_b, sublayer_L1_ffn][:upto]
    chain = [x, r1, r2, r1, r2, r1, r2, r1, out]
    with ExitStack() as es:
        C = Ctx(nc, es, T)
        for i, f in enumerate(sublayers):
            src = chain[i]
            dst = out if i == len(sublayers) - 1 else chain[i + 1]
            f(C, W, sc, src, dst)
    return nc


def host_weights(inputs):
    m = {}
    for name, (key, tf, _) in WEIGHT_SPECS.items():
        a = np.asarray(inputs[key], dtype=np.float32)
        if tf == "T":
            a = np.ascontiguousarray(a.T)
        elif tf == "L1WD":
            wd = a
            wsd = np.asarray(inputs["L1_ffn_ws_d"], dtype=np.float32)
            a = np.ascontiguousarray(np.concatenate(
                [np.concatenate([wd[e].T for e in range(8)], axis=1), wsd.T], axis=1))
        elif tf == "L1BD":
            bsd = np.asarray(inputs["L1_ffn_bs_d"], dtype=np.float32)
            a = np.ascontiguousarray(np.concatenate([a.reshape(-1), bsd]))
        elif tf == "L1WU":
            wu = a
            wsu = np.asarray(inputs["L1_ffn_ws_u"], dtype=np.float32)
            a = np.ascontiguousarray(np.concatenate(
                [np.concatenate([wu[e].T for e in range(8)], axis=0), wsu.T], axis=0))
        m[name] = a
    return m


def kernel(**inputs):
    T = 2048
    nc = build_program(T=T, upto=8)
    wmap = host_weights(inputs)
    x_all = np.asarray(inputs["L0_attn_a_x"], dtype=np.float32)
    in_maps = []
    for b in range(8):
        d = dict(wmap)
        d["x"] = np.ascontiguousarray(x_all[b].T)
        in_maps.append(d)
    res = run_bass_kernel_spmd(nc, in_maps, core_ids=list(range(8)))
    return np.stack([res.results[b]["out"].T for b in range(8)]).astype(np.float32)


@_scoped
def ph_router_comb(C, ln_sbuf, wr_hbm, c_b):
    nc, T = C.nc, C.T
    u = C.uid()
    wr = C.sb(f"rt_wr{u}", [128, ND, 8], BF16)
    wrs = C.sb(f"rt_ws{u}", [128, ND, 8], F32)
    lg = C.sb(f"rt_lg{u}", [128, 8], F32)
    ex = C.sb(f"rt_ex{u}", [128, 8], F32)
    sg = C.sb(f"rt_sg{u}", [128, 8], F32)
    st = C.sb(f"rt_st{u}", [128, 6], F32)
    ccol = C.sb(f"rt_cc{u}", [128, C.NT], F32)
    crow = C.sb(f"rt_cr{u}", [1, T], F32)
    identf = C.sb(f"rt_id{u}", [128, 128], F32)
    onesf = C.sb(f"rt_on{u}", [1, 128], F32)
    pl = C.psum(f"rt_pl{u}", [128, 512], F32)
    pr = C.psum(f"rt_pr{u}", [128, 512], F32)

    idf_h = C.inline("c_identf", np.eye(128, dtype=np.float32))
    ld0 = C.base("ld")
    with nc.Block() as blk:
        @blk.sync
        def _(e_):
            e_.dma_start(wrs[:, :, :], chunked(wr_hbm)).then_inc(C.sem("ld"), 16)
            e_.dma_start(identf[:, :], idf_h[:, :]).then_inc(C.sem("ld"), 16)
        @blk.gpsimd
        def _(e_):
            e_.memset(onesf[:, :], 1.0)
        @blk.vector
        def _(e_):
            e_.wait_ge(C.sem("ld"), ld0 + 32)
    C.bump("ld", 32)
    with nc.Block() as blk:
        @blk.vector
        def _(e_):
            e_.tensor_copy(wr[:, :, :], wrs[:, :, :])

    for tt in range(C.NT):
        with nc.Block() as blk:
            @blk.tensor
            def _(e_):
                for c in range(ND):
                    e_.matmul(pl[:, 0:8], ln_sbuf[:, c, tt * 128:(tt + 1) * 128],
                              wr[:, c, :], start=(c == 0), stop=(c == ND - 1))
        with nc.Block() as blk:
            @blk.vector
            def _(e_):
                C.fence(e_, e_.reduce_max(st[:, 0:1], pl[:, 0:8], axis=AX.X,
                                          negate=True), "fv")
                C.fence(e_, e_.tensor_scalar(lg[:, :], pl[:, 0:8], st[:, 0:1], None,
                                             ALU.add), "fv")
        with nc.Block() as blk:
            @blk.scalar
            def _(e_):
                C.fence(e_, e_.activation(ex[:, :], lg[:, :], AF.Exp, bias=0.0,
                                          scale=1.0), "fs")
                e_.activation(sg[:, :], lg[:, :], AF.Sign, bias=0.0, scale=1.0)
        with nc.Block() as blk:
            @blk.vector
            def _(e_):
                C.fence(e_, e_.tensor_scalar(sg[:, :], sg[:, :], 1.0, 1.0,
                                             ALU.add, ALU.min), "fv")
                C.fence(e_, e_.tensor_sub(sg[:, :], ex[:, :], sg[:, :]), "fv")
                C.fence(e_, e_.reduce_max(st[:, 1:2], sg[:, :], axis=AX.X), "fv")
                C.fence(e_, e_.tensor_scalar(st[:, 2:3], st[:, 1:2], 1.0, None,
                                             ALU.add), "fv")
                C.fence(e_, e_.reciprocal(st[:, 3:4], st[:, 2:3]), "fv")
                C.fence(e_, e_.tensor_mul(st[:, 4:5], st[:, 1:2], st[:, 3:4]), "fv")
                C.fence(e_, e_.tensor_add(ccol[:, tt:tt + 1], st[:, 3:4], st[:, 4:5]),
                        "fv")
    for tt in range(C.NT):
        with nc.Block() as blk:
            @blk.tensor
            def _(e_):
                e_.transpose(pr[0:1, 0:128], ccol[:, tt:tt + 1], identf[:, :])
        with nc.Block() as blk:
            @blk.vector
            def _(e_):
                e_.tensor_copy(crow[0:1, tt * 128:(tt + 1) * 128], pr[0:1, 0:128])
    for tb in range(C.NB):
        with nc.Block() as blk:
            @blk.tensor
            def _(e_):
                e_.matmul(pr[:, 0:512], onesf[:, :], crow[0:1, tb * 512:(tb + 1) * 512],
                          start=True, stop=True)
        with nc.Block() as blk:
            @blk.vector
            def _(e_):
                e_.tensor_copy(c_b[:, tb * 512:(tb + 1) * 512], pr[:, 0:512])


@_scoped
def ph_gu(C, ln_sbuf, wg_hbm, wu_hbm, h_hbm, *, h_row0=0, c_b=None):
    nc, T = C.nc, C.T
    E = wg_hbm.shape[1]
    NE = E // 128
    u = C.uid()
    wsg = [C.sb(f"gu_ws{u}_{i}", [128, ND, 128], F32) for i in range(2)]
    wsu = [C.sb(f"gu_wt{u}_{i}", [128, ND, 128], F32) for i in range(2)]
    wgb = [C.sb(f"gu_wg{u}_{i}", [128, ND, 128], BF16) for i in range(2)]
    wub = [C.sb(f"gu_wu{u}_{i}", [128, ND, 128], BF16) for i in range(2)]
    gs = [C.sb(f"gu_gs{u}_{i}", [128, 512], BF16) for i in range(2)]
    hb = [C.sb(f"gu_hb{u}_{i}", [128, 512], BF16) for i in range(2)]
    ps = C.psum(f"gu_ps{u}", [128, 2048], F32)

    b_ld = C.base("ld")
    b_ca = C.base("ca")
    b_mm = C.base("mm")
    b_ev = C.base("ev")
    b_x1 = C.base("x1")
    b_s = [C.base("s0"), C.base("s1")]
    nunits = NE * C.NB
    with nc.Block() as blk:
        @blk.sync
        def _(e_):
            for ib in range(NE):
                b = ib % 2
                if ib >= 1:
                    e_.wait_ge(C.sem("ld"), b_ld + 32 * ib)
                if ib >= 2:
                    e_.wait_ge(C.sem("ca"), b_ca + 2 * (ib - 1))
                e_.dma_start(
                    wsg[b][:, :, :],
                    wg_hbm[:, ib * 128:(ib + 1) * 128].rearrange("(c p) e -> p c e", p=128),
                ).then_inc(C.sem("ld"), 16)
                e_.wait_ge(C.sem("ld"), b_ld + 32 * ib + 16)
                if ib >= 2:
                    e_.wait_ge(C.sem("ca"), b_ca + 2 * (ib - 1) + 1)
                e_.dma_start(
                    wsu[b][:, :, :],
                    wu_hbm[:, ib * 128:(ib + 1) * 128].rearrange("(c p) e -> p c e", p=128),
                ).then_inc(C.sem("ld"), 16)
        @blk.vector
        def _(e_):
            for ib in range(NE):
                b = ib % 2
                e_.wait_ge(C.sem("ld"), b_ld + 32 * ib + 16)
                if ib >= 2:
                    e_.wait_ge(C.sem("mm"), b_mm + 2 * C.NB * (ib - 1))
                e_.tensor_copy(wgb[b][:, :, :], wsg[b][:, :, :]).then_inc(C.sem("ca"), 1)
                e_.wait_ge(C.sem("ld"), b_ld + 32 * (ib + 1))
                e_.tensor_copy(wub[b][:, :, :], wsu[b][:, :, :]).then_inc(C.sem("ca"), 1)
                for tb in range(C.NB):
                    i = ib * C.NB + tb
                    p = i % 2
                    e_.wait_ge(C.sem("mm"), b_mm + 2 * i + 2)
                    e_.wait_ge(C.sem("ev"), b_ev + i + 1)
                    if i >= 2:
                        e_.wait_ge(C.sem(f"s{p}"), b_s[p] + 16 * (i // 2))
                    if c_b is None:
                        e_.tensor_mul(hb[p][:, :], gs[p][:, :],
                                      ps[:, p * 1024 + 512:p * 1024 + 1024]).then_inc(
                                          C.sem("x1"), 1)
                    else:
                        C.fence(e_, e_.tensor_mul(hb[p][:, :], gs[p][:, :],
                                                  ps[:, p * 1024 + 512:p * 1024 + 1024]),
                                "fv")
                        e_.tensor_mul(hb[p][:, :], hb[p][:, :],
                                      c_b[:, tb * 512:(tb + 1) * 512]).then_inc(
                                          C.sem("x1"), 1)
        @blk.tensor
        def _(e_):
            for ib in range(NE):
                b = ib % 2
                for tb in range(C.NB):
                    i = ib * C.NB + tb
                    p = i % 2
                    if i >= 2:
                        e_.wait_ge(C.sem("ev"), b_ev + i - 1)
                        e_.wait_ge(C.sem("x1"), b_x1 + i - 1)
                    e_.wait_ge(C.sem("ca"), b_ca + 2 * ib + 1)
                    for c in range(ND):
                        ins = e_.matmul(ps[:, p * 1024:p * 1024 + 512],
                                        wgb[b][:, c, :], ln_sbuf[:, c, tb * 512:(tb + 1) * 512],
                                        start=(c == 0), stop=(c == ND - 1))
                    ins.then_inc(C.sem("mm"), 1)
                    e_.wait_ge(C.sem("ca"), b_ca + 2 * ib + 2)
                    for c in range(ND):
                        ins = e_.matmul(ps[:, p * 1024 + 512:p * 1024 + 1024],
                                        wub[b][:, c, :], ln_sbuf[:, c, tb * 512:(tb + 1) * 512],
                                        start=(c == 0), stop=(c == ND - 1))
                    ins.then_inc(C.sem("mm"), 1)
        @blk.scalar
        def _(e_):
            for ib in range(NE):
                for tb in range(C.NB):
                    i = ib * C.NB + tb
                    p = i % 2
                    e_.wait_ge(C.sem("mm"), b_mm + 2 * i + 1)
                    if i >= 2:
                        e_.wait_ge(C.sem("x1"), b_x1 + i - 1)
                    e_.activation(gs[p][:, :], ps[:, p * 1024:p * 1024 + 512],
                                  AF.Silu, bias=0.0, scale=1.0).then_inc(C.sem("ev"), 1)
                    e_.wait_ge(C.sem("x1"), b_x1 + i + 1)
                    e_.dma_start(
                        h_hbm[h_row0 + ib * 128:h_row0 + (ib + 1) * 128,
                              tb * 512:(tb + 1) * 512],
                        hb[p][:, :]).then_inc(C.sem(f"s{p}"), 16)
    C.bump("ld", 32 * NE)
    C.bump("ca", 2 * NE)
    C.bump("mm", 2 * nunits)
    C.bump("ev", nunits)
    C.bump("x1", nunits)
    C.bump("s0", 16 * ceil_div(nunits, 2))
    C.bump("s1", 16 * (nunits // 2))
    e0, e1 = C.base("s0"), C.base("s1")
    with nc.Block() as blk:
        @blk.sync
        def _(e_):
            e_.wait_ge(C.sem("s0"), e0)
            e_.wait_ge(C.sem("s1"), e1)




@_scoped
def ph_down_add(C, h_hbm, wd_specs, r_src, r_dst, n_i_blocks):
    nc, T = C.nc, C.T
    u = C.uid()
    NIB = n_i_blocks
    wq = C.sb(f"dn_wq{u}", [128, NIB, 512], BF16)
    wstage = [C.sb(f"dn_ws{u}_{i}", [128, 512], F32) for i in range(4)]
    hq = C.sb(f"dn_hq{u}", [128, NIB, 512], BF16)
    yb = [C.sb(f"dn_yb{u}_{i}", [128, 512], F32) for i in range(2)]
    rtile = [C.sb(f"dn_rt{u}_{i}", [128, 512], F32) for i in range(2)]
    ob = [C.sb(f"dn_ob{u}_{i}", [128, 512], F32) for i in range(2)]
    ps = C.psum(f"dn_ps{u}", [128, 4096], F32)

    def wd_block_ap(ib):
        for w_hbm, i0, nb in wd_specs:
            if i0 <= ib < i0 + nb:
                return w_hbm[(ib - i0) * 128:(ib - i0 + 1) * 128, :]
        raise ValueError(ib)

    b_w = [C.base(f"w{i}") for i in range(4)]
    b_ca = C.base("ca")
    b_mm = C.base("mm")
    b_ev = C.base("ev")
    b_x2 = C.base("x2")
    b_hl = C.base("hl")
    b_r = [C.base("r0"), C.base("r1")]
    b_s = [C.base("s0"), C.base("s1")]
    nw = [0, 0, 0, 0]

    for eq in range(4):
        with nc.Block() as blk:
            @blk.sync
            def _(e_):
                for ib in range(NIB):
                    sp = ib % 4
                    ca_abs = eq * NIB + ib
                    if ca_abs >= 4:
                        e_.wait_ge(C.sem("ca"), b_ca + ca_abs - 3)
                    e_.dma_start(wstage[sp][:, :],
                                 wd_block_ap(ib)[:, eq * 512:(eq + 1) * 512]
                                 ).then_inc(C.sem(f"w{sp}"), 16)
            @blk.vector
            def _(e_):
                for ib in range(NIB):
                    sp = ib % 4
                    nw[sp] += 16
                    e_.wait_ge(C.sem(f"w{sp}"), b_w[sp] + nw[sp])
                    e_.tensor_copy(wq[:, ib, :], wstage[sp][:, :]).then_inc(C.sem("ca"), 1)
        C.bump("ca", NIB)

        with nc.Block() as blk:
            @blk.sync
            def _(e_):
                for tb in range(C.NB):
                    tb_abs = eq * C.NB + tb
                    if tb_abs >= 1:
                        e_.wait_ge(C.sem("mm"), b_mm + 4 * tb_abs)
                    e_.dma_start(hq[:, :, :],
                                 chunked(h_hbm[:, tb * 512:(tb + 1) * 512])
                                 ).then_inc(C.sem("hl"), 16)
                    for ec in range(4):
                        i = tb_abs * 4 + ec
                        if i >= 2:
                            e_.wait_ge(C.sem("x2"), b_x2 + i - 1)
                        e_.dma_start(
                            rtile[i % 2][:, :],
                            r_src[eq * 512 + ec * 128:eq * 512 + (ec + 1) * 128,
                                  tb * 512:(tb + 1) * 512]).then_inc(
                                      C.sem(f"r{i % 2}"), 16)
            @blk.tensor
            def _(e_):
                for tb in range(C.NB):
                    tb_abs = eq * C.NB + tb
                    tp = tb_abs % 2
                    e_.wait_ge(C.sem("hl"), b_hl + 16 * (tb_abs + 1))
                    if tb_abs >= 2:
                        e_.wait_ge(C.sem("x2"), b_x2 + 4 * (tb_abs - 1))
                    for ec in range(4):
                        for ib in range(NIB):
                            ins = e_.matmul(
                                ps[:, tp * 2048 + ec * 512:tp * 2048 + (ec + 1) * 512],
                                wq[:, ib, ec * 128:(ec + 1) * 128], hq[:, ib, :],
                                start=(ib == 0), stop=(ib == NIB - 1))
                        ins.then_inc(C.sem("mm"), 1)
            @blk.scalar
            def _(e_):
                for tb in range(C.NB):
                    tb_abs = eq * C.NB + tb
                    for ec in range(4):
                        i = tb_abs * 4 + ec
                        pb = i % 2
                        e_.wait_ge(C.sem("x2"), b_x2 + i + 1)
                        e_.dma_start(
                            r_dst[eq * 512 + ec * 128:eq * 512 + (ec + 1) * 128,
                                  tb * 512:(tb + 1) * 512],
                            ob[pb][:, :]).then_inc(C.sem(f"s{pb}"), 16)
            @blk.vector
            def _(e_):
                for tb in range(C.NB):
                    tb_abs = eq * C.NB + tb
                    tp = tb_abs % 2
                    for ec in range(4):
                        i = tb_abs * 4 + ec
                        pb = i % 2
                        e_.wait_ge(C.sem("mm"), b_mm + 4 * tb_abs + ec + 1)
                        e_.wait_ge(C.sem(f"r{pb}"), b_r[pb] + 16 * (i // 2 + 1))
                        if i >= 2:
                            e_.wait_ge(C.sem(f"s{pb}"), b_s[pb] + 16 * (i // 2))
                        e_.scalar_tensor_tensor(
                            ob[pb][:, :],
                            ps[:, tp * 2048 + ec * 512:tp * 2048 + (ec + 1) * 512],
                            0.0, rtile[pb][:, :], ALU.add, ALU.add).then_inc(
                                C.sem("x2"), 1)
        n = 4 * C.NB
        C.bump("hl", 16 * C.NB)
        C.bump("mm", 4 * C.NB)
        C.bump("x2", n)
    ntot = 16 * C.NB
    for sp in range(4):
        C.bump(f"w{sp}", nw[sp])
    C.bump("r0", 16 * ceil_div(ntot, 2))
    C.bump("r1", 16 * (ntot // 2))
    C.bump("s0", 16 * ceil_div(ntot, 2))
    C.bump("s1", 16 * (ntot // 2))
    e0, e1 = C.base("s0"), C.base("s1")
    with nc.Block() as blk:
        @blk.sync
        def _(e_):
            e_.wait_ge(C.sem("s0"), e0)
            e_.wait_ge(C.sem("s1"), e1)


@_scoped
def ph_down_add2(C, h_hbm, wd_specs, r_src, r_dst, n_i_blocks):
    nc, T = C.nc, C.T
    u = C.uid()
    NIB = n_i_blocks
    assert NIB % 2 == 0
    NH = NIB // 2
    NB = C.NB
    wq = C.sb(f"dn_wq{u}", [128, NIB, 512], BF16)
    wstage = [C.sb(f"dn_ws{u}_{i}", [128, 512], F32) for i in range(4)]
    hq = C.sb(f"dn_hq{u}", [128, NIB, 512], BF16)
    rtile = [C.sb(f"dn_rt{u}_{i}", [128, 512], F32) for i in range(2)]
    ob = [C.sb(f"dn_ob{u}_{i}", [128, 512], F32) for i in range(2)]
    ps = C.psum(f"dn_ps{u}", [128, 4096], F32)

    def wd_block_ap(ib):
        for w_hbm, i0, nb in wd_specs:
            if i0 <= ib < i0 + nb:
                return w_hbm[(ib - i0) * 128:(ib - i0 + 1) * 128, :]
        raise ValueError(ib)

    b_w = [C.base(f"w{i}") for i in range(4)]
    b_ca = C.base("ca")
    b_mm = C.base("mm")
    b_x2 = C.base("x2")
    b_hl = C.base("hl")
    b_l0 = C.base("l0")
    b_r = [C.base("r0"), C.base("r1")]
    b_s = [C.base("s0"), C.base("s1")]

    def mm_idx(eq, tb, half, ec, ibl):
        return ((eq * NB + tb) * 8 + half * 4 + ec) * NH + ibl

    with nc.Block() as blk:
        @blk.sync
        def _(e_):
            nwl = [0, 0, 0, 0]

            def wload(eq, lo, hi_):
                for ib in range(lo, hi_):
                    k = eq * NIB + ib
                    sp = k % 4
                    if k >= 4:
                        e_.wait_ge(C.sem("ca"), b_ca + k - 3)
                    nwl[sp] += 16
                    e_.dma_start(wstage[sp][:, :],
                                 wd_block_ap(ib)[:, eq * 512:(eq + 1) * 512]
                                 ).then_inc(C.sem(f"w{sp}"), 16)
            for eq in range(4):
                for tb in range(NB):
                    tb_abs = eq * NB + tb
                    for half in range(2):
                        if tb_abs >= 1:
                            e_.wait_ge(C.sem("mm"),
                                       b_mm + mm_idx(0, tb_abs - 1, half, 3, NH - 1) + 1)
                        e_.dma_start(
                            hq[:, half * NH:(half + 1) * NH, :],
                            chunked(h_hbm[half * NH * 128:(half + 1) * NH * 128,
                                          tb * 512:(tb + 1) * 512])
                        ).then_inc(C.sem(["hl", "l0"][half]), 16)
                    for ec in range(4):
                        i = tb_abs * 4 + ec
                        if i >= 2:
                            e_.wait_ge(C.sem("x2"), b_x2 + i - 1)
                        e_.dma_start(
                            rtile[i % 2][:, :],
                            r_src[eq * 512 + ec * 128:eq * 512 + (ec + 1) * 128,
                                  tb * 512:(tb + 1) * 512]).then_inc(
                                      C.sem(f"r{i % 2}"), 16)
                        if ec == 1:
                            if eq == 0 and tb == 0:
                                wload(0, 0, NIB)
                            if tb == NB - 1 and eq < 3:
                                wload(eq + 1, 0, NH + 4)
                        if ec == 3 and tb == NB - 1 and eq < 3:
                            wload(eq + 1, NH + 4, NIB)
        @blk.vector
        def _(e_):
            nw = [0, 0, 0, 0]

            def casts(eq, lo, hi_):
                for ib in range(lo, hi_):
                    k = eq * NIB + ib
                    sp = k % 4
                    nw[sp] += 16
                    e_.wait_ge(C.sem(f"w{sp}"), b_w[sp] + nw[sp])
                    if eq > 0:
                        e_.wait_ge(C.sem("mm"),
                                   b_mm + mm_idx(eq - 1, NB - 1, ib // NH, 3, ib % NH) + 1)
                    e_.tensor_copy(wq[:, ib, :], wstage[sp][:, :]).then_inc(
                        C.sem("ca"), 1)

            def stt(eq, tb, ec):
                tb_abs = eq * NB + tb
                i = tb_abs * 4 + ec
                pb = i % 2
                tp = tb_abs % 2
                e_.wait_ge(C.sem("mm"), b_mm + mm_idx(eq, tb, 1, ec, NH - 1) + 1)
                e_.wait_ge(C.sem(f"r{pb}"), b_r[pb] + 16 * (i // 2 + 1))
                if i >= 2:
                    e_.wait_ge(C.sem(f"s{pb}"), b_s[pb] + 16 * (i // 2))
                e_.scalar_tensor_tensor(
                    ob[pb][:, :],
                    ps[:, tp * 2048 + ec * 512:tp * 2048 + (ec + 1) * 512],
                    0.0, rtile[pb][:, :], ALU.add, ALU.add).then_inc(
                        C.sem("x2"), 1)

            casts(0, 0, NIB)
            for eq in range(4):
                for tb in range(NB):
                    if eq < 3 and tb == NB - 1:
                        casts(eq + 1, 0, NH)
                        for ec in range(3):
                            stt(eq, tb, ec)
                        casts(eq + 1, NH, NIB)
                        stt(eq, tb, 3)
                    else:
                        for ec in range(4):
                            stt(eq, tb, ec)
        @blk.tensor
        def _(e_):
            for eq in range(4):
                for tb in range(NB):
                    tb_abs = eq * NB + tb
                    tp = tb_abs % 2
                    if tb_abs >= 2:
                        e_.wait_ge(C.sem("x2"), b_x2 + 4 * (tb_abs - 1))
                    for half in range(2):
                        e_.wait_ge(C.sem(["hl", "l0"][half]),
                                   [b_hl, b_l0][half] + 16 * (tb_abs + 1))
                        for ec in range(4):
                            for ibl in range(NH):
                                ib = half * NH + ibl
                                if tb == 0 and ec == 0:
                                    e_.wait_ge(C.sem("ca"),
                                               b_ca + eq * NIB + ib + 1)
                                e_.matmul(
                                    ps[:, tp * 2048 + ec * 512:tp * 2048 + (ec + 1) * 512],
                                    wq[:, ib, ec * 128:(ec + 1) * 128],
                                    hq[:, ib, :],
                                    start=(half == 0 and ibl == 0),
                                    stop=(half == 1 and ibl == NH - 1),
                                    skip_group_check=True).then_inc(C.sem("mm"), 1)
        @blk.scalar
        def _(e_):
            for eq in range(4):
                for tb in range(NB):
                    tb_abs = eq * NB + tb
                    for ec in range(4):
                        i = tb_abs * 4 + ec
                        pb = i % 2
                        e_.wait_ge(C.sem("x2"), b_x2 + i + 1)
                        e_.dma_start(
                            r_dst[eq * 512 + ec * 128:eq * 512 + (ec + 1) * 128,
                                  tb * 512:(tb + 1) * 512],
                            ob[pb][:, :]).then_inc(C.sem(f"s{pb}"), 16)

    ntot = 16 * NB
    C.bump("mm", 4 * NB * 8 * NH)
    C.bump("ca", 4 * NIB)
    C.bump("hl", 16 * 4 * NB)
    C.bump("l0", 16 * 4 * NB)
    for sp in range(4):
        C.bump(f"w{sp}", 16 * NIB)
    C.bump("x2", ntot)
    C.bump("r0", 16 * ceil_div(ntot, 2))
    C.bump("r1", 16 * (ntot // 2))
    C.bump("s0", 16 * ceil_div(ntot, 2))
    C.bump("s1", 16 * (ntot // 2))
    e0, e1 = C.base("s0"), C.base("s1")
    with nc.Block() as blk:
        @blk.sync
        def _(e_):
            e_.wait_ge(C.sem("s0"), e0)
            e_.wait_ge(C.sem("s1"), e1)


def sublayer_L0_ffn(C, W, sc, r_src, r_dst):
    C.push_scope()
    su = C.uid()
    ln = C.sb(f"sf_ln{su}", [128, ND, C.T], BF16)
    c_b = C.sb(f"sf_cb{su}", [128, C.T], F32)
    ph_layernorm(C, r_src, ln)
    ph_router_comb(C, ln, W["L0f_wr"], c_b)
    ph_gu(C, ln, W["L0f_g"], W["L0f_u"], sc["h"], h_row0=0)
    ph_gu(C, ln, W["L0f_rg"], W["L0f_ru"], sc["h"], h_row0=8192, c_b=c_b)
    C.pop_scope()
    ph_down_add2(C, sc["h"], [(W["L0f_d"], 0, 64), (W["L0f_rd"], 64, 8)],
                 r_src, r_dst, 72)


def _l0b_rope_tables(scale_q):
    Dh = 128
    inv_freq = 1.0 / 10000.0 ** (np.arange(0, Dh, 2, dtype=np.float32) / Dh)
    t = np.arange(16, dtype=np.float32)
    freqs = np.einsum("i,j->ij", t, inv_freq)
    emb = np.concatenate([freqs, freqs], axis=-1)
    cos = np.cos(emb)
    sin = np.sin(emb)
    sn = np.concatenate([-sin[:, 0:64], sin[:, 64:128]], axis=-1)
    return (cos * scale_q).astype(np.float32), (sn * scale_q).astype(np.float32)


def _make_rope_w_hook(C, tag0, n_heads, scale):
    tag = f"{tag0}_{C.uid()}"
    cos_np, sn_np = _l0b_rope_tables(scale)
    cos_t = np.tile(cos_np[None, :, :], (128, 1, 1)).astype(ml_dtypes.bfloat16)
    sn_t = np.tile(sn_np[None, :, :], (128, 1, 1)).astype(ml_dtypes.bfloat16)
    cos_h = C.inline(f"c_cos{tag0}_{n_heads}_{scale}", cos_t)
    sn_h = C.inline(f"c_sn{tag0}_{n_heads}_{scale}", sn_t)
    cos_sb = C.sb(f"rp_c{tag}", [128, 16, 128], BF16)
    sn_sb = C.sb(f"rp_s{tag}", [128, 16, 128], BF16)
    t1 = C.sb(f"rp_t1{tag}", [128, ND, 128], BF16)
    t2 = C.sb(f"rp_t2{tag}", [128, ND, 128], BF16)
    nc = C.nc
    ld0 = C.base("ld")
    with nc.Block() as blk:
        @blk.sync
        def _(e_):
            e_.dma_start(cos_sb[:, :, :], cos_h[:, :, :]).then_inc(C.sem("ld"), 16)
            e_.dma_start(sn_sb[:, :, :], sn_h[:, :, :]).then_inc(C.sem("ld"), 16)
        @blk.vector
        def _(e_):
            e_.wait_ge(C.sem("ld"), ld0 + 32)
    C.bump("ld", 32)

    def hook(e_, wb, ec):
        h = ec
        for c in range(ND):
            e_.tensor_mul(t1[:, c, :], wb[:, c, :], cos_sb[:, h, :])
            e_.tensor_mul(t2[:, c, 0:64], wb[:, c, 64:128], sn_sb[:, h, 0:64])
            ins = e_.tensor_mul(t2[:, c, 64:128], wb[:, c, 0:64], sn_sb[:, h, 64:128])
        C.fence(e_, ins, "fv")
        return e_.tensor_add(wb[:, :, :], t1[:, :, :], t2[:, :, :])

    return hook


@_scoped
def ph_rope_bias_row(C, bias_hbm, n_heads, scale, brow_out, ones_out):
    nc = C.nc
    u = C.uid()
    E = n_heads * 128
    cos_np, sn_np = _l0b_rope_tables(scale)
    cosr = C.inline(f"c_bcr{u}", cos_np[:n_heads].reshape(1, E))
    snr = C.inline(f"c_bsr{u}", sn_np[:n_heads].reshape(1, E))
    bf = C.sb(f"rb_b{u}", [1, E], F32)
    cr = C.sb(f"rb_c{u}", [1, E], F32)
    sr = C.sb(f"rb_s{u}", [1, E], F32)
    t1 = C.sb(f"rb_t1{u}", [1, E], F32)
    t2 = C.sb(f"rb_t2{u}", [1, E], F32)
    ld0 = C.base("ld")
    with nc.Block() as blk:
        @blk.sync
        def _(e_):
            e_.dma_start(bf[0:1, :], bias_hbm.rearrange("(o e) -> o e", o=1)).then_inc(
                C.sem("ld"), 16)
            e_.dma_start(cr[0:1, :], cosr[:, :]).then_inc(C.sem("ld"), 16)
            e_.dma_start(sr[0:1, :], snr[:, :]).then_inc(C.sem("ld"), 16)
        @blk.gpsimd
        def _(e_):
            e_.memset(ones_out[:, :], 1.0)
        @blk.vector
        def _(e_):
            e_.wait_ge(C.sem("ld"), ld0 + 48)
    C.bump("ld", 48)
    with nc.Block() as blk:
        @blk.vector
        def _(e_):
            C.fence(e_, e_.tensor_mul(t1[0:1, :], bf[0:1, :], cr[0:1, :]), "fv")
            for h in range(n_heads):
                lo = h * 128
                e_.tensor_mul(t2[0:1, lo:lo + 64], bf[0:1, lo + 64:lo + 128],
                              sr[0:1, lo:lo + 64])
                ins = e_.tensor_mul(t2[0:1, lo + 64:lo + 128], bf[0:1, lo:lo + 64],
                                    sr[0:1, lo + 64:lo + 128])
            C.fence(e_, ins, "fv")
            e_.tensor_add(brow_out[0:1, :], t1[0:1, :], t2[0:1, :])


def sublayer_L0_attn_b(C, W, sc, r_src, r_dst):
    C.push_scope()
    su = C.uid()
    ln = C.sb(f"s3_ln{su}", [128, ND, C.T], BF16)
    bqr = C.sb(f"s3_bqr{su}", [1, D], BF16)
    bkr = C.sb(f"s3_bkr{su}", [1, 512], BF16)
    onr = C.sb(f"s3_onr{su}", [1, 512], BF16)
    ph_layernorm(C, r_src, ln)
    ph_rope_bias_row(C, W["L0ab_bq"], 16, INV_SQRT_DK, bqr, onr)
    ph_rope_bias_row(C, W["L0ab_bk"], 4, 1.0, bkr, onr)
    C.push_scope()
    hook_q = _make_rope_w_hook(C, "q3", 16, INV_SQRT_DK)
    ph_proj(C, ln, W["L0ab_wq"], sc["q"], w_hook=hook_q, bias_row=(bqr, onr))
    C.pop_scope()
    C.push_scope()
    hook_k = _make_rope_w_hook(C, "k3", 4, 1.0)
    ph_proj(C, ln, W["L0ab_wk"], sc["k"], w_hook=hook_k, bias_row=(bkr, onr))
    C.pop_scope()
    ph_proj_tm(C, ln, W["L0ab_wv"], sc["v"], bias_hbm=W["L0ab_bv"])
    C.pop_scope()
    ph_attention4(C, sc["q"], sc["k"], sc["v"], sc["o"],
                  n_heads=16, n_kv=4, mask_kind="trilbias")
    ph_out_proj_add(C, sc["o"], W["L0ab_wo"], W["L0ab_bo"], r_src, r_dst)


@_scoped
def ph_rope_norm(C, tm_hbm, fm_hbm, n_heads, *, w_b=None):
    nc, T = C.nc, C.T
    u = C.uid()
    H = n_heads
    qld = [C.sb(f"rn_q{u}_{i}", [128, H, 128], BF16) for i in range(2)]
    t1 = C.sb(f"rn_t1{u}", [128, H, 128], BF16)
    t2 = C.sb(f"rn_t2{u}", [128, H, 128], BF16)
    q1 = C.sb(f"rn_q1{u}", [128, H, 128], BF16)
    q2 = C.sb(f"rn_q2{u}", [128, H, 128], BF16)
    sqd = C.sb(f"rn_sq{u}", [128, 128], F32)
    nrm = C.sb(f"rn_nr{u}", [128, H], F32)
    rr = C.sb(f"rn_rr{u}", [128, H], F32)
    cos_sb = C.sb(f"rn_c{u}", [128, C.NT, 128], F32)
    sin_sb = C.sb(f"rn_s{u}", [128, C.NT, 128], F32)
    ob = [C.sb(f"rn_ob{u}_{i}", [128, H, 128], BF16) for i in range(2)]
    pT = C.psum(f"rn_pt{u}", [128, 2048], BF16)
    ident = C.sb(f"rn_id{u}", [128, 128], BF16)

    Dh = 128
    inv_freq = 1.0 / 10000.0 ** (np.arange(0, Dh, 2, dtype=np.float32) / Dh)
    ang = np.arange(T, dtype=np.float32)[:, None] * inv_freq[None, :]
    cos_full = np.concatenate([np.cos(ang), np.cos(ang)], axis=-1)
    sin_full = np.concatenate([np.sin(ang), np.sin(ang)], axis=-1)
    sn_full = np.concatenate([-sin_full[:, 0:64], sin_full[:, 64:128]], axis=-1)
    cos_h = C.inline(f"c_l1cos{T}", cos_full.reshape(C.NT, 128, 128).transpose(1, 0, 2)
                     .copy())
    sin_h = C.inline(f"c_l1sin{T}", sn_full.reshape(C.NT, 128, 128).transpose(1, 0, 2)
                     .copy())
    id_h = C.inline("c_ident", np.eye(128, dtype=ml_dtypes.bfloat16))

    eps_ap = C.constcol(1e-6)
    ld0 = C.base("ld")
    with nc.Block() as blk:
        @blk.sync
        def _(e_):
            e_.dma_start(cos_sb[:, :, :], cos_h[:, :, :]).then_inc(C.sem("ld"), 16)
            e_.dma_start(sin_sb[:, :, :], sin_h[:, :, :]).then_inc(C.sem("ld"), 16)
            e_.dma_start(ident[:, :], id_h[:, :]).then_inc(C.sem("ld"), 16)
            e_.dma_start(qld[0][:, :, :],
                         tm_hbm[0:128, :].rearrange("(c p) e -> p c e", p=128)
                         if False else
                         tm_hbm[0:128, 0:H * 128].rearrange("t (h d) -> t h d", d=128)
                         ).then_inc(C.sem("q0"), 16)
        @blk.vector
        def _(e_):
            e_.wait_ge(C.sem("ld"), ld0 + 48)
        @blk.tensor
        def _(e_):
            e_.wait_ge(C.sem("ld"), ld0 + 48)
    C.bump("ld", 48)
    C.bump("q0", 16)

    b_q = [C.base("q0") - 16, C.base("q1")]
    b_x1 = C.base("x1")
    b_ev = C.base("ev")
    b_mm = C.base("mm")
    b_s = [C.base("s0"), C.base("s1")]

    with nc.Block() as blk:
        @blk.sync
        def _(e_):
            for tt in range(1, C.NT):
                if tt >= 2:
                    e_.wait_ge(C.sem("x1"), b_x1 + 3 * (tt - 2) + 1)
                e_.dma_start(qld[tt % 2][:, :, :],
                             tm_hbm[tt * 128:(tt + 1) * 128, 0:H * 128]
                             .rearrange("t (h d) -> t h d", d=128)
                             ).then_inc(C.sem(["q0", "q1"][tt % 2]), 16)

        @blk.vector
        def _(e_):
            for tt in range(C.NT):
                p = tt % 2
                e_.wait_ge(C.sem(["q0", "q1"][p]), b_q[p] + 16 * (tt // 2 + 1))
                ins = None
                for h in range(H):
                    e_.tensor_mul(t1[:, h, :], qld[p][:, h, :], cos_sb[:, tt, :])
                    e_.tensor_mul(t2[:, h, 0:64], qld[p][:, h, 64:128],
                                  sin_sb[:, tt, 0:64])
                    ins = e_.tensor_mul(t2[:, h, 64:128], qld[p][:, h, 0:64],
                                        sin_sb[:, tt, 64:128])
                C.fence(e_, ins, "fv")
                e_.tensor_add(q1[:, :, :], t1[:, :, :], t2[:, :, :]).then_inc(
                    C.sem("x1"), 1)
                e_.wait_ge(C.sem("x1"), b_x1 + 3 * tt + 1)
                ins = None
                for h in range(H):
                    ins = e_.scalar_tensor_tensor(t1[:, h, :], q1[:, h, :], 1.0,
                                                  q1[:, h, :], ALU.bypass, ALU.mult,
                                                  accum_out=nrm[:, h:h + 1])
                ins.then_inc(C.sem("x1"), 1)
                e_.wait_ge(C.sem("x1"), b_x1 + 3 * tt + 2)
                e_.wait_ge(C.sem("ev"), b_ev + 2 * tt + 1)
                C.fence(e_, e_.reciprocal(rr[:, :], nrm[:, :]), "fv")
                if tt >= 2:
                    e_.wait_ge(C.sem("mm"), b_mm + tt - 1)
                ins = None
                for h in range(H):
                    if w_b is None:
                        ins = e_.tensor_scalar(q2[:, h, :], q1[:, h, :], rr[:, h:h + 1],
                                               None, ALU.mult)
                    else:
                        C.fence(e_, e_.tensor_scalar(q2[:, h, :], q1[:, h, :],
                                                     rr[:, h:h + 1], None, ALU.mult),
                                "fv")
                        ins = e_.tensor_mul(q2[:, h, :], q2[:, h, :], w_b[:, :])
                ins.then_inc(C.sem("x1"), 1)

        @blk.scalar
        def _(e_):
            for tt in range(C.NT):
                e_.wait_ge(C.sem("x1"), b_x1 + 3 * tt + 2)
                ins = e_.activation(nrm[:, :], nrm[:, :], AF.Sqrt, bias=0.0, scale=1.0)
                C.fence(e_, ins, "fs")
                e_.activation(nrm[:, :], nrm[:, :], AF.Identity,
                              bias=eps_ap[:, :], scale=1.0).then_inc(C.sem("ev"), 1)
                e_.wait_ge(C.sem("mm"), b_mm + tt + 1)
                if tt >= 2:
                    e_.wait_ge(C.sem(["s0", "s1"][tt % 2]),
                               b_s[tt % 2] + 16 * (tt // 2))
                ins = None
                for h in range(H):
                    ins = e_.activation(ob[tt % 2][:, h, :], pT[:, h * 128:(h + 1) * 128],
                                        AF.Identity, bias=0.0, scale=1.0)
                ins.then_inc(C.sem("ev"), 1)
                e_.wait_ge(C.sem("ev"), b_ev + 2 * tt + 2)
                e_.dma_start(
                    fm_hbm[0:H * 128, tt * 128:(tt + 1) * 128]
                    .rearrange("(h d) t -> d h t", d=128),
                    ob[tt % 2][:, :, :]).then_inc(C.sem(["s0", "s1"][tt % 2]), 16)

        @blk.tensor
        def _(e_):
            for tt in range(C.NT):
                e_.wait_ge(C.sem("x1"), b_x1 + 3 * tt + 3)
                if tt >= 1:
                    e_.wait_ge(C.sem("ev"), b_ev + 2 * tt)
                ins = None
                for h in range(H):
                    ins = e_.transpose(pT[:, h * 128:(h + 1) * 128], q2[:, h, :],
                                       ident[:, :])
                ins.then_inc(C.sem("mm"), 1)

    C.bump("q0", 16 * (C.NT // 2 - 1) if C.NT >= 2 else 0)
    C.bump("q1", 16 * ceil_div(C.NT - 1, 2))
    C.bump("x1", 3 * C.NT)
    C.bump("ev", 2 * C.NT)
    C.bump("mm", C.NT)
    C.bump("s0", 16 * ceil_div(C.NT, 2))
    C.bump("s1", 16 * (C.NT // 2))
    e0, e1 = C.base("s0"), C.base("s1")
    with nc.Block() as blk:
        @blk.sync
        def _(e_):
            e_.wait_ge(C.sem("s0"), e0)
            e_.wait_ge(C.sem("s1"), e1)


@_scoped
def ph_gamma_tile(C, gq_hbm, gk_hbm, w_b, scale):
    nc = C.nc
    u = C.uid()
    rows = C.sb(f"gm_r{u}", [1, 3, 128], F32)
    onesf = C.sb(f"gm_on{u}", [1, 128], F32)
    pr = C.psum(f"gm_pr{u}", [128, 512], F32)
    ld0 = C.base("ld")
    with nc.Block() as blk:
        @blk.sync
        def _(e_):
            e_.dma_start(rows[0:1, 0, :], gq_hbm.rearrange("(o e) -> o e", o=1)
                         ).then_inc(C.sem("ld"), 16)
            e_.dma_start(rows[0:1, 1, :], gk_hbm.rearrange("(o e) -> o e", o=1)
                         ).then_inc(C.sem("ld"), 16)
        @blk.gpsimd
        def _(e_):
            e_.memset(onesf[:, :], 1.0)
        @blk.vector
        def _(e_):
            e_.wait_ge(C.sem("ld"), ld0 + 32)
    C.bump("ld", 32)
    with nc.Block() as blk:
        @blk.vector
        def _(e_):
            C.fence(e_, e_.tensor_mul(rows[0:1, 2, :], rows[0:1, 0, :],
                                      rows[0:1, 1, :]), "fv")
            e_.tensor_scalar(rows[0:1, 2, :], rows[0:1, 2, :], scale, None, ALU.mult)
    with nc.Block() as blk:
        @blk.tensor
        def _(e_):
            e_.matmul(pr[:, 0:128], onesf[:, :], rows[0:1, 2, :], start=True, stop=True)
    with nc.Block() as blk:
        @blk.vector
        def _(e_):
            e_.tensor_copy(w_b[:, :], pr[:, 0:128])


def sublayer_L1_attn_a(C, W, sc, r_src, r_dst):
    C.push_scope()
    su = C.uid()
    ln = C.sb(f"s5_ln{su}", [128, ND, C.T], BF16)
    w_b = C.sb(f"s5_wb{su}", [128, 128], F32)
    ph_layernorm(C, r_src, ln)
    ph_gamma_tile(C, W["L1aa_gq"], W["L1aa_gk"], w_b, INV_SQRT_DK)
    ph_proj_tm(C, ln, W["L1aa_wq"], sc["qt"])
    ph_proj_tm(C, ln, W["L1aa_wk"], sc["kt"])
    ph_proj_tm(C, ln, W["L1aa_wv"], sc["v"])
    C.pop_scope()
    C.push_scope()
    ph_rope_norm(C, sc["qt"], sc["q"], 16)
    C.pop_scope()
    C.push_scope()
    su = C.uid()
    w_b2 = C.sb(f"s5_wc{su}", [128, 128], F32)
    ph_gamma_tile(C, W["L1aa_gq"], W["L1aa_gk"], w_b2, INV_SQRT_DK)
    ph_rope_norm(C, sc["kt"], sc["k"], 4, w_b=w_b2)
    C.pop_scope()
    ph_attention4(C, sc["q"], sc["k"], sc["v"], sc["o"],
                  n_heads=16, n_kv=4, mask_kind="window", window_chunks=8)
    ph_out_proj_add(C, sc["o"], W["L1aa_wo"], None, r_src, r_dst)


@_scoped
def ph_router_l1(C, ln_sbuf, wr_hbm, crows, c_em_bf):
    nc, T = C.nc, C.T
    u = C.uid()
    wr = C.sb(f"r1_wr{u}", [128, ND, 8], BF16)
    wrs = C.sb(f"r1_ws{u}", [128, ND, 8], F32)
    lg = C.sb(f"r1_lg{u}", [128, 8], F32)
    ex = C.sb(f"r1_ex{u}", [128, 8], F32)
    sg = C.sb(f"r1_sg{u}", [128, 8], F32)
    msk = C.sb(f"r1_mk{u}", [128, 8], F32)
    cc = C.sb(f"r1_cc{u}", [128, 8], F32)
    st = C.sb(f"r1_st{u}", [128, 6], F32)
    cem = C.sb(f"r1_ce{u}", [8, T], F32)
    stg = C.sb(f"r1_tg{u}", [8, 128], F32)
    identf = C.sb(f"r1_id{u}", [128, 128], F32)
    pl = C.psum(f"r1_pl{u}", [128, 512], F32)
    pr = C.psum(f"r1_pr{u}", [128, 512], F32)

    idf_h = C.inline("c_identf", np.eye(128, dtype=np.float32))
    ld0 = C.base("ld")
    with nc.Block() as blk:
        @blk.sync
        def _(e_):
            e_.dma_start(wrs[:, :, :], chunked(wr_hbm)).then_inc(C.sem("ld"), 16)
            e_.dma_start(identf[:, :], idf_h[:, :]).then_inc(C.sem("ld"), 16)
        @blk.vector
        def _(e_):
            e_.wait_ge(C.sem("ld"), ld0 + 32)
    C.bump("ld", 32)
    with nc.Block() as blk:
        @blk.vector
        def _(e_):
            e_.tensor_copy(wr[:, :, :], wrs[:, :, :])

    for tt in range(C.NT):
        with nc.Block() as blk:
            @blk.tensor
            def _(e_):
                for c in range(ND):
                    e_.matmul(pl[:, 0:8], ln_sbuf[:, c, tt * 128:(tt + 1) * 128],
                              wr[:, c, :], start=(c == 0), stop=(c == ND - 1))
        with nc.Block() as blk:
            @blk.vector
            def _(e_):
                C.fence(e_, e_.reduce_max(st[:, 0:1], pl[:, 0:8], axis=AX.X,
                                          negate=True), "fv")
                C.fence(e_, e_.tensor_scalar(lg[:, :], pl[:, 0:8], st[:, 0:1], None,
                                             ALU.add), "fv")
        with nc.Block() as blk:
            @blk.scalar
            def _(e_):
                C.fence(e_, e_.activation(ex[:, :], lg[:, :], AF.Exp, bias=0.0,
                                          scale=1.0), "fs")
                e_.activation(sg[:, :], lg[:, :], AF.Sign, bias=0.0, scale=1.0)
        with nc.Block() as blk:
            @blk.vector
            def _(e_):
                C.fence(e_, e_.tensor_scalar(sg[:, :], sg[:, :], 1.0, 1.0,
                                             ALU.add, ALU.min), "fv")
                C.fence(e_, e_.tensor_sub(sg[:, :], ex[:, :], sg[:, :]), "fv")
                C.fence(e_, e_.reduce_max(st[:, 1:2], sg[:, :], axis=AX.X), "fv")
                C.fence(e_, e_.tensor_scalar(st[:, 2:3], st[:, 1:2], 1.0, None,
                                             ALU.add), "fv")
                C.fence(e_, e_.reciprocal(st[:, 3:4], st[:, 2:3]), "fv")
                C.fence(e_, e_.tensor_scalar(msk[:, :], ex[:, :], st[:, 1:2], None,
                                             ALU.subtract), "fv")
        with nc.Block() as blk:
            @blk.scalar
            def _(e_):
                e_.activation(msk[:, :], msk[:, :], AF.Sign, bias=0.0, scale=1.0)
        with nc.Block() as blk:
            @blk.vector
            def _(e_):
                C.fence(e_, e_.tensor_scalar(msk[:, :], msk[:, :], 1.0, 1.0,
                                             ALU.add, ALU.min), "fv")
                C.fence(e_, e_.tensor_mul(cc[:, :], ex[:, :], msk[:, :]), "fv")
                C.fence(e_, e_.tensor_scalar(cc[:, :], cc[:, :], st[:, 3:4], None,
                                             ALU.mult), "fv")
        with nc.Block() as blk:
            @blk.tensor
            def _(e_):
                e_.transpose(pr[0:8, 0:128], cc[:, :], identf[:, :])
        with nc.Block() as blk:
            @blk.vector
            def _(e_):
                e_.tensor_copy(cem[:, tt * 128:(tt + 1) * 128], pr[0:8, 0:128])
    st0 = C.base("st")
    with nc.Block() as blk:
        @blk.vector
        def _(e_):
            e_.tensor_copy(c_em_bf[:, :], cem[:, :])
    with nc.Block() as blk:
        @blk.sync
        def _(e_):
            e_.dma_start(crows[0:1, :, :], c_em_bf[:, :]).then_inc(C.sem("st"), 16)
        @blk.scalar
        def _(e_):
            e_.wait_ge(C.sem("st"), st0 + 16)
    C.bump("st", 16)


@_scoped
def ph_expert_up_add(C, h_hbm, wu_hbm, bu_hbm, bsu_hbm, c_em_bf, crows,
                     r_src, r_dst, n_blocks, n_routed_blocks):
    nc, T = C.nc, C.T
    u = C.uid()
    NIB = n_blocks
    n_th = 2 if T >= 2048 else 1
    hw = T // n_th
    nhb = hw // 512
    hq = C.sb(f"up_hq{u}", [128, NIB, hw], BF16)
    wstage = [C.sb(f"up_ws{u}_{i}", [128, NIB, 128], F32) for i in range(2)]
    wuc = [C.sb(f"up_wc{u}_{i}", [128, NIB, 128], BF16) for i in range(2)]
    bu_s = C.sb(f"up_bu{u}", [8, D], F32)
    bu_b = C.sb(f"up_bb{u}", [8, D], BF16)
    bsu_c = C.sb(f"up_bs{u}", [128, ND], F32)
    cb = C.sb(f"up_cb{u}", [128, hw], F32)
    onesf = C.sb(f"up_on{u}", [1, 128], BF16)
    yb = [C.sb(f"up_yb{u}_{i}", [128, 512], F32) for i in range(2)]
    rtile = [C.sb(f"up_rt{u}_{i}", [128, 512], F32) for i in range(2)]
    ob = [C.sb(f"up_ob{u}_{i}", [128, 512], F32) for i in range(2)]
    ps = C.psum(f"up_ps{u}", [128, 2048], F32)
    pc = C.psum(f"up_pc{u}", [128, 1024], F32)

    ld0 = C.base("ld")
    with nc.Block() as blk:
        @blk.sync
        def _(e_):
            e_.dma_start(bu_s[:, :], bu_hbm[:, :]).then_inc(C.sem("ld"), 16)
            e_.dma_start(bsu_c[:, 0:ND], bsu_hbm.rearrange("(c p) -> p c", p=128),
                         allow_slow_non_contiguous=True).then_inc(C.sem("ld"), 16)
        @blk.gpsimd
        def _(e_):
            e_.memset(onesf[:, :], 1.0)
        @blk.vector
        def _(e_):
            e_.wait_ge(C.sem("ld"), ld0 + 32)
    C.bump("ld", 32)
    with nc.Block() as blk:
        @blk.vector
        def _(e_):
            e_.tensor_copy(bu_b[:, :], bu_s[:, :])

    b_ld = C.base("ld")
    b_mm = C.base("mm")
    b_ev = C.base("ev")
    b_x2 = C.base("x2")
    b_hl = C.base("hl")
    b_r = [C.base("r0"), C.base("r1")]
    b_s = [C.base("s0"), C.base("s1")]
    iu = 0
    nld = 0

    for th in range(n_th):
        t0 = th * hw
        with nc.Block() as blk:
            @blk.sync
            def _(e_):
                e_.dma_start(hq[:, :, :],
                             chunked(h_hbm[0:NIB * 128, t0:t0 + hw])).then_inc(
                                 C.sem("hl"), 16)
            @blk.vector
            def _(e_):
                e_.wait_ge(C.sem("hl"), b_hl + 16 * (th + 1))
        C.bump("hl", 16)
        for e in range(n_routed_blocks // 4):
            with nc.Block() as blk:
                @blk.tensor
                def _(e_):
                    for hb in range(nhb):
                        e_.matmul(pc[:, hb % 2 * 512:(hb % 2 + 1) * 512], onesf[:, :],
                                  crows[0:1, e, t0 + hb * 512:t0 + (hb + 1) * 512],
                                  start=True, stop=True)
                        assert nhb <= 2
            with nc.Block() as blk:
                @blk.vector
                def _(e_):
                    for hb in range(nhb):
                        C.fence(e_, e_.tensor_copy(cb[:, hb * 512:(hb + 1) * 512],
                                                   pc[:, hb % 2 * 512:(hb % 2 + 1) * 512]),
                                "fv")
                    for b4 in range(4):
                        C.fence(e_, e_.tensor_mul(hq[:, e * 4 + b4, :],
                                                  hq[:, e * 4 + b4, :],
                                                  cb[:, :]), "fv")
        for ec in range(ND):
            b = ec % 2
            with nc.Block() as blk:
                @blk.sync
                def _(e_):
                    e_.dma_start(
                        wstage[b][:, :, :],
                        wu_hbm[:, ec * 128:(ec + 1) * 128]
                        .rearrange("(c p) e -> p c e", p=128)).then_inc(C.sem("ld"), 16)
                @blk.vector
                def _(e_):
                    e_.wait_ge(C.sem("ld"), b_ld + 16 * (th * ND + ec + 1))
                    e_.tensor_copy(wuc[b][:, :, :], wstage[b][:, :, :])
            nld += 16
            with nc.Block() as blk:
                @blk.tensor
                def _(e_):
                    for tb in range(nhb):
                        i = iu + tb
                        pb = i % 2
                        for ib in range(NIB):
                            e_.matmul(ps[:, pb * 512:(pb + 1) * 512],
                                      wuc[b][:, ib, :],
                                      hq[:, ib, tb * 512:(tb + 1) * 512],
                                      start=(ib == 0), stop=False)
                        e_.matmul(ps[:, pb * 512:(pb + 1) * 512],
                                  bu_b[:, ec * 128:(ec + 1) * 128],
                                  c_em_bf[:, t0 + tb * 512:t0 + (tb + 1) * 512],
                                  start=False, stop=True).then_inc(C.sem("mm"), 1)
                @blk.sync
                def _(e_):
                    for tb in range(nhb):
                        i = iu + tb
                        if i >= 2:
                            e_.wait_ge(C.sem("x2"), b_x2 + i - 1)
                        e_.dma_start(
                            rtile[i % 2][:, :],
                            r_src[ec * 128:(ec + 1) * 128,
                                  t0 + tb * 512:t0 + (tb + 1) * 512]).then_inc(
                                      C.sem(f"r{i % 2}"), 16)
                @blk.scalar
                def _(e_):
                    for tb in range(nhb):
                        i = iu + tb
                        pb = i % 2
                        e_.wait_ge(C.sem("mm"), b_mm + i + 1)
                        e_.activation(yb[pb][:, :], ps[:, pb * 512:(pb + 1) * 512],
                                      AF.Identity, bias=bsu_c[:, ec:ec + 1],
                                      scale=1.0).then_inc(C.sem("ev"), 1)
                        e_.wait_ge(C.sem("x2"), b_x2 + i + 1)
                        e_.dma_start(
                            r_dst[ec * 128:(ec + 1) * 128,
                                  t0 + tb * 512:t0 + (tb + 1) * 512],
                            ob[pb][:, :]).then_inc(C.sem(f"s{pb}"), 16)
                @blk.gpsimd
                def _(e_):
                    for tb in range(nhb):
                        i = iu + tb
                        pb = i % 2
                        e_.wait_ge(C.sem("ev"), b_ev + i + 1)
                        e_.wait_ge(C.sem(f"r{pb}"), b_r[pb] + 16 * (i // 2 + 1))
                        if i >= 2:
                            e_.wait_ge(C.sem(f"s{pb}"), b_s[pb] + 16 * (i // 2))
                        e_.tensor_add(ob[pb][:, :], yb[pb][:, :], rtile[pb][:, :]
                                      ).then_inc(C.sem("x2"), 1)
            iu += nhb
            C.bump("mm", nhb)
            C.bump("ev", nhb)
            C.bump("x2", nhb)
        C.bump("ld", 16 * ND)
    ntot = iu
    C.bump("r0", 16 * ceil_div(ntot, 2))
    C.bump("r1", 16 * (ntot // 2))
    C.bump("s0", 16 * ceil_div(ntot, 2))
    C.bump("s1", 16 * (ntot // 2))
    e0, e1 = C.base("s0"), C.base("s1")
    with nc.Block() as blk:
        @blk.sync
        def _(e_):
            e_.wait_ge(C.sem("s0"), e0)
            e_.wait_ge(C.sem("s1"), e1)


def sublayer_L1_ffn(C, W, sc, r_src, r_dst):
    C.push_scope()
    su = C.uid()
    crows = C.sb(f"s6_cr{su}", [1, 8, C.T], BF16)
    c_em_bf = C.sb(f"s6_ce{su}", [8, C.T], BF16)
    C.push_scope()
    ln = C.sb(f"s6_ln{su}", [128, ND, C.T], BF16)
    ph_layernorm(C, r_src, ln)
    ph_router_l1(C, ln, W["L1f_wr"], crows, c_em_bf)
    ph_proj(C, ln, W["L1f_wd"], sc["h"], bias_hbm=W["L1f_bd"], func=AF.Gelu)
    C.pop_scope()
    ph_expert_up_add(C, sc["h"], W["L1f_wu"], W["L1f_bu"], W["L1f_bsu"],
                     c_em_bf, crows, r_src, r_dst, 36, 32)
    C.pop_scope()


@_scoped
def ph_phi(C, src_hbm, dst_hbm, *, scale=1.0):
    nc, T = C.nc, C.T
    u = C.uid()
    E = src_hbm.shape[0]
    NE = E // 128
    xin = [C.sb(f"ph_x{u}_{i}", [128, 512], BF16) for i in range(2)]
    m1 = C.sb(f"ph_m{u}", [128, 512], BF16)
    e1 = C.sb(f"ph_e{u}", [128, 512], BF16)
    r1 = C.sb(f"ph_r{u}", [128, 512], BF16)
    out = [C.sb(f"ph_o{u}_{i}", [128, 512], BF16) for i in range(2)]
    lns = C.constcol(math.log(scale))
    b_l = [C.base("l0"), C.base("l1")]
    b_x1 = C.base("x1")
    b_ev = C.base("ev")
    b_s = [C.base("s0"), C.base("s1")]
    n = NE * C.NB
    with nc.Block() as blk:
        @blk.sync
        def _(e_):
            for i in range(n):
                ec, tb = divmod(i, C.NB)
                if i >= 2:
                    e_.wait_ge(C.sem("x1"), b_x1 + 2 * (i - 1))
                e_.dma_start(xin[i % 2][:, :],
                             src_hbm[ec * 128:(ec + 1) * 128, tb * 512:(tb + 1) * 512]
                             ).then_inc(C.sem(["l0", "l1"][i % 2]), 16)
        @blk.vector
        def _(e_):
            for i in range(n):
                p = i % 2
                e_.wait_ge(C.sem(["l0", "l1"][p]), b_l[p] + 16 * (i // 2 + 1))
                e_.tensor_scalar(m1[:, :], xin[p][:, :], 0.0, None, ALU.min
                                 ).then_inc(C.sem("x1"), 1)
                e_.wait_ge(C.sem("ev"), b_ev + 2 * i + 2)
                if i >= 2:
                    e_.wait_ge(C.sem(f"s{p}"), b_s[p] + 16 * (i // 2))
                e_.tensor_add(out[p][:, :], e1[:, :], r1[:, :]).then_inc(C.sem("x1"), 1)
        @blk.scalar
        def _(e_):
            for i in range(n):
                ec, tb = divmod(i, C.NB)
                p = i % 2
                e_.wait_ge(C.sem("x1"), b_x1 + 2 * i + 1)
                C.fence(e_, e_.activation(e1[:, :], m1[:, :], AF.Exp,
                                          bias=lns[0:1, :] if False else lns,
                                          scale=1.0), "fs")
                e_.activation(r1[:, :], xin[p][:, :], AF.Relu, bias=0.0,
                              scale=scale).then_inc(C.sem("ev"), 2)
                e_.wait_ge(C.sem("x1"), b_x1 + 2 * i + 2)
                e_.dma_start(dst_hbm[ec * 128:(ec + 1) * 128, tb * 512:(tb + 1) * 512],
                             out[p][:, :]).then_inc(C.sem(f"s{p}"), 16)
    C.bump("l0", 16 * ceil_div(n, 2))
    C.bump("l1", 16 * (n // 2))
    C.bump("x1", 2 * n)
    C.bump("ev", 2 * n)
    C.bump("s0", 16 * ceil_div(n, 2))
    C.bump("s1", 16 * (n // 2))
    e0, e1v = C.base("s0"), C.base("s1")
    with nc.Block() as blk:
        @blk.sync
        def _(e_):
            e_.wait_ge(C.sem("s0"), e0)
            e_.wait_ge(C.sem("s1"), e1v)


@_scoped
def ph_pkv_scan(C, phik_hbm, v_hbm, phiq_hbm, ctx_hbm):
    nc, T = C.nc, C.T
    u = C.uid()
    ka = [C.sb(f"sc_k{u}_{i}", [128, T], BF16) for i in range(2)]
    va = [C.sb(f"sc_v{u}_{i}", [128, T], BF16) for i in range(2)]
    qa = [C.sb(f"sc_q{u}_{i}", [128, T], BF16) for i in range(2)]
    pkv = C.sb(f"sc_p{u}", [128, T], F32)
    kv = C.sb(f"sc_c{u}", [128, T], F32)
    ctx = [C.sb(f"sc_o{u}_{i}", [128, T], BF16) for i in range(2)]
    b_l = [C.base("l0"), C.base("l1")]
    b_x1 = C.base("x1")
    b_s = [C.base("s0"), C.base("s1")]
    with nc.Block() as blk:
        @blk.sync
        def _(e_):
            for c in range(ND):
                p = c % 2
                if c >= 2:
                    e_.wait_ge(C.sem("x1"), b_x1 + 4 * (c - 1))
                e_.dma_start(ka[p][:, :], phik_hbm[c * 128:(c + 1) * 128, :]
                             ).then_inc(C.sem(["l0", "l1"][p]), 16)
                e_.dma_start(va[p][:, :], v_hbm[c * 128:(c + 1) * 128, :]
                             ).then_inc(C.sem(["l0", "l1"][p]), 16)
                e_.dma_start(qa[p][:, :], phiq_hbm[c * 128:(c + 1) * 128, :]
                             ).then_inc(C.sem(["l0", "l1"][p]), 16)
        @blk.vector
        def _(e_):
            for c in range(ND):
                p = c % 2
                e_.wait_ge(C.sem(["l0", "l1"][p]), b_l[p] + 48 * (c // 2 + 1))
                C.fence(e_, e_.tensor_mul(pkv[:, :], ka[p][:, :], va[p][:, :]), "fv")
                C.fence(e_, e_.tensor_tensor_scan(kv[:, :], pkv[:, :], pkv[:, :],
                                                  0.0, ALU.add, ALU.bypass), "fv")
                if c >= 2:
                    e_.wait_ge(C.sem(f"s{p}"), b_s[p] + 16 * (c // 2))
                e_.tensor_mul(ctx[p][:, :], qa[p][:, :], kv[:, :]).then_inc(
                    C.sem("x1"), 4)
        @blk.scalar
        def _(e_):
            for c in range(ND):
                p = c % 2
                e_.wait_ge(C.sem("x1"), b_x1 + 4 * c + 4)
                e_.dma_start(ctx_hbm[c * 128:(c + 1) * 128, :], ctx[p][:, :]
                             ).then_inc(C.sem(f"s{p}"), 16)
    C.bump("l0", 48 * ceil_div(ND, 2))
    C.bump("l1", 48 * (ND // 2))
    C.bump("x1", 4 * ND)
    C.bump("s0", 16 * ceil_div(ND, 2))
    C.bump("s1", 16 * (ND // 2))
    e0, e1 = C.base("s0"), C.base("s1")
    with nc.Block() as blk:
        @blk.sync
        def _(e_):
            e_.wait_ge(C.sem("s0"), e0)
            e_.wait_ge(C.sem("s1"), e1)


def sublayer_L1_attn_b(C, W, sc, r_src, r_dst):
    C.push_scope()
    su = C.uid()
    ln = C.sb(f"s7_ln{su}", [128, ND, C.T], BF16)
    ph_layernorm(C, r_src, ln)
    ph_proj(C, ln, W["L1ab_wq"], sc["q"], bias_hbm=W["L1ab_bq"])
    ph_proj(C, ln, W["L1ab_wk"], sc["k"], bias_hbm=W["L1ab_bk"])
    ph_proj(C, ln, W["L1ab_wv"], sc["g3"], bias_hbm=W["L1ab_bv"])
    C.pop_scope()
    ph_phi(C, sc["q"], sc["g1"], scale=INV_SQRT_DK)
    ph_phi(C, sc["k"], sc["g2"])
    ph_pkv_scan(C, sc["g2"], sc["g3"], sc["g1"], sc["o"])
    ph_out_proj_add(C, sc["o"], W["L1ab_wo"], W["L1ab_bo"], r_src, r_dst)



@_scoped
def ph_attention2(C, q_hbm, k_hbm, v_hbm, o_hbm, *, n_heads=16, n_kv=16,
                  mask_kind="causal", window_chunks=8):
    nc, T = C.nc, C.T
    NT = C.NT
    u = C.uid()
    EK = n_kv * 128
    GR = n_heads // n_kv

    ksb = C.sb(f"a2_k{u}", [128, n_kv, T], BF16)
    vsb = C.sb(f"a2_v{u}", [128, NT, EK], BF16)
    qsb = [C.sb(f"a2_q{u}_{i}", [128, n_heads, 512], BF16) for i in range(2)]
    pt = C.sb(f"a2_ptb{u}", [128, NT, 512], BF16)
    osb = [C.sb(f"a2_o{u}_{i}", [128, 512], BF16) for i in range(2)]
    mcol = C.sb(f"a2_mc{u}", [128, 4], F32)
    tmpm = C.sb(f"a2_tm{u}", [128, 1], F32)
    nmb_r = C.sb(f"a2_nb{u}", [1, 2], F32)
    mbk = C.sb(f"a2_mb{u}", [128, 1], F32)
    frow = C.sb(f"a2_fr{u}", [1, 512], F32)
    f_b = C.sb(f"a2_fb{u}", [128, 512], F32)
    onesc = C.sb(f"a2_on{u}", [128, 1], BF16)
    onesf = C.sb(f"a2_of{u}", [1, 128], F32)
    identf = C.sb(f"a2_if{u}", [128, 128], F32)
    zdiag = C.sb(f"a2_zd{u}", [128, 128], BF16)
    zwin = C.sb(f"a2_zw{u}", [128, 128], BF16)
    ediag = C.sb(f"a2_ed{u}", [128, 128], F32)

    pS = C.psum(f"a2_ps{u}", [128, 2048], F32)
    pST = C.psum(f"a2_pq{u}", [128, 1024], F32)
    pR = C.psum(f"a2_pr{u}", [128, 512], F32)
    pO = C.psum(f"a2_po{u}", [128, 512], F32)

    E_CONST = float(np.exp(np.float32(1.0)))
    idf_h = C.inline("c_identf", np.eye(128, dtype=np.float32))
    kq = np.arange(128)
    zdiag_h = C.inline("c_zdiag", (kq[:, None] <= kq[None, :]).astype(ml_dtypes.bfloat16))
    zwin_h = C.inline("c_zwin", (kq[:, None] >= kq[None, :]).astype(ml_dtypes.bfloat16))
    ediag_h = C.inline("c_ediag", np.where(kq[:, None] <= kq[None, :],
                                           np.float32(E_CONST),
                                           np.float32(1.0)).astype(np.float32))

    ld0 = C.base("ld")
    with nc.Block() as blk:
        @blk.sync
        def _(e_):
            e_.dma_start(ksb[:, :, :], chunked(k_hbm[0:EK, :])).then_inc(C.sem("ld"), 16)
            e_.dma_start(vsb[:, :, :],
                         v_hbm[:, 0:EK].rearrange("(c p) t -> p c t", p=128)
                         ).then_inc(C.sem("ld"), 16)
            e_.dma_start(identf[:, :], idf_h[:, :]).then_inc(C.sem("ld"), 16)
            e_.dma_start(zdiag[:, :], zdiag_h[:, :]).then_inc(C.sem("ld"), 16)
            e_.dma_start(zwin[:, :], zwin_h[:, :]).then_inc(C.sem("ld"), 16)
            e_.dma_start(ediag[:, :], ediag_h[:, :]).then_inc(C.sem("ld"), 16)
        @blk.gpsimd
        def _(e_):
            e_.memset(onesc[:, :], 1.0)
            e_.memset(onesf[:, :], 1.0)
        @blk.vector
        def _(e_):
            e_.wait_ge(C.sem("ld"), ld0 + 16 * 6)
        @blk.tensor
        def _(e_):
            e_.wait_ge(C.sem("ld"), ld0 + 16 * 6)
        @blk.scalar
        def _(e_):
            e_.wait_ge(C.sem("ld"), ld0 + 16 * 6)
    C.bump("ld", 16 * 6)

    units = [(qb, h) for qb in range(C.NB) for h in range(n_heads)]
    nu = len(units)

    def qt_kparams(qt_abs):
        k_lo = max(0, qt_abs - window_chunks) if mask_kind == "window" else 0
        k_hi = NT if mask_kind == "trilbias" else qt_abs + 1
        return k_lo, k_hi

    def blk_kparams(qb):
        hi = NT if mask_kind == "trilbias" else 4 * (qb + 1)
        lo = max(0, 4 * qb - window_chunks) if mask_kind == "window" else 0
        return lo, hi


    def tile_mask_ops(qb, kc):
        ops = []
        qt_above = [qt for qt in range(4) if kc > qb * 4 + qt]
        qt_below = [qt for qt in range(4) if kc < qb * 4 + qt]
        if mask_kind in ("causal", "window"):
            if qt_above:
                ops.append(("zs", min(qt_above) * 128, (max(qt_above) + 1) * 128))
            if kc - qb * 4 in range(4):
                ops.append(("tt", zdiag, (kc - qb * 4) * 128))
            if mask_kind == "window":
                far = [qt for qt in range(4) if kc < qb * 4 + qt - window_chunks]
                if far:
                    ops.append(("zs", min(far) * 128, (max(far) + 1) * 128))
                if (kc + window_chunks - qb * 4) in range(4):
                    ops.append(("tt", zwin, (kc + window_chunks - qb * 4) * 128))
        else:
            if qt_below:
                ops.append(("es", min(qt_below) * 128, (max(qt_below) + 1) * 128))
            if kc - qb * 4 in range(4):
                ops.append(("tt", ediag, (kc - qb * 4) * 128))
        return ops

    sgroups = []
    nkbs = []
    pools = []
    for (qb, h) in units:
        lo, hi = blk_kparams(qb)
        nkbs.append(hi - lo)
        gs = []
        for qt in range(4):
            qt_abs = qb * 4 + qt
            klo, khi = qt_kparams(qt_abs)
            for g in range(klo, khi, 4):
                gs.append((qt, g, min(4, khi - g)))
        sgroups.append(gs)
        pu = [len(tile_mask_ops(qb, lo + j)) for j in range(hi - lo)]
        pools.append(pu)

    mm_cum = [0]
    x1_cum = [0]
    x2_cum = [0]
    ev_cum = [0]
    pl_cum = [0]
    for ui in range(nu):
        mm_cum.append(mm_cum[-1] + len(sgroups[ui]) + nkbs[ui] + 5)
        x1_cum.append(x1_cum[-1] + 6)
        x2_cum.append(x2_cum[-1] + len(sgroups[ui]))
        ev_cum.append(ev_cum[-1] + nkbs[ui])
        pl_cum.append(pl_cum[-1] + sum(1 for p in pools[ui] if p > 0))

    b_mm = C.base("mm")
    b_x1 = C.base("x1")
    b_x2 = C.base("x2")
    b_ev = C.base("ev")
    b_pl = C.base("hl")
    b_q = [C.base("q0"), C.base("q1")]
    b_s = [C.base("s0"), C.base("s1")]

    with nc.Block() as blk:
        @blk.sync
        def _(e_):
            for qb in range(C.NB):
                if qb >= 2:
                    done = len([x for x in units if x[0] <= qb - 2])
                    e_.wait_ge(C.sem("mm"), b_mm + mm_cum[done])
                e_.dma_start(qsb[qb % 2][:, :, :],
                             chunked(q_hbm[:, qb * 512:(qb + 1) * 512])).then_inc(
                                 C.sem(f"q{qb % 2}"), 16)

        @blk.tensor
        def _(e_):
            gidx = 0
            for ui, (qb, h) in enumerate(units):
                lo, hi = blk_kparams(qb)
                nkb = hi - lo
                kv = h // GR
                MM0 = b_mm + mm_cum[ui]
                X10 = b_x1 + x1_cum[ui]
                EV0 = b_ev + ev_cum[ui]
                nS = len(sgroups[ui])
                e_.wait_ge(C.sem(f"q{qb % 2}"), b_q[qb % 2] + 16 * (qb // 2 + 1))
                for gl, (qt, g, kw) in enumerate(sgroups[ui]):
                    if gidx >= 4:
                        e_.wait_ge(C.sem("x2"), b_x2 + gidx - 3)
                    bank = gidx % 4
                    e_.matmul(pS[:, bank * 512:bank * 512 + kw * 128],
                              qsb[qb % 2][:, h, qt * 128:(qt + 1) * 128],
                              ksb[:, kv, g * 128:(g + kw) * 128],
                              start=True, stop=True).then_inc(C.sem("mm"), 1)
                    gidx += 1
                e_.wait_ge(C.sem("x1"), X10 + 1)
                e_.transpose(pR[0:1, 0:128], tmpm[:, 0:1], identf[:, :]
                             ).then_inc(C.sem("mm"), 1)
                e_.wait_ge(C.sem("x1"), X10 + 2)
                e_.matmul(pO[:, 0:1], onesf[:, :], nmb_r[0:1, 0:1],
                          start=True, stop=True).then_inc(C.sem("mm"), 1)
                for j in range(nkb):
                    pb = j % 2
                    if j == 0:
                        if ui >= 1:
                            pass
                    if j >= 2:
                        e_.wait_ge(C.sem("ev"), EV0 + j - 1)
                    e_.matmul(pST[:, pb * 512:(pb + 1) * 512],
                              ksb[:, kv, (lo + j) * 128:(lo + j + 1) * 128],
                              qsb[qb % 2][:, h, 0:512],
                              start=True, stop=True).then_inc(C.sem("mm"), 1)
                e_.wait_ge(C.sem("ev"), EV0 + nkb)
                e_.wait_ge(C.sem("hl"), b_pl + pl_cum[ui + 1])
                ins = None
                for j in range(nkb):
                    ins = e_.matmul(pR[0:1, 0:512], onesc[:, :], pt[:, j, :],
                                    start=(j == 0), stop=(j == nkb - 1))
                ins.then_inc(C.sem("mm"), 1)
                e_.wait_ge(C.sem("x1"), X10 + 4)
                e_.matmul(pR[:, 0:512], onesf[:, :], frow[0:1, 0:512],
                          start=True, stop=True).then_inc(C.sem("mm"), 1)
                e_.wait_ge(C.sem("x1"), X10)
                ins = None
                for j in range(nkb):
                    ins = e_.matmul(pO[:, 0:512],
                                    vsb[:, lo + j, kv * 128:(kv + 1) * 128],
                                    pt[:, j, :],
                                    start=(j == 0), stop=(j == nkb - 1))
                ins.then_inc(C.sem("mm"), 1)

        @blk.vector
        def _(e_):
            gidx = 0
            for ui, (qb, h) in enumerate(units):
                lo, hi = blk_kparams(qb)
                nkb = hi - lo
                MM0 = b_mm + mm_cum[ui]
                X10 = b_x1 + x1_cum[ui]
                nS = len(sgroups[ui])
                for gl, (qt, g, kw) in enumerate(sgroups[ui]):
                    bank = gidx % 4
                    e_.wait_ge(C.sem("mm"), MM0 + gl + 1)
                    if gl < 4:
                        pass
                    first_of_qt = (gl == 0) or (sgroups[ui][gl - 1][0] != qt)
                    if first_of_qt:
                        C.fence(e_, e_.reduce_max(
                            mcol[:, qt:qt + 1], pS[:, bank * 512:bank * 512 + kw * 128],
                            axis=AX.X, negate=True), "fv")
                    else:
                        C.fence(e_, e_.reduce_max(
                            tmpm[:, 0:1], pS[:, bank * 512:bank * 512 + kw * 128],
                            axis=AX.X, negate=True), "fv")
                        C.fence(e_, e_.tensor_tensor(
                            mcol[:, qt:qt + 1], mcol[:, qt:qt + 1], tmpm[:, 0:1],
                            ALU.min), "fv")
                    nc.vector.engine_nop().then_inc(C.sem("x2"), 1)
                    gidx += 1
                C.fence(e_, e_.tensor_reduce(tmpm[:, 0:1], mcol[:, 0:4], AX.X,
                                             ALU.min), "fv")
                nc.vector.engine_nop().then_inc(C.sem("x1"), 1)
                e_.wait_ge(C.sem("mm"), MM0 + nS + 1)
                C.fence(e_, e_.tensor_reduce(nmb_r[0:1, 0:1], pR[0:1, 0:128], AX.X,
                                             ALU.min), "fv")
                nc.vector.engine_nop().then_inc(C.sem("x1"), 1)
                e_.wait_ge(C.sem("mm"), MM0 + nS + 2)
                e_.tensor_copy(mbk[:, 0:1], pO[:, 0:1]).then_inc(C.sem("x1"), 1)
                e_.wait_ge(C.sem("mm"), MM0 + nS + 2 + nkb + 1)
                C.fence(e_, e_.reciprocal(frow[0:1, 0:512], pR[0:1, 0:512]), "fv")
                nc.vector.engine_nop().then_inc(C.sem("x1"), 1)
                e_.wait_ge(C.sem("mm"), MM0 + nS + 2 + nkb + 2)
                e_.tensor_copy(f_b[:, :], pR[:, 0:512]).then_inc(C.sem("x1"), 1)
                e_.wait_ge(C.sem("mm"), MM0 + nS + 2 + nkb + 3)
                e_.wait_ge(C.sem("x1"), X10 + 5)
                if ui >= 2:
                    e_.wait_ge(C.sem(f"s{ui % 2}"), b_s[ui % 2] + 16 * (ui // 2))
                e_.tensor_tensor(osb[ui % 2][:, :], pO[:, 0:512], f_b[:, :],
                                 ALU.mult).then_inc(C.sem("x1"), 1)

        @blk.scalar
        def _(e_):
            for ui, (qb, h) in enumerate(units):
                lo, hi = blk_kparams(qb)
                nkb = hi - lo
                MM0 = b_mm + mm_cum[ui]
                X10 = b_x1 + x1_cum[ui]
                nS = len(sgroups[ui])
                for j in range(nkb):
                    pb = j % 2
                    e_.wait_ge(C.sem("mm"), MM0 + nS + 2 + j + 1)
                    if j == 0:
                        e_.wait_ge(C.sem("x1"), X10 + 3)
                        if ui >= 1:
                            e_.wait_ge(C.sem("mm"), MM0)
                    e_.activation(pt[:, j, :], pST[:, pb * 512:(pb + 1) * 512],
                                  AF.Exp, bias=mbk[:, 0:1], scale=1.0).then_inc(
                                      C.sem("ev"), 1)
                e_.wait_ge(C.sem("x1"), X10 + 6)
                e_.dma_start(o_hbm[h * 128:(h + 1) * 128, qb * 512:(qb + 1) * 512],
                             osb[ui % 2][:, :]).then_inc(C.sem(f"s{ui % 2}"), 16)

        @blk.gpsimd
        def _(e_):
            for ui, (qb, h) in enumerate(units):
                lo, hi = blk_kparams(qb)
                nkb = hi - lo
                EV0 = b_ev + ev_cum[ui]
                for j in range(nkb):
                    kc = lo + j
                    ops = tile_mask_ops(qb, kc)
                    if not ops:
                        continue
                    e_.wait_ge(C.sem("ev"), EV0 + j + 1)
                    ins = None
                    for op in ops:
                        if op[0] == "zs":
                            ins = e_.memset(pt[:, j, op[1]:op[2]], 0.0)
                        elif op[0] == "es":
                            ins = e_.tensor_scalar(pt[:, j, op[1]:op[2]],
                                                   pt[:, j, op[1]:op[2]],
                                                   E_CONST, None, ALU.mult)
                        else:
                            col = op[2]
                            ins = e_.tensor_mul(pt[:, j, col:col + 128],
                                                pt[:, j, col:col + 128], op[1][:, :])
                        C.fence(e_, ins, "fp")
                    nc.gpsimd.engine_nop().then_inc(C.sem("hl"), 1)

    C.bump("q0", 16 * ceil_div(C.NB, 2))
    C.bump("q1", 16 * (C.NB // 2))
    C.bump("mm", mm_cum[-1])
    C.bump("x1", x1_cum[-1])
    C.bump("x2", x2_cum[-1])
    C.bump("ev", ev_cum[-1])
    C.bump("hl", pl_cum[-1])
    C.bump("s0", 16 * ceil_div(nu, 2))
    C.bump("s1", 16 * (nu // 2))
    e0, e1 = C.base("s0"), C.base("s1")
    with nc.Block() as blk:
        @blk.sync
        def _(e_):
            e_.wait_ge(C.sem("s0"), e0)
            e_.wait_ge(C.sem("s1"), e1)


@_scoped
def ph_attention3(C, q_hbm, k_hbm, v_hbm, o_hbm, *, n_heads=16, n_kv=16,
                  mask_kind="causal", window_chunks=8):
    nc, T = C.nc, C.T
    NT = C.NT
    u = C.uid()
    EK = n_kv * 128
    GR = n_heads // n_kv

    ksb = C.sb(f"a3_k{u}", [128, n_kv, T], BF16)
    vsb = C.sb(f"a3_v{u}", [128, NT, EK], BF16)
    qsb = [C.sb(f"a3_q{u}_{i}", [128, n_heads, 512], BF16) for i in range(2)]
    pt = C.sb(f"a3_ptb{u}", [128, NT, 512], BF16)
    osb = [C.sb(f"a3_o{u}_{i}", [128, 512], BF16) for i in range(2)]
    mst = C.sb(f"a3_ms{u}", [128, 16], F32)
    tmpm = C.sb(f"a3_tm{u}", [128, 1], F32)
    nmb_r = C.sb(f"a3_nb{u}", [1, 2], F32)
    mbk = C.sb(f"a3_mb{u}", [128, 1], F32)
    frow = C.sb(f"a3_fr{u}", [1, 512], F32)
    f_b = C.sb(f"a3_fb{u}", [128, 512], F32)
    onesc = C.sb(f"a3_on{u}", [128, 1], BF16)
    onesf = C.sb(f"a3_of{u}", [1, 128], F32)
    identf = C.sb(f"a3_if{u}", [128, 128], F32)
    zdiag = C.sb(f"a3_zd{u}", [128, 128], BF16)
    zwin = C.sb(f"a3_zw{u}", [128, 128], BF16)
    ediag = C.sb(f"a3_ed{u}", [128, 128], F32)

    pS = C.psum(f"a3_ps{u}", [128, 2048], F32)
    pST = C.psum(f"a3_pq{u}", [128, 1024], F32)
    pR = C.psum(f"a3_pr{u}", [128, 512], F32)
    pO = C.psum(f"a3_po{u}", [128, 512], F32)

    E_CONST = float(np.exp(np.float32(1.0)))
    idf_h = C.inline("c_identf", np.eye(128, dtype=np.float32))
    kq = np.arange(128)
    zdiag_h = C.inline("c_zdiag", (kq[:, None] <= kq[None, :]).astype(ml_dtypes.bfloat16))
    zwin_h = C.inline("c_zwin", (kq[:, None] >= kq[None, :]).astype(ml_dtypes.bfloat16))
    ediag_h = C.inline("c_ediagf", np.where(kq[:, None] <= kq[None, :],
                                            np.float32(E_CONST),
                                            np.float32(1.0)).astype(np.float32))

    ld0 = C.base("ld")
    with nc.Block() as blk:
        @blk.sync
        def _(e_):
            e_.dma_start(ksb[:, :, :], chunked(k_hbm[0:EK, :])).then_inc(C.sem("ld"), 16)
            e_.dma_start(vsb[:, :, :],
                         v_hbm[:, 0:EK].rearrange("(c p) t -> p c t", p=128)
                         ).then_inc(C.sem("ld"), 16)
            e_.dma_start(identf[:, :], idf_h[:, :]).then_inc(C.sem("ld"), 16)
            e_.dma_start(zdiag[:, :], zdiag_h[:, :]).then_inc(C.sem("ld"), 16)
            e_.dma_start(zwin[:, :], zwin_h[:, :]).then_inc(C.sem("ld"), 16)
            e_.dma_start(ediag[:, :], ediag_h[:, :]).then_inc(C.sem("ld"), 16)
        @blk.gpsimd
        def _(e_):
            e_.memset(onesc[:, :], 1.0)
            e_.memset(onesf[:, :], 1.0)
        @blk.vector
        def _(e_):
            e_.wait_ge(C.sem("ld"), ld0 + 16 * 6)
        @blk.tensor
        def _(e_):
            e_.wait_ge(C.sem("ld"), ld0 + 16 * 6)
        @blk.scalar
        def _(e_):
            e_.wait_ge(C.sem("ld"), ld0 + 16 * 6)
    C.bump("ld", 16 * 6)

    units = [(qb, h) for qb in range(C.NB) for h in range(n_heads)]
    nu = len(units)

    def qt_kparams(qt_abs):
        k_lo = max(0, qt_abs - window_chunks) if mask_kind == "window" else 0
        k_hi = NT if mask_kind == "trilbias" else qt_abs + 1
        return k_lo, k_hi

    def blk_kparams(qb):
        hi = NT if mask_kind == "trilbias" else 4 * (qb + 1)
        lo = max(0, 4 * qb - window_chunks) if mask_kind == "window" else 0
        return lo, hi

    def valid_qslice(qb, kc):
        if mask_kind == "trilbias":
            return 0, 512
        qts = [qt for qt in range(4)
               if (qb * 4 + qt >= kc) and
               (mask_kind != "window" or kc >= qb * 4 + qt - window_chunks)]
        return min(qts) * 128, (max(qts) + 1) * 128

    def dve_patterns(qb, kc):
        ops = []
        if kc - qb * 4 in range(4):
            qt = kc - qb * 4
            ops.append((ediag if mask_kind == "trilbias" else zdiag, qt * 128))
        if mask_kind == "window" and (kc + window_chunks - qb * 4) in range(4):
            ops.append((zwin, (kc + window_chunks - qb * 4) * 128))
        if mask_kind == "trilbias":
            qt_below = [qt for qt in range(4) if kc < qb * 4 + qt]
            if qt_below:
                ops.append(("escale", min(qt_below) * 128, (max(qt_below) + 1) * 128))
        else:
            qt_above = [qt for qt in range(4) if kc > qb * 4 + qt]
            if qt_above:
                ops.append(("zero", min(qt_above) * 128, (max(qt_above) + 1) * 128))
            if mask_kind == "window":
                far = [qt for qt in range(4) if kc < qb * 4 + qt - window_chunks]
                if far:
                    ops.append(("zero", min(far) * 128, (max(far) + 1) * 128))
        return ops

    sgroups = []
    nkbs = []
    los = []
    for (qb, h) in units:
        lo, hi = blk_kparams(qb)
        nkbs.append(hi - lo)
        los.append(lo)
        gs = []
        for qt in range(4):
            qt_abs = qb * 4 + qt
            klo, khi = qt_kparams(qt_abs)
            for g in range(klo, khi, 4):
                gs.append((qt, g, min(4, khi - g)))
        sgroups.append(gs)

    tab = {}

    def emit(dry):
        cnt = {"mm": 0, "x1": 0, "x2": 0, "ev": 0, "dp": 0}

        def inc(ins, role, key=None):
            cnt[role] += 1
            if dry:
                if key is not None:
                    tab[key] = cnt[role]
            else:
                ins.then_inc(C.sem(role), 1)
            return cnt[role]

        def wv(e_, role, key, base):
            if not dry:
                e_.wait_ge(C.sem(role), base[role] + tab[key])

        return cnt, inc, wv

    b = {r: C.base(r) for r in ("mm", "x1", "x2", "ev", "dp")}
    b_q = [C.base("q0"), C.base("q1")]
    b_s = [C.base("s0"), C.base("s1")]

    def run_pe(e_, dry):
        cnt, inc, wvf = emit(dry)
        gidx = 0
        for it in range(nu + 1):
            if it < nu:
                ui = it
                qb, h = units[ui]
                kv = h // GR
                if not dry:
                    e_.wait_ge(C.sem(f"q{qb % 2}"), b_q[qb % 2] + 16 * (qb // 2 + 1))
                for gl, (qt, g, kw) in enumerate(sgroups[ui]):
                    if not dry and gidx >= 4:
                        e_.wait_ge(C.sem("x2"), b["x2"] + gidx - 3)
                    bank = gidx % 4
                    ins = None if dry else e_.matmul(
                        pS[:, bank * 512:bank * 512 + kw * 128],
                        qsb[qb % 2][:, h, qt * 128:(qt + 1) * 128],
                        ksb[:, kv, g * 128:(g + kw) * 128],
                        start=True, stop=True)
                    inc(ins, "mm", ("smm", ui, gl))
                    gidx += 1
            if it >= 1:
                ui = it - 1
                qb, h = units[ui]
                lo, hi = blk_kparams(qb)
                nkb = hi - lo
                kv = h // GR
                if not dry:
                    e_.wait_ge(C.sem("x1"), b["x1"] + tab[("colmin", ui)])
                    if ui >= 1:
                        e_.wait_ge(C.sem("x1"), b["x1"] + tab[("fbev", ui - 1)])
                ins = None if dry else e_.transpose(pR[0:1, 0:128], tmpm[:, 0:1],
                                                    identf[:, :])
                inc(ins, "mm", ("tpose", ui))
                if not dry:
                    e_.wait_ge(C.sem("x1"), b["x1"] + tab[("rowmin", ui)])
                ins = None if dry else e_.matmul(pO[:, 0:1], onesf[:, :],
                                                 nmb_r[0:1, 0:1], start=True, stop=True)
                inc(ins, "mm", ("mbbc", ui))
                for j in range(nkb):
                    pb = j % 2
                    if not dry and j < 2 and ui >= 1:
                        nkp = nkbs[ui - 1]
                        e_.wait_ge(C.sem("ev"), b["ev"] + tab[("exp", ui - 1, nkp - 1)])
                    if not dry and j >= 2:
                        e_.wait_ge(C.sem("ev"), b["ev"] + tab[("exp", ui, j - 2)])
                    ins = None if dry else e_.matmul(
                        pST[:, pb * 512:(pb + 1) * 512],
                        ksb[:, kv, (lo + j) * 128:(lo + j + 1) * 128],
                        qsb[qb % 2][:, h, 0:512],
                        start=True, stop=True)
                    inc(ins, "mm", ("stmm", ui, j))
                if not dry:
                    e_.wait_ge(C.sem("ev"), b["ev"] + tab[("exp", ui, nkb - 1)])
                    e_.wait_ge(C.sem("dp"), b["dp"] + tab[("pat", ui)])
                ins = None
                for j in range(nkb):
                    if not dry:
                        ins = e_.matmul(pR[0:1, 0:512], onesc[:, :], pt[:, j, :],
                                        start=(j == 0), stop=(j == nkb - 1))
                inc(ins, "mm", ("summ", ui))
                if not dry:
                    e_.wait_ge(C.sem("x1"), b["x1"] + tab[("recip", ui)])
                ins = None if dry else e_.matmul(pR[:, 0:512], onesf[:, :],
                                                 frow[0:1, 0:512], start=True, stop=True)
                inc(ins, "mm", ("fbbc", ui))
                if not dry:
                    if ui >= 1:
                        e_.wait_ge(C.sem("x1"), b["x1"] + tab[("oev", ui - 1)])
                ins = None
                for j in range(nkb):
                    if not dry:
                        ins = e_.matmul(pO[:, 0:512],
                                        vsb[:, lo + j, kv * 128:(kv + 1) * 128],
                                        pt[:, j, :],
                                        start=(j == 0), stop=(j == nkb - 1))
                inc(ins, "mm", ("pvmm", ui))
        return cnt

    def run_dve(e_, dry):
        cnt, inc, wvf = emit(dry)
        gidx = 0
        for it in range(nu + 1):
            if it < nu:
                ui = it
                qb, h = units[ui]
                nS = len(sgroups[ui])
                for gl, (qt, g, kw) in enumerate(sgroups[ui]):
                    bank = gidx % 4
                    if not dry:
                        e_.wait_ge(C.sem("mm"), b["mm"] + tab[("smm", ui, gl)])
                    ins = None if dry else e_.reduce_max(
                        mst[:, gl:gl + 1], pS[:, bank * 512:bank * 512 + kw * 128],
                        axis=AX.X, negate=True)
                    inc(ins, "x2", ("sred", ui, gl))
                    gidx += 1
                if not dry:
                    e_.wait_ge(C.sem("x2"), b["x2"] + tab[("sred", ui, nS - 1)])
                    if ui >= 1:
                        e_.wait_ge(C.sem("mm"), b["mm"] + tab[("tpose", ui - 1)])
                    C.fence(e_, e_.tensor_reduce(tmpm[:, 0:1], mst[:, 0:nS], AX.X,
                                                 ALU.min), "fv")
                    ins = nc.vector.engine_nop()
                else:
                    ins = None
                inc(ins, "x1", ("colmin", ui))
            if it >= 1:
                ui = it - 1
                qb, h = units[ui]
                lo, hi = blk_kparams(qb)
                nkb = hi - lo
                nS = len(sgroups[ui])
                if not dry:
                    e_.wait_ge(C.sem("mm"), b["mm"] + tab[("tpose", ui)])
                    C.fence(e_, e_.tensor_reduce(nmb_r[0:1, 0:1], pR[0:1, 0:128], AX.X,
                                                 ALU.min), "fv")
                    ins = nc.vector.engine_nop()
                else:
                    ins = None
                inc(ins, "x1", ("rowmin", ui))
                if not dry:
                    e_.wait_ge(C.sem("mm"), b["mm"] + tab[("mbbc", ui)])
                    if ui >= 1:
                        nkp = nkbs[ui - 1]
                        e_.wait_ge(C.sem("ev"), b["ev"] + tab[("exp", ui - 1, nkp - 1)])
                ins = None if dry else e_.tensor_copy(mbk[:, 0:1], pO[:, 0:1])
                inc(ins, "x1", ("mbk", ui))
                npat = 0
                for j in range(nkb):
                    ops = dve_patterns(qb, lo + j)
                    if not ops:
                        continue
                    if not dry:
                        e_.wait_ge(C.sem("ev"), b["ev"] + tab[("exp", ui, j)])
                    for op in ops:
                        if op[0] == "escale":
                            ins = None if dry else e_.tensor_scalar(
                                pt[:, j, op[1]:op[2]], pt[:, j, op[1]:op[2]],
                                E_CONST, None, ALU.mult)
                        elif op[0] == "zero":
                            ins = None if dry else e_.memset(pt[:, j, op[1]:op[2]], 0.0)
                        else:
                            col = op[1]
                            ins = None if dry else e_.tensor_mul(
                                pt[:, j, col:col + 128], pt[:, j, col:col + 128],
                                op[0][:, :])
                        if not dry:
                            C.fence(e_, ins, "fv")
                        npat += 1
                ins = None if dry else nc.vector.engine_nop()
                inc(ins, "dp", ("pat", ui))
                if not dry:
                    e_.wait_ge(C.sem("mm"), b["mm"] + tab[("summ", ui)])
                    C.fence(e_, e_.reciprocal(frow[0:1, 0:512], pR[0:1, 0:512]), "fv")
                    ins = nc.vector.engine_nop()
                else:
                    ins = None
                inc(ins, "x1", ("recip", ui))
                if not dry:
                    e_.wait_ge(C.sem("mm"), b["mm"] + tab[("fbbc", ui)])
                ins = None if dry else e_.tensor_copy(f_b[:, :], pR[:, 0:512])
                inc(ins, "x1", ("fbev", ui))
                if not dry:
                    e_.wait_ge(C.sem("mm"), b["mm"] + tab[("pvmm", ui)])
                    e_.wait_ge(C.sem("x1"), b["x1"] + tab[("fbev", ui)])
                    if ui >= 2:
                        e_.wait_ge(C.sem(f"s{ui % 2}"), b_s[ui % 2] + 16 * (ui // 2))
                ins = None if dry else e_.tensor_tensor(
                    osb[ui % 2][:, :], pO[:, 0:512], f_b[:, :], ALU.mult)
                inc(ins, "x1", ("oev", ui))
        return cnt

    def run_act(e_, dry):
        cnt, inc, wvf = emit(dry)
        for ui in range(nu):
            qb, h = units[ui]
            lo, hi = blk_kparams(qb)
            nkb = hi - lo
            for j in range(nkb):
                pb = j % 2
                if not dry:
                    e_.wait_ge(C.sem("mm"), b["mm"] + tab[("stmm", ui, j)])
                    if j == 0:
                        e_.wait_ge(C.sem("x1"), b["x1"] + tab[("mbk", ui)])
                        if ui >= 1:
                            e_.wait_ge(C.sem("mm"), b["mm"] + tab[("pvmm", ui - 1)])
                ins = None if dry else e_.activation(
                    pt[:, j, :], pST[:, pb * 512:(pb + 1) * 512], AF.Exp,
                    bias=mbk[:, 0:1], scale=1.0)
                inc(ins, "ev", ("exp", ui, j))
            if not dry:
                e_.wait_ge(C.sem("x1"), b["x1"] + tab[("oev", ui)])
                e_.dma_start(o_hbm[h * 128:(h + 1) * 128, qb * 512:(qb + 1) * 512],
                             osb[ui % 2][:, :]).then_inc(C.sem(f"s{ui % 2}"), 16)
        return cnt

    cpe = run_pe(None, True)
    cdv = run_dve(None, True)
    cac = run_act(None, True)

    with nc.Block() as blk:
        @blk.sync
        def _(e_):
            for qb in range(C.NB):
                if qb >= 2:
                    done = len([x for x in units if x[0] <= qb - 2])
                    e_.wait_ge(C.sem("mm"), b["mm"] + tab[("pvmm", done - 1)])
                e_.dma_start(qsb[qb % 2][:, :, :],
                             chunked(q_hbm[:, qb * 512:(qb + 1) * 512])).then_inc(
                                 C.sem(f"q{qb % 2}"), 16)
        @blk.tensor
        def _(e_):
            run_pe(e_, False)
        @blk.vector
        def _(e_):
            run_dve(e_, False)
        @blk.scalar
        def _(e_):
            run_act(e_, False)

    C.bump("q0", 16 * ceil_div(C.NB, 2))
    C.bump("q1", 16 * (C.NB // 2))
    C.bump("mm", cpe["mm"])
    C.bump("x1", cdv["x1"])
    C.bump("x2", cdv["x2"])
    C.bump("dp", cdv["dp"])
    C.bump("ev", cac["ev"])
    C.bump("s0", 16 * ceil_div(nu, 2))
    C.bump("s1", 16 * (nu // 2))
    e0, e1 = C.base("s0"), C.base("s1")
    with nc.Block() as blk:
        @blk.sync
        def _(e_):
            e_.wait_ge(C.sem("s0"), e0)
            e_.wait_ge(C.sem("s1"), e1)


@_scoped
def ph_attention4(C, q_hbm, k_hbm, v_hbm, o_hbm, *, n_heads=16, n_kv=16,
                  mask_kind="causal", window_chunks=8):
    nc, T = C.nc, C.T
    NT = C.NT
    u = C.uid()
    EK = n_kv * 128
    GR = n_heads // n_kv

    def qt_kparams(qt_abs):
        k_lo = max(0, qt_abs - window_chunks) if mask_kind == "window" else 0
        k_hi = NT if mask_kind == "trilbias" else qt_abs + 1
        return k_lo, k_hi

    def blk_kparams(qb):
        hi = NT if mask_kind == "trilbias" else 4 * (qb + 1)
        lo = max(0, 4 * qb - window_chunks) if mask_kind == "window" else 0
        return lo, hi

    punits = []
    head_cols = []
    col = 0
    for h in range(n_heads):
        c0 = col
        for qt_abs in range(NT):
            klo, khi = qt_kparams(qt_abs)
            for g in range(klo, khi, 4):
                punits.append((h, qt_abs, g, min(4, khi - g), col))
                col += 1
        head_cols.append((c0, col - c0))
    ncols = col
    punits.sort(key=lambda p_: (p_[1], p_[0], p_[2]))

    ksb = C.sb(f"a4_k{u}", [128, n_kv, T], BF16)
    vsb = C.sb(f"a4_v{u}", [128, NT, EK], BF16)
    qsb = [C.sb(f"a4_q{u}_{i}", [128, n_heads, 512], BF16) for i in range(2)]
    pt_p = [C.sb(f"a4_ptb{u}_{i}", [128, 4, 512], BF16) for i in range(2)]

    def pts(t):
        return pt_p[t % 2][:, (t // 2) % 4, :]

    osb = [C.sb(f"a4_o{u}_{i}", [128, 512], BF16) for i in range(4)]
    mst = C.sb(f"a4_ms{u}", [128, ncols], F32)
    mtmp = C.sb(f"a4_mt{u}", [128, n_heads], F32)
    mcol = C.sb(f"a4_mc{u}", [n_heads, 1], F32)
    mrow = C.sb(f"a4_mr{u}", [1, n_heads], F32)
    mbk = C.sb(f"a4_mb{u}", [128, n_heads], F32)
    frow = C.sb(f"a4_fr{u}", [1, 1024], F32)
    f_b = C.sb(f"a4_fb{u}", [128, 1024], F32)
    onesc = C.sb(f"a4_on{u}", [128, 1], BF16)
    onesf = C.sb(f"a4_of{u}", [1, 128], F32)
    identf = C.sb(f"a4_if{u}", [128, 128], F32)
    zdiag = C.sb(f"a4_zd{u}", [128, 128], BF16)
    zwin = C.sb(f"a4_zw{u}", [128, 128], BF16)
    ediag = C.sb(f"a4_ed{u}", [128, 128], F32)

    pS = C.psum(f"a4_ps{u}", [128, 2048], F32)
    pR = C.psum(f"a4_pr{u}", [128, 1024], F32)
    pO = C.psum(f"a4_po{u}", [128, 1024], F32)

    E_CONST = float(np.exp(np.float32(1.0)))
    idf_h = C.inline("c_identf", np.eye(128, dtype=np.float32))
    kq = np.arange(128)
    zdiag_h = C.inline("c_zdiag", (kq[:, None] <= kq[None, :]).astype(ml_dtypes.bfloat16))
    zwin_h = C.inline("c_zwin", (kq[:, None] >= kq[None, :]).astype(ml_dtypes.bfloat16))
    ediag_h = C.inline("c_ediagf", np.where(kq[:, None] <= kq[None, :],
                                            np.float32(E_CONST),
                                            np.float32(1.0)).astype(np.float32))

    ld0 = C.base("ld")
    with nc.Block() as blk:
        @blk.sync
        def _(e_):
            e_.dma_start(ksb[:, :, :], chunked(k_hbm[0:EK, :])).then_inc(C.sem("ld"), 16)
            e_.dma_start(vsb[:, :, :],
                         v_hbm[:, 0:EK].rearrange("(c p) t -> p c t", p=128)
                         ).then_inc(C.sem("ld"), 16)
            e_.dma_start(identf[:, :], idf_h[:, :]).then_inc(C.sem("ld"), 16)
            e_.dma_start(zdiag[:, :], zdiag_h[:, :]).then_inc(C.sem("ld"), 16)
            e_.dma_start(zwin[:, :], zwin_h[:, :]).then_inc(C.sem("ld"), 16)
            e_.dma_start(ediag[:, :], ediag_h[:, :]).then_inc(C.sem("ld"), 16)
            e_.dma_start(qsb[0][:, :, :], chunked(q_hbm[:, 0:512])).then_inc(
                C.sem("q0"), 16)
        @blk.gpsimd
        def _(e_):
            e_.memset(onesc[:, :], 1.0)
            e_.memset(onesf[:, :], 1.0)
        @blk.vector
        def _(e_):
            e_.wait_ge(C.sem("ld"), ld0 + 16 * 6)
        @blk.tensor
        def _(e_):
            e_.wait_ge(C.sem("ld"), ld0 + 16 * 6)
        @blk.scalar
        def _(e_):
            e_.wait_ge(C.sem("ld"), ld0 + 16 * 6)
    C.bump("ld", 16 * 6)
    C.bump("q0", 16)
    b_q = [C.base("q0") - 16, C.base("q1")]

    b_mm = C.base("mm")
    b_x2 = C.base("x2")
    with nc.Block() as blk:
        @blk.sync
        def _(e_):
            for qb in range(1, C.NB):
                if qb >= 2:
                    need = len([1 for p_ in punits if p_[1] // 4 <= qb - 2])
                    e_.wait_ge(C.sem("mm"), b_mm + need)
                e_.dma_start(qsb[qb % 2][:, :, :],
                             chunked(q_hbm[:, qb * 512:(qb + 1) * 512])).then_inc(
                                 C.sem(["q0", "q1"][qb % 2]), 16)
        @blk.tensor
        def _(e_):
            for pi, (h, qt_abs, g, kw, cl) in enumerate(punits):
                qb = qt_abs // 4
                kv = h // GR
                e_.wait_ge(C.sem(["q0", "q1"][qb % 2]), b_q[qb % 2] + 16 * (qb // 2 + 1))
                if pi >= 4:
                    e_.wait_ge(C.sem("x2"), b_x2 + pi - 3)
                bank = pi % 4
                e_.matmul(pS[:, bank * 512:bank * 512 + kw * 128],
                          qsb[qb % 2][:, h, (qt_abs % 4) * 128:(qt_abs % 4 + 1) * 128],
                          ksb[:, kv, g * 128:(g + kw) * 128],
                          start=True, stop=True).then_inc(C.sem("mm"), 1)
        @blk.vector
        def _(e_):
            for pi, (h, qt_abs, g, kw, cl) in enumerate(punits):
                bank = pi % 4
                e_.wait_ge(C.sem("mm"), b_mm + pi + 1)
                e_.reduce_max(mst[:, cl:cl + 1], pS[:, bank * 512:bank * 512 + kw * 128],
                              axis=AX.X, negate=True).then_inc(C.sem("x2"), 1)
    C.bump("mm", len(punits))
    C.bump("x2", len(punits))
    nql = [0, 0]
    for qb in range(1, C.NB):
        nql[qb % 2] += 16
    C.bump("q0", nql[0])
    C.bump("q1", nql[1])

    with nc.Block() as blk:
        @blk.vector
        def _(e_):
            for h in range(n_heads):
                c0, cn = head_cols[h]
                e_.tensor_reduce(mtmp[:, h:h + 1], mst[:, c0:c0 + cn], AX.X, ALU.min)
    with nc.Block() as blk:
        @blk.tensor
        def _(e_):
            e_.transpose(pR[0:n_heads, 0:128], mtmp[:, 0:n_heads], identf[:, :])
    with nc.Block() as blk:
        @blk.vector
        def _(e_):
            e_.tensor_reduce(mcol[0:n_heads, 0:1], pR[0:n_heads, 0:128], AX.X, ALU.min)
    with nc.Block() as blk:
        @blk.tensor
        def _(e_):
            e_.transpose(pR[0:1, 0:n_heads], mcol[0:n_heads, 0:1],
                         identf[0:n_heads, 0:n_heads])
    with nc.Block() as blk:
        @blk.vector
        def _(e_):
            e_.tensor_copy(mrow[0:1, 0:n_heads], pR[0:1, 0:n_heads])
    with nc.Block() as blk:
        @blk.tensor
        def _(e_):
            e_.matmul(pR[:, 0:n_heads], onesf[:, :], mrow[0:1, 0:n_heads],
                      start=True, stop=True)
    with nc.Block() as blk:
        @blk.vector
        def _(e_):
            e_.tensor_copy(mbk[:, :], pR[:, 0:n_heads])

    units = [(qb, h) for qb in range(C.NB) for h in range(n_heads)]
    nu = len(units)
    tiles = []
    first_t = {}
    last_t = {}
    for ui, (qb, h) in enumerate(units):
        lo, hi = blk_kparams(qb)
        first_t[ui] = len(tiles)
        for j in range(hi - lo):
            tiles.append((ui, qb, h, j, lo + j, hi - lo))
        last_t[ui] = len(tiles) - 1
    ntiles = len(tiles)
    NS = 8
    LAG = 3
    DELAY = 6

    def dve_patterns(qb, kc):
        ops = []
        if kc - qb * 4 in range(4):
            qt = kc - qb * 4
            ops.append((ediag if mask_kind == "trilbias" else zdiag, qt * 128))
        if mask_kind == "window" and (kc + window_chunks - qb * 4) in range(4):
            ops.append((zwin, (kc + window_chunks - qb * 4) * 128))
        if mask_kind == "trilbias":
            qt_below = [qt for qt in range(4) if kc < qb * 4 + qt]
            if qt_below:
                ops.append(("escale", min(qt_below) * 128, (max(qt_below) + 1) * 128))
        else:
            qt_above = [qt for qt in range(4) if kc > qb * 4 + qt]
            if qt_above:
                ops.append(("zero", min(qt_above) * 128, (max(qt_above) + 1) * 128))
            if mask_kind == "window":
                far = [qt for qt in range(4) if kc < qb * 4 + qt - window_chunks]
                if far:
                    ops.append(("zero", min(far) * 128, (max(far) + 1) * 128))
        return ops

    tab = {}

    def mk(dry):
        cnt = {"mm": 0, "x1": 0, "ev": 0, "dp": 0}

        def inc(ins, role, key=None):
            cnt[role] += 1
            if dry:
                if key is not None:
                    tab[key] = cnt[role]
            else:
                ins.then_inc(C.sem(role), 1)
            return cnt[role]
        return cnt, inc

    b2 = {r: C.base(r) for r in ("mm", "x1", "ev", "dp")}
    b_q2 = [C.base("q0"), C.base("q1")]
    SROLE = ["s0", "s1", "r0", "r1"]
    b_s = [C.base(r) for r in SROLE]

    def run_pe(e_, dry):
        cnt, inc = mk(dry)
        for step in range(ntiles + LAG + 1):
            if step < ntiles:
                ui, qb, h, j, kc, nkb = tiles[step]
                kv = h // GR
                if not dry:
                    if step == first_t[ui] and ui == qb * n_heads:
                        e_.wait_ge(C.sem(["q0", "q1"][qb % 2]),
                                   b_q2[qb % 2] + 16 * (qb // 2 + 1))
                    if step >= 4:
                        e_.wait_ge(C.sem("ev"), b2["ev"] + tab[("exp", step - 4)])
                ins = None if dry else e_.matmul(
                    pS[:, (step % 4) * 512:(step % 4 + 1) * 512],
                    ksb[:, kv, kc * 128:(kc + 1) * 128],
                    qsb[qb % 2][:, h, 0:512],
                    start=True, stop=True)
                inc(ins, "mm", ("stmm", step))
            tf = step - LAG - 1
            if 0 <= tf < ntiles and tf == last_t[tiles[tf][0]]:
                uf = tiles[tf][0]
                p = uf % 2
                if not dry:
                    e_.wait_ge(C.sem("x1"), b2["x1"] + tab[("recip", uf)])
                ins = None if dry else e_.matmul(
                    pR[:, p * 512:(p + 1) * 512], onesf[:, :],
                    frow[0:1, p * 512:p * 512 + 512], start=True, stop=True)
                inc(ins, "mm", ("fbbc", uf))
            t2 = step - LAG
            if 0 <= t2 < ntiles:
                ui, qb, h, j, kc, nkb = tiles[t2]
                kv = h // GR
                p = ui % 2
                if not dry:
                    e_.wait_ge(C.sem("ev"), b2["ev"] + tab[("exp", t2)])
                    e_.wait_ge(C.sem("dp"), b2["dp"] + tab[("pat", t2)])
                    if j == 0 and ui >= 2:
                        e_.wait_ge(C.sem("x1"), b2["x1"] + tab[("fbev", ui - 2)])
                        e_.wait_ge(C.sem("x1"), b2["x1"] + tab[("oev", ui - 2)])
                ins = None if dry else e_.matmul(
                    pR[0:1, p * 512:p * 512 + 512], onesc[:, :], pts(t2),
                    start=(j == 0), stop=(j == nkb - 1), skip_group_check=True)
                inc(ins, "mm", ("summa", t2))
                ins = None if dry else e_.matmul(
                    pO[:, p * 512:(p + 1) * 512],
                    vsb[:, kc, kv * 128:(kv + 1) * 128], pts(t2),
                    start=(j == 0), stop=(j == nkb - 1), skip_group_check=True)
                inc(ins, "mm", ("pvmm", t2))
        return cnt

    def run_act(e_, dry):
        cnt, inc = mk(dry)
        for step in range(ntiles + DELAY + 1):
            if step < ntiles:
                ui, qb, h, j, kc, nkb = tiles[step]
                if not dry:
                    e_.wait_ge(C.sem("mm"), b2["mm"] + tab[("stmm", step)])
                    if step >= NS:
                        e_.wait_ge(C.sem("mm"), b2["mm"] + tab[("pvmm", step - NS)])
                ins = None if dry else e_.activation(
                    pts(step), pS[:, (step % 4) * 512:(step % 4 + 1) * 512],
                    AF.Exp, bias=mbk[:, h:h + 1], scale=1.0)
                inc(ins, "ev", ("exp", step))
            t5 = step - DELAY
            if 0 <= t5 < ntiles and t5 == last_t[tiles[t5][0]]:
                u5 = tiles[t5][0]
                qb5, h5 = units[u5]
                if not dry:
                    e_.wait_ge(C.sem("x1"), b2["x1"] + tab[("oev", u5)])
                    e_.dma_start(
                        o_hbm[h5 * 128:(h5 + 1) * 128, qb5 * 512:(qb5 + 1) * 512],
                        osb[u5 % 4][:, :]).then_inc(C.sem(SROLE[u5 % 4]), 16)
        return cnt

    def run_dve(e_, dry):
        cnt, inc = mk(dry)
        for t in range(ntiles):
            ui, qb, h, j, kc, nkb = tiles[t]
            p = ui % 2
            ops = dve_patterns(qb, kc)
            if ops and not dry:
                e_.wait_ge(C.sem("ev"), b2["ev"] + tab[("exp", t)])
            for op in ops:
                if op[0] == "escale":
                    ins = None if dry else e_.tensor_scalar(
                        pts(t)[:, op[1]:op[2]], pts(t)[:, op[1]:op[2]],
                        E_CONST, None, ALU.mult)
                elif op[0] == "zero":
                    ins = None if dry else e_.memset(pts(t)[:, op[1]:op[2]], 0.0)
                else:
                    col = op[1]
                    ins = None if dry else e_.tensor_mul(
                        pts(t)[:, col:col + 128], pts(t)[:, col:col + 128],
                        op[0][:, :])
                inc(ins, "dp")
            if dry:
                tab[("pat", t)] = cnt["dp"]
            if t == last_t[ui]:
                if not dry:
                    e_.wait_ge(C.sem("mm"), b2["mm"] + tab[("summa", t)])
                    if ui >= 2:
                        e_.wait_ge(C.sem("mm"), b2["mm"] + tab[("fbbc", ui - 2)])
                ins = None if dry else e_.reciprocal(
                    frow[0:1, p * 512:p * 512 + 512], pR[0:1, p * 512:p * 512 + 512])
                inc(ins, "x1", ("recip", ui))
                if not dry:
                    e_.wait_ge(C.sem("mm"), b2["mm"] + tab[("fbbc", ui)])
                ins = None if dry else e_.tensor_copy(
                    f_b[:, p * 512:(p + 1) * 512], pR[:, p * 512:(p + 1) * 512])
                inc(ins, "x1", ("fbev", ui))
                if not dry:
                    e_.wait_ge(C.sem("mm"), b2["mm"] + tab[("pvmm", t)])
                    e_.wait_ge(C.sem("x1"), b2["x1"] + tab[("fbev", ui)])
                    if ui >= 4:
                        e_.wait_ge(C.sem(SROLE[ui % 4]),
                                   b_s[ui % 4] + 16 * (ui // 4))
                ins = None if dry else e_.tensor_tensor(
                    osb[ui % 4][:, :], pO[:, p * 512:(p + 1) * 512],
                    f_b[:, p * 512:(p + 1) * 512], ALU.mult)
                inc(ins, "x1", ("oev", ui))
        return cnt

    cpe = run_pe(None, True)
    cac = run_act(None, True)
    cdv = run_dve(None, True)

    with nc.Block() as blk:
        @blk.sync
        def _(e_):
            for qb in range(C.NB):
                if qb < 2:
                    e_.wait_ge(C.sem(["q0", "q1"][qb % 2]), b_q2[qb % 2])
                else:
                    done_u = (qb - 2 + 1) * n_heads - 1
                    e_.wait_ge(C.sem("mm"), b2["mm"] + tab[("pvmm", last_t[done_u])])
                e_.dma_start(qsb[qb % 2][:, :, :],
                             chunked(q_hbm[:, qb * 512:(qb + 1) * 512])).then_inc(
                                 C.sem(f"q{qb % 2}"), 16)
        @blk.tensor
        def _(e_):
            run_pe(e_, False)
        @blk.scalar
        def _(e_):
            run_act(e_, False)
        @blk.vector
        def _(e_):
            run_dve(e_, False)

    C.bump("q0", 16 * ceil_div(C.NB, 2))
    C.bump("q1", 16 * (C.NB // 2))
    C.bump("mm", cpe["mm"])
    C.bump("x1", cdv["x1"])
    C.bump("dp", cdv["dp"])
    C.bump("ev", cac["ev"])
    for sl in range(4):
        C.bump(SROLE[sl], 16 * len([1 for u_ in range(nu) if u_ % 4 == sl]))
    ends = [C.base(r) for r in SROLE]
    with nc.Block() as blk:
        @blk.sync
        def _(e_):
            for sl in range(4):
                e_.wait_ge(C.sem(SROLE[sl]), ends[sl])


# version 10
# speedup vs baseline: 1.1980x; 1.1980x over previous
import math
import numpy as np
import ml_dtypes
from contextlib import ExitStack

import concourse.bass as bass
import concourse.mybir as mybir
from concourse.bass_utils import run_bass_kernel_spmd

F32 = mybir.dt.float32
BF16 = mybir.dt.bfloat16
AF = mybir.ActivationFunctionType
ALU = mybir.AluOpType
AX = mybir.AxisListType

D = 2048
ND = D // 128
NEG = -1e30
INV_SQRT_DK = 1.0 / math.sqrt(128.0)


class Ctx:
    def __init__(self, nc, es, T):
        self.nc = nc
        self.es = es
        self.scopes = [es]
        self.T = T
        self.NT = T // 128
        self.NB = T // 512
        self._sems = {}
        self._cnt = {}
        self._uid = 0
        self.const = {}
        self._cc_tile = es.enter_context(nc.sbuf_tensor("constcols", [128, 16], F32))
        self._cc_n = 0

    def push_scope(self):
        st = ExitStack()
        self.scopes.append(st)
        return st

    def pop_scope(self):
        self.scopes.pop().close()

    def uid(self):
        self._uid += 1
        return self._uid

    def sem(self, role):
        if role not in self._sems:
            self._sems[role] = self.es.enter_context(self.nc.semaphore(role))
            self._cnt[role] = 0
        return self._sems[role]

    def base(self, role):
        self.sem(role)
        return self._cnt[role]

    def bump(self, role, k):
        self.sem(role)
        self._cnt[role] += k
        return self._cnt[role]

    def fence(self, e_, ins, role):
        n = self.bump(role, 1)
        ins.then_inc(self.sem(role), 1)
        e_.wait_ge(self.sem(role), n)

    def sb(self, name, shape, dtype):
        return self.scopes[-1].enter_context(self.nc.sbuf_tensor(name, shape, dtype))

    def psum(self, name, shape, dtype=F32):
        return self.scopes[-1].enter_context(self.nc.psum_tensor(name, shape, dtype))

    def inline(self, name, arr):
        if name not in self.const:
            self.const[name] = self.nc.inline_tensor(np.ascontiguousarray(arr), name)
        return self.const[name]

    def constcol(self, value):
        key = ("cc", float(value))
        if key not in self.const:
            idx = self._cc_n
            self._cc_n += 1
            assert idx < 16
            with self.nc.Block() as blk:
                @blk.gpsimd
                def _(e_):
                    e_.memset(self._cc_tile[:, idx:idx + 1], float(value))
            self.const[key] = self._cc_tile[:, idx:idx + 1]
        return self.const[key]


def chunked(hbm_2d, p=128):
    return hbm_2d.rearrange("(c p) t -> p c t", p=p)


def ceil_div(a, b):
    return (a + b - 1) // b


def _scoped(fn):
    def wrap(C, *a, **kw):
        C.push_scope()
        try:
            return fn(C, *a, **kw)
        finally:
            C.pop_scope()
    return wrap


@_scoped
def ph_layernorm(C, x_hbm, ln_sbuf, post_tb_hook=None):
    nc, T = C.nc, C.T
    u = C.uid()
    xs = [C.sb(f"ln_xs{u}_0", [128, ND, 512], F32)]
    xcb = C.sb(f"ln_xcb{u}", [128, ND, 512], BF16)
    xsq = C.sb(f"ln_xsq{u}", [128, ND, 512], BF16)
    ones = C.sb(f"ln_one{u}", [128, 1], BF16)
    onesf = C.sb(f"ln_onef{u}", [1, 128], F32)
    rows = C.sb(f"ln_rows{u}", [1, 4, 512], F32)
    nmrow = C.sb(f"ln_nm{u}", [1, 512], F32)
    rb = C.sb(f"ln_rb{u}", [128, 512], F32)
    nmb = C.sb(f"ln_nmb{u}", [128, 512], F32)
    tmp = C.sb(f"ln_tmp{u}", [128, 512], F32)
    pstat = C.psum(f"ln_ps{u}", [128, 1024], F32)
    pbc = C.psum(f"ln_pb{u}", [128, 1024], F32)

    eps_ap = C.constcol(1e-5)
    with nc.Block() as blk:
        @blk.gpsimd
        def _(e_):
            e_.memset(ones[:, :], 1.0)
            e_.memset(onesf[:, :], 1.0)

    for tb in range(C.NB):
        buf = xs[0]
        l0b = C.base("l0")
        with nc.Block() as blk:
            @blk.sync
            def _(e_):
                e_.dma_start(buf[:, :, :],
                             chunked(x_hbm[:, tb * 512:(tb + 1) * 512]),
                             ).then_inc(C.sem("l0"), 16)
            @blk.vector
            def _(e_):
                e_.wait_ge(C.sem("l0"), l0b + 16)
                e_.tensor_copy(xcb[:, :, :], buf[:, :, :])
            @blk.scalar
            def _(e_):
                e_.wait_ge(C.sem("l0"), l0b + 16)
                e_.activation(xsq[:, :, :], buf[:, :, :], AF.Square, bias=0.0, scale=1.0)
        C.bump("l0", 16)
        b_mm = C.base("mm")
        b_x1 = C.base("x1")
        b_ev = C.base("ev")
        with nc.Block() as blk:
            @blk.tensor
            def _(e_):
                for c in range(ND):
                    ins = e_.matmul(pstat[0:1, 0:512], ones[:, :], xcb[:, c, :],
                                    start=(c == 0), stop=(c == ND - 1))
                ins.then_inc(C.sem("mm"), 1)
                for c in range(ND):
                    ins = e_.matmul(pstat[0:1, 512:1024], ones[:, :], xsq[:, c, :],
                                    start=(c == 0), stop=(c == ND - 1))
                ins.then_inc(C.sem("mm"), 1)
                e_.wait_ge(C.sem("x1"), b_x1 + 2)
                e_.matmul(pbc[:, 0:512], onesf[:, :], rows[:, 3, :], start=True,
                          stop=True).then_inc(C.sem("mm"), 1)
                e_.matmul(pbc[:, 512:1024], onesf[:, :], nmrow[:, :], start=True,
                          stop=True).then_inc(C.sem("mm"), 1)
            @blk.scalar
            def _(e_):
                e_.wait_ge(C.sem("x1"), b_x1 + 1)
                e_.activation(rows[:, 1, :], rows[:, 2, :], AF.Sqrt,
                              bias=eps_ap[0:1, :], scale=1.0).then_inc(C.sem("ev"), 1)
            @blk.vector
            def _(e_):
                e_.wait_ge(C.sem("mm"), b_mm + 2)
                C.fence(e_, e_.tensor_scalar(rows[:, 0, :], pstat[0:1, 0:512],
                                             1.0 / D, None, ALU.mult), "fv")
                C.fence(e_, e_.tensor_scalar(rows[:, 1, :], pstat[0:1, 512:1024],
                                             1.0 / D, None, ALU.mult), "fv")
                C.fence(e_, e_.tensor_mul(rows[:, 2, :], rows[:, 0, :],
                                          rows[:, 0, :]), "fv")
                e_.tensor_sub(rows[:, 2, :], rows[:, 1, :], rows[:, 2, :]).then_inc(
                    C.sem("x1"), 1)
                e_.wait_ge(C.sem("ev"), b_ev + 1)
                C.fence(e_, e_.reciprocal(rows[:, 3, :], rows[:, 1, :]), "fv")
                e_.tensor_mul(nmrow[:, :], rows[:, 0, :], rows[:, 3, :]).then_inc(
                    C.sem("x1"), 1)
                e_.wait_ge(C.sem("mm"), b_mm + 4)
                C.fence(e_, e_.tensor_copy(rb[:, :], pbc[:, 0:512]), "fv")
                C.fence(e_, e_.tensor_copy(nmb[:, :], pbc[:, 512:1024]), "fv")
                for c in range(ND):
                    C.fence(e_, e_.tensor_mul(tmp[:, :], buf[:, c, :], rb[:, :]), "fv")
                    C.fence(e_, e_.tensor_sub(
                        ln_sbuf[:, c, tb * 512:(tb + 1) * 512], tmp[:, :], nmb[:, :]), "fv")
        C.bump("mm", 4)
        C.bump("x1", 2)
        C.bump("ev", 1)
        if post_tb_hook is not None:
            post_tb_hook(tb)


@_scoped
def ph_proj(C, src, w_hbm, out_hbm, *, bias_hbm=None, func=AF.Identity,
            scale=1.0, bias_scale=None, w_hook=None, src_in_hbm=False,
            out_row0=0, bias_row=None):
    nc, T = C.nc, C.T
    Din = w_hbm.shape[0]
    NDin = Din // 128
    E = w_hbm.shape[1]
    NE = E // 128
    u = C.uid()

    if src_in_hbm:
        ln = C.sb(f"pj_in{u}", [128, NDin, T], BF16)
        ld0 = C.base("ld")
        with nc.Block() as blk:
            @blk.sync
            def _(e_):
                e_.dma_start(ln[:, :, :], chunked(src)).then_inc(C.sem("ld"), 16)
            @blk.vector
            def _(e_):
                e_.wait_ge(C.sem("ld"), ld0 + 16)
        C.bump("ld", 16)
    else:
        ln = src

    wstage = [C.sb(f"pj_ws{u}_{i}", [128, NDin, 128], F32) for i in range(2)]
    wb = [C.sb(f"pj_wb{u}_{i}", [128, NDin, 128], BF16) for i in range(2)]
    ob = [C.sb(f"pj_ob{u}_{i}", [128, 512], BF16) for i in range(2)]
    bcols = C.sb(f"pj_bc{u}", [128, max(NE, 1)], F32)
    ps = C.psum(f"pj_ps{u}", [128, 1024], F32)

    if bias_hbm is not None:
        ld0 = C.base("ld")
        with nc.Block() as blk:
            @blk.sync
            def _(e_):
                e_.dma_start(bcols[:, 0:NE], bias_hbm.rearrange("(c p) -> p c", p=128),
                             allow_slow_non_contiguous=True).then_inc(C.sem("ld"), 16)
            @blk.vector
            def _(e_):
                e_.wait_ge(C.sem("ld"), ld0 + 16)
        C.bump("ld", 16)
        if bias_scale is not None:
            with nc.Block() as blk:
                @blk.vector
                def _(e_):
                    e_.tensor_scalar(bcols[:, 0:NE], bcols[:, 0:NE], bias_scale, None,
                                     ALU.mult)

    b_ld = C.base("ld")
    b_ca = C.base("ca")
    b_mm = C.base("mm")
    b_ev = C.base("ev")
    b_s = [C.base("s0"), C.base("s1")]
    nunits = NE * C.NB
    with nc.Block() as blk:
        @blk.sync
        def _(e_):
            for ec in range(NE):
                b = ec % 2
                if ec >= 1:
                    e_.wait_ge(C.sem("ld"), b_ld + 16 * ec)
                if ec >= 2:
                    e_.wait_ge(C.sem("ca"), b_ca + ec - 1)
                e_.dma_start(
                    wstage[b][:, :, :],
                    w_hbm[:, ec * 128:(ec + 1) * 128].rearrange("(c p) e -> p c e", p=128),
                ).then_inc(C.sem("ld"), 16)
        @blk.vector
        def _(e_):
            for ec in range(NE):
                b = ec % 2
                e_.wait_ge(C.sem("ld"), b_ld + 16 * (ec + 1))
                if ec >= 2:
                    e_.wait_ge(C.sem("mm"), b_mm + (ec - 1) * C.NB)
                ins = e_.tensor_copy(wb[b][:, :, :], wstage[b][:, :, :])
                if w_hook is not None:
                    C.fence(e_, ins, "fv")
                    ins = w_hook(e_, wb[b], ec)
                ins.then_inc(C.sem("ca"), 1)
        @blk.tensor
        def _(e_):
            for ec in range(NE):
                b = ec % 2
                e_.wait_ge(C.sem("ca"), b_ca + ec + 1)
                for tb in range(C.NB):
                    i = ec * C.NB + tb
                    pb = i % 2
                    if i >= 2:
                        e_.wait_ge(C.sem("ev"), b_ev + i - 1)
                    last = NDin - 1 if bias_row is None else -1
                    for c in range(NDin):
                        ins = e_.matmul(ps[:, pb * 512:(pb + 1) * 512],
                                        wb[b][:, c, :], ln[:, c, tb * 512:(tb + 1) * 512],
                                        start=(c == 0), stop=(c == last))
                    if bias_row is not None:
                        brs, onr = bias_row
                        ins = e_.matmul(ps[:, pb * 512:(pb + 1) * 512],
                                        brs[0:1, ec * 128:(ec + 1) * 128],
                                        onr[0:1, 0:512], start=False, stop=True)
                    ins.then_inc(C.sem("mm"), 1)
        @blk.scalar
        def _(e_):
            for ec in range(NE):
                for tb in range(C.NB):
                    i = ec * C.NB + tb
                    pb = i % 2
                    e_.wait_ge(C.sem("mm"), b_mm + i + 1)
                    if i >= 2:
                        e_.wait_ge(C.sem(f"s{pb}"), b_s[pb] + 16 * (i // 2))
                    bias_ap = bcols[:, ec:ec + 1] if bias_hbm is not None else 0.0
                    ins = e_.activation(ob[pb][:, :], ps[:, pb * 512:(pb + 1) * 512],
                                        func, bias=bias_ap, scale=scale)
                    ins.then_inc(C.sem("ev"), 1)
                    e_.wait_ge(C.sem("ev"), b_ev + i + 1)
                    e_.dma_start(
                        out_hbm[out_row0 + ec * 128:out_row0 + (ec + 1) * 128,
                                tb * 512:(tb + 1) * 512],
                        ob[pb][:, :]).then_inc(C.sem(f"s{pb}"), 16)
    C.bump("ld", 16 * NE)
    C.bump("ca", NE)
    C.bump("mm", nunits)
    C.bump("ev", nunits)
    C.bump("s0", 16 * ceil_div(nunits, 2))
    C.bump("s1", 16 * (nunits // 2))
    e0, e1 = C.base("s0"), C.base("s1")
    with nc.Block() as blk:
        @blk.sync
        def _(e_):
            e_.wait_ge(C.sem("s0"), e0)
            e_.wait_ge(C.sem("s1"), e1)


@_scoped
def ph_proj_tm(C, ln_sbuf, w_hbm, out_hbm, *, bias_hbm=None):
    nc, T = C.nc, C.T
    Din = w_hbm.shape[0]
    NDin = Din // 128
    E = w_hbm.shape[1]
    NEB = ceil_div(E, 512)
    u = C.uid()
    wstage = [C.sb(f"pt_ws{u}_{i}", [128, NDin, 512], F32) for i in range(2)]
    wb = [C.sb(f"pt_wb{u}_{i}", [128, NDin, 512], BF16) for i in range(2)]
    ob = [C.sb(f"pt_ob{u}_{i}", [128, 512], BF16) for i in range(2)]
    brow_f = C.sb(f"pt_bf{u}", [1, E], F32)
    brow = C.sb(f"pt_br{u}", [1, E], BF16)
    onest = C.sb(f"pt_on{u}", [1, 128], BF16)
    ps = C.psum(f"pt_ps{u}", [128, 1024], F32)

    ld0 = C.base("ld")
    with nc.Block() as blk:
        @blk.gpsimd
        def _(e_):
            e_.memset(onest[:, :], 1.0)
            if bias_hbm is None:
                e_.memset(brow[:, :], 0.0)
        if bias_hbm is not None:
            @blk.sync
            def _(e_):
                e_.dma_start(brow_f[0:1, :],
                             bias_hbm.rearrange("(o e) -> o e", o=1)).then_inc(
                    C.sem("ld"), 16)
            @blk.vector
            def _(e_):
                e_.wait_ge(C.sem("ld"), ld0 + 16)
    if bias_hbm is not None:
        C.bump("ld", 16)
        with nc.Block() as blk:
            @blk.vector
            def _(e_):
                e_.tensor_copy(brow[0:1, :], brow_f[0:1, :])

    b_ld = C.base("ld")
    b_ca = C.base("ca")
    b_mm = C.base("mm")
    b_ev = C.base("ev")
    b_s = [C.base("s0"), C.base("s1")]
    with nc.Block() as blk:
        @blk.sync
        def _(e_):
            for eb in range(NEB):
                ew = min(512, E - eb * 512)
                b = eb % 2
                if eb >= 1:
                    e_.wait_ge(C.sem("ld"), b_ld + 16 * eb)
                if eb >= 2:
                    e_.wait_ge(C.sem("ca"), b_ca + eb - 1)
                e_.dma_start(
                    wstage[b][:, :, 0:ew],
                    w_hbm[:, eb * 512:eb * 512 + ew].rearrange("(c p) e -> p c e", p=128),
                ).then_inc(C.sem("ld"), 16)
        @blk.vector
        def _(e_):
            for eb in range(NEB):
                ew = min(512, E - eb * 512)
                b = eb % 2
                e_.wait_ge(C.sem("ld"), b_ld + 16 * (eb + 1))
                if eb >= 2:
                    e_.wait_ge(C.sem("mm"), b_mm + (eb - 1) * C.NT)
                e_.tensor_copy(wb[b][:, :, 0:ew], wstage[b][:, :, 0:ew]).then_inc(
                    C.sem("ca"), 1)
        @blk.tensor
        def _(e_):
            for eb in range(NEB):
                ew = min(512, E - eb * 512)
                b = eb % 2
                e_.wait_ge(C.sem("ca"), b_ca + eb + 1)
                for tt in range(C.NT):
                    i = eb * C.NT + tt
                    pb = i % 2
                    if i >= 2:
                        e_.wait_ge(C.sem("ev"), b_ev + i - 1)
                    for c in range(NDin):
                        e_.matmul(ps[:, pb * 512:pb * 512 + ew],
                                  ln_sbuf[:, c, tt * 128:(tt + 1) * 128],
                                  wb[b][:, c, 0:ew],
                                  start=(c == 0), stop=False)
                    e_.matmul(ps[:, pb * 512:pb * 512 + ew], onest[:, :],
                              brow[0:1, eb * 512:eb * 512 + ew],
                              start=False, stop=True).then_inc(C.sem("mm"), 1)
        @blk.scalar
        def _(e_):
            for eb in range(NEB):
                ew = min(512, E - eb * 512)
                for tt in range(C.NT):
                    i = eb * C.NT + tt
                    pb = i % 2
                    e_.wait_ge(C.sem("mm"), b_mm + i + 1)
                    if i >= 2:
                        e_.wait_ge(C.sem(f"s{pb}"), b_s[pb] + 16 * (i // 2))
                    ins = e_.activation(ob[pb][:, 0:ew], ps[:, pb * 512:pb * 512 + ew],
                                        AF.Identity, bias=0.0, scale=1.0)
                    ins.then_inc(C.sem("ev"), 1)
                    e_.wait_ge(C.sem("ev"), b_ev + i + 1)
                    e_.dma_start(out_hbm[tt * 128:(tt + 1) * 128, eb * 512:eb * 512 + ew],
                                 ob[pb][:, 0:ew]).then_inc(C.sem(f"s{pb}"), 16)
    n = NEB * C.NT
    C.bump("ld", 16 * NEB)
    C.bump("ca", NEB)
    C.bump("mm", n)
    C.bump("ev", n)
    C.bump("s0", 16 * ceil_div(n, 2))
    C.bump("s1", 16 * (n // 2))
    e0, e1 = C.base("s0"), C.base("s1")
    with nc.Block() as blk:
        @blk.sync
        def _(e_):
            e_.wait_ge(C.sem("s0"), e0)
            e_.wait_ge(C.sem("s1"), e1)


@_scoped
def ph_out_proj_add(C, src_hbm, w_hbm, bias_hbm, r_src, r_dst):
    nc, T = C.nc, C.T
    Din = w_hbm.shape[0]
    NDin = Din // 128
    E = w_hbm.shape[1]
    NE = E // 128
    u = C.uid()
    ln = C.sb(f"op_in{u}", [128, NDin, T], BF16)
    wstage = [C.sb(f"op_ws{u}_{i}", [128, NDin, 128], F32) for i in range(2)]
    wb = [C.sb(f"op_wb{u}_{i}", [128, NDin, 128], BF16) for i in range(2)]
    yb = [C.sb(f"op_yb{u}_{i}", [128, 512], F32) for i in range(2)]
    rtile = [C.sb(f"op_rt{u}_{i}", [128, 512], F32) for i in range(2)]
    ob = [C.sb(f"op_ob{u}_{i}", [128, 512], F32) for i in range(2)]
    bcols = C.sb(f"op_bc{u}", [128, max(NE, 1)], F32)
    ps = C.psum(f"op_ps{u}", [128, 1024], F32)

    ld0 = C.base("ld")
    nld = 1 + (1 if bias_hbm is not None else 0)
    with nc.Block() as blk:
        @blk.sync
        def _(e_):
            e_.dma_start(ln[:, :, :], chunked(src_hbm)).then_inc(C.sem("ld"), 16)
            if bias_hbm is not None:
                e_.dma_start(bcols[:, 0:NE], bias_hbm.rearrange("(c p) -> p c", p=128),
                             allow_slow_non_contiguous=True).then_inc(C.sem("ld"), 16)
        @blk.vector
        def _(e_):
            e_.wait_ge(C.sem("ld"), ld0 + 16 * nld)
    C.bump("ld", 16 * nld)

    b_ld = C.base("ld")
    b_ca = C.base("ca")
    b_mm = C.base("mm")
    b_ev = C.base("ev")
    b_x2 = C.base("x2")
    b_r = [C.base("r0"), C.base("r1")]
    b_s = [C.base("s0"), C.base("s1")]
    nunits = NE * C.NB
    with nc.Block() as blk:
        @blk.sync
        def _(e_):
            for ec in range(NE):
                b = ec % 2
                if ec >= 1:
                    e_.wait_ge(C.sem("ld"), b_ld + 16 * ec)
                if ec >= 2:
                    e_.wait_ge(C.sem("ca"), b_ca + ec - 1)
                e_.dma_start(
                    wstage[b][:, :, :],
                    w_hbm[:, ec * 128:(ec + 1) * 128].rearrange("(c p) e -> p c e", p=128),
                ).then_inc(C.sem("ld"), 16)
                for tb in range(C.NB):
                    i = ec * C.NB + tb
                    if i >= 2:
                        e_.wait_ge(C.sem("x2"), b_x2 + i - 1)
                    e_.dma_start(rtile[i % 2][:, :],
                                 r_src[ec * 128:(ec + 1) * 128, tb * 512:(tb + 1) * 512]
                                 ).then_inc(C.sem(f"r{i % 2}"), 16)
        @blk.vector
        def _(e_):
            for ec in range(NE + 1):
                if ec >= 1:
                    for tb in range(C.NB):
                        i = (ec - 1) * C.NB + tb
                        pb = i % 2
                        e_.wait_ge(C.sem("mm"), b_mm + i + 1)
                        e_.wait_ge(C.sem(f"r{pb}"), b_r[pb] + 16 * (i // 2 + 1))
                        if i >= 2:
                            e_.wait_ge(C.sem(f"s{pb}"), b_s[pb] + 16 * (i // 2))
                        bias_ap = (bcols[:, ec - 1:ec] if bias_hbm is not None else 0.0)
                        e_.scalar_tensor_tensor(
                            ob[pb][:, :], ps[:, pb * 512:(pb + 1) * 512], bias_ap,
                            rtile[pb][:, :], ALU.add, ALU.add).then_inc(C.sem("x2"), 1)
                        e_.wait_ge(C.sem("x2"), b_x2 + i + 1)
                if ec < NE:
                    b = ec % 2
                    e_.wait_ge(C.sem("ld"), b_ld + 16 * (ec + 1))
                    if ec >= 2:
                        e_.wait_ge(C.sem("mm"), b_mm + (ec - 1) * C.NB)
                    e_.tensor_copy(wb[b][:, :, :], wstage[b][:, :, :]).then_inc(
                        C.sem("ca"), 1)
        @blk.tensor
        def _(e_):
            for ec in range(NE):
                b = ec % 2
                e_.wait_ge(C.sem("ca"), b_ca + ec + 1)
                for tb in range(C.NB):
                    i = ec * C.NB + tb
                    pb = i % 2
                    if i >= 2:
                        e_.wait_ge(C.sem("x2"), b_x2 + i - 1)
                    for c in range(NDin):
                        ins = e_.matmul(ps[:, pb * 512:(pb + 1) * 512],
                                        wb[b][:, c, :], ln[:, c, tb * 512:(tb + 1) * 512],
                                        start=(c == 0), stop=(c == NDin - 1))
                    ins.then_inc(C.sem("mm"), 1)
        @blk.scalar
        def _(e_):
            for ec in range(NE):
                for tb in range(C.NB):
                    i = ec * C.NB + tb
                    pb = i % 2
                    e_.wait_ge(C.sem("x2"), b_x2 + i + 1)
                    e_.dma_start(r_dst[ec * 128:(ec + 1) * 128, tb * 512:(tb + 1) * 512],
                                 ob[pb][:, :]).then_inc(C.sem(f"s{pb}"), 16)
    C.bump("ld", 16 * NE)
    C.bump("ca", NE)
    C.bump("mm", nunits)
    C.bump("x2", nunits)
    C.bump("r0", 16 * ceil_div(nunits, 2))
    C.bump("r1", 16 * (nunits // 2))
    C.bump("s0", 16 * ceil_div(nunits, 2))
    C.bump("s1", 16 * (nunits // 2))
    e0, e1 = C.base("s0"), C.base("s1")
    with nc.Block() as blk:
        @blk.sync
        def _(e_):
            e_.wait_ge(C.sem("s0"), e0)
            e_.wait_ge(C.sem("s1"), e1)


@_scoped
def ph_attention(C, q_hbm, k_hbm, v_hbm, o_hbm, *, n_heads=16, n_kv=16,
                 mask_kind="causal", window_chunks=8):
    nc, T = C.nc, C.T
    NT = C.NT
    u = C.uid()
    EK = n_kv * 128

    ksb = C.sb(f"at_k{u}", [128, n_kv, T], BF16)
    vsb = C.sb(f"at_v{u}", [128, NT, EK], BF16)
    qsb = [C.sb(f"at_q{u}_{i}", [128, n_heads, 512], BF16) for i in range(2)]
    p2 = C.sb(f"at_p2{u}", [128, T], BF16)
    pt = C.sb(f"at_ptb{u}", [128, NT, 128], BF16)
    osb = [C.sb(f"at_o{u}_{i}", [128, 128], BF16) for i in range(2)]
    stat = C.sb(f"at_st{u}", [128, 4], F32)
    ident = C.sb(f"at_id{u}", [128, 128], BF16)
    mneg = C.sb(f"at_mn{u}", [128, 128], F32)
    mwin = C.sb(f"at_mw{u}", [128, 128], F32)
    mbias = C.sb(f"at_mb{u}", [128, 128], F32)

    pS = C.psum(f"at_ps{u}", [128, 2048], F32)
    pT = C.psum(f"at_pt{u}", [128, 2048], BF16)
    pO = C.psum(f"at_po{u}", [128, 1024], F32)

    id_h = C.inline("c_ident", np.eye(128, dtype=ml_dtypes.bfloat16))
    mneg_h = C.inline("c_mneg", np.triu(np.full((128, 128), NEG, dtype=np.float32), 1))
    mwin_h = C.inline("c_mwin", np.tril(np.full((128, 128), NEG, dtype=np.float32), -1))
    mbias_h = C.inline("c_mbias", np.tril(np.ones((128, 128), dtype=np.float32)))

    ld0 = C.base("ld")
    with nc.Block() as blk:
        @blk.sync
        def _(e_):
            e_.dma_start(ksb[:, :, :], chunked(k_hbm[0:EK, :])).then_inc(C.sem("ld"), 16)
            e_.dma_start(vsb[:, :, :],
                         v_hbm[:, 0:EK].rearrange("(c p) t -> p c t", p=128)
                         ).then_inc(C.sem("ld"), 16)
            e_.dma_start(ident[:, :], id_h[:, :]).then_inc(C.sem("ld"), 16)
            e_.dma_start(mneg[:, :], mneg_h[:, :]).then_inc(C.sem("ld"), 16)
            e_.dma_start(mwin[:, :], mwin_h[:, :]).then_inc(C.sem("ld"), 16)
            e_.dma_start(mbias[:, :], mbias_h[:, :]).then_inc(C.sem("ld"), 16)
        @blk.vector
        def _(e_):
            e_.wait_ge(C.sem("ld"), ld0 + 16 * 6)
        @blk.tensor
        def _(e_):
            e_.wait_ge(C.sem("ld"), ld0 + 16 * 6)
    C.bump("ld", 16 * 6)

    b_mm = C.base("mm")
    b_ev = C.base("ev")
    b_x1 = C.base("x1")
    b_q = [C.base("q0"), C.base("q1")]
    b_s = [C.base("s0"), C.base("s1")]

    units = []
    for qb in range(C.NB):
        for h in range(n_heads):
            for qt in range(4):
                units.append((qb, h, qb * 4 + qt))
    nu = len(units)

    def kparams(qt_abs):
        k_lo = max(0, qt_abs - window_chunks) if mask_kind == "window" else 0
        k_hi = NT if mask_kind == "trilbias" else qt_abs + 1
        return k_lo, k_hi

    with nc.Block() as blk:
        @blk.sync
        def _(e_):
            for qb in range(C.NB):
                if qb >= 2:
                    done_units = len([x for x in units if x[0] <= qb - 2])
                    e_.wait_ge(C.sem("mm"), b_mm + 3 * done_units)
                e_.dma_start(qsb[qb % 2][:, :, :],
                             chunked(q_hbm[:, qb * 512:(qb + 1) * 512])).then_inc(
                                 C.sem(f"q{qb % 2}"), 16)

        @blk.tensor
        def _(e_):
            for ui, (qb, h, qt_abs) in enumerate(units):
                k_lo, k_hi = kparams(qt_abs)
                nk = k_hi - k_lo
                kv = h // (n_heads // n_kv)
                qcol = (qt_abs - qb * 4) * 128
                e_.wait_ge(C.sem(f"q{qb % 2}"), b_q[qb % 2] + 16 * (qb // 2 + 1))
                if ui >= 1:
                    e_.wait_ge(C.sem("ev"), b_ev + 3 * (ui - 1) + 1)
                ins = None
                for kc in range(k_lo, k_hi, 4):
                    kw = min(4, k_hi - kc)
                    ins = e_.matmul(pS[:, (kc - k_lo) * 128:(kc - k_lo) * 128 + kw * 128],
                                    qsb[qb % 2][:, h, qcol:qcol + 128],
                                    ksb[:, kv, kc * 128:(kc + kw) * 128],
                                    start=True, stop=True)
                ins.then_inc(C.sem("mm"), 1)
                e_.wait_ge(C.sem("x1"), b_x1 + 3 * ui + 3)
                for j in range(nk):
                    ins = e_.transpose(pT[:, j * 128:(j + 1) * 128],
                                       p2[:, j * 128:(j + 1) * 128], ident[:, :])
                ins.then_inc(C.sem("mm"), 1)
                e_.wait_ge(C.sem("ev"), b_ev + 3 * ui + 2)
                if ui >= 2:
                    e_.wait_ge(C.sem("ev"), b_ev + 3 * (ui - 2) + 3)
                for j in range(nk):
                    ins = e_.matmul(pO[:, (ui % 2) * 512:(ui % 2) * 512 + 128],
                                    vsb[:, k_lo + j, kv * 128:(kv + 1) * 128],
                                    pt[:, j, :],
                                    start=(j == 0), stop=(j == nk - 1))
                ins.then_inc(C.sem("mm"), 1)

        @blk.vector
        def _(e_):
            for ui, (qb, h, qt_abs) in enumerate(units):
                k_lo, k_hi = kparams(qt_abs)
                nk = k_hi - k_lo
                e_.wait_ge(C.sem("mm"), b_mm + 3 * ui + 1)
                if mask_kind in ("causal", "window"):
                    dcol = (qt_abs - k_lo) * 128
                    C.fence(e_, e_.tensor_add(pS[:, dcol:dcol + 128],
                                              pS[:, dcol:dcol + 128], mneg[:, :]), "fv")
                    if mask_kind == "window" and qt_abs >= window_chunks:
                        C.fence(e_, e_.tensor_add(pS[:, 0:128], pS[:, 0:128],
                                                  mwin[:, :]), "fv")
                else:
                    if qt_abs > 0:
                        C.fence(e_, e_.tensor_scalar(pS[:, 0:qt_abs * 128],
                                                     pS[:, 0:qt_abs * 128],
                                                     1.0, None, ALU.add), "fv")
                    C.fence(e_, e_.tensor_add(pS[:, qt_abs * 128:(qt_abs + 1) * 128],
                                              pS[:, qt_abs * 128:(qt_abs + 1) * 128],
                                              mbias[:, :]), "fv")
                e_.reduce_max(stat[:, 0:1], pS[:, 0:nk * 128], axis=AX.X,
                              negate=True).then_inc(C.sem("x1"), 1)
                e_.wait_ge(C.sem("ev"), b_ev + 3 * ui + 1)
                e_.reciprocal(stat[:, 2:3], stat[:, 1:2]).then_inc(C.sem("x1"), 1)
                e_.wait_ge(C.sem("x1"), b_x1 + 3 * ui + 2)
                e_.tensor_scalar(p2[:, 0:nk * 128], p2[:, 0:nk * 128], stat[:, 2:3],
                                 None, ALU.mult).then_inc(C.sem("x1"), 1)

        @blk.scalar
        def _(e_):
            for ui, (qb, h, qt_abs) in enumerate(units):
                k_lo, k_hi = kparams(qt_abs)
                nk = k_hi - k_lo
                e_.wait_ge(C.sem("x1"), b_x1 + 3 * ui + 1)
                if ui >= 1:
                    e_.wait_ge(C.sem("mm"), b_mm + 3 * (ui - 1) + 2)
                e_.activation(p2[:, 0:nk * 128], pS[:, 0:nk * 128], AF.Exp,
                              bias=stat[:, 0:1], scale=1.0,
                              accum_out=stat[:, 1:2]).then_inc(C.sem("ev"), 1)
                e_.wait_ge(C.sem("mm"), b_mm + 3 * ui + 2)
                ins = None
                for j in range(nk):
                    ins = e_.activation(pt[:, j, :], pT[:, j * 128:(j + 1) * 128],
                                        AF.Identity, bias=0.0, scale=1.0)
                ins.then_inc(C.sem("ev"), 1)
                e_.wait_ge(C.sem("mm"), b_mm + 3 * ui + 3)
                if ui >= 2:
                    e_.wait_ge(C.sem(f"s{ui % 2}"), b_s[ui % 2] + 16 * (ui // 2))
                ins = e_.activation(osb[ui % 2][:, :],
                                    pO[:, (ui % 2) * 512:(ui % 2) * 512 + 128],
                                    AF.Identity, bias=0.0, scale=1.0)
                ins.then_inc(C.sem("ev"), 1)
                e_.wait_ge(C.sem("ev"), b_ev + 3 * ui + 3)
                e_.dma_start(o_hbm[h * 128:(h + 1) * 128, qt_abs * 128:(qt_abs + 1) * 128],
                             osb[ui % 2][:, :]).then_inc(C.sem(f"s{ui % 2}"), 16)

    C.bump("q0", 16 * ceil_div(C.NB, 2))
    C.bump("q1", 16 * (C.NB // 2))
    C.bump("mm", 3 * nu)
    C.bump("x1", 3 * nu)
    C.bump("ev", 3 * nu)
    C.bump("s0", 16 * ceil_div(nu, 2))
    C.bump("s1", 16 * (nu // 2))
    e0, e1 = C.base("s0"), C.base("s1")
    with nc.Block() as blk:
        @blk.sync
        def _(e_):
            e_.wait_ge(C.sem("s0"), e0)
            e_.wait_ge(C.sem("s1"), e1)


def sublayer_L0_attn_a(C, W, sc, r_src, r_dst):
    C.push_scope()
    ln = C.sb(f"s1_ln{C.uid()}", [128, ND, C.T], BF16)
    ph_layernorm(C, r_src, ln)
    ph_proj(C, ln, W["L0aa_wq"], sc["q"], bias_hbm=W["L0aa_bq"],
            scale=INV_SQRT_DK, bias_scale=INV_SQRT_DK)
    ph_proj(C, ln, W["L0aa_wk"], sc["k"], bias_hbm=W["L0aa_bk"])
    ph_proj_tm(C, ln, W["L0aa_wv"], sc["v"], bias_hbm=W["L0aa_bv"])
    C.pop_scope()
    ph_attention4(C, sc["q"], sc["k"], sc["v"], sc["o"],
                   n_heads=16, n_kv=16, mask_kind="causal")
    ph_out_proj_add(C, sc["o"], W["L0aa_wo"], W["L0aa_bo"], r_src, r_dst)


WEIGHT_SPECS = {
    "L0aa_wq": ("L0_attn_a_w_q", "T", (D, D)),
    "L0aa_bq": ("L0_attn_a_b_q", None, (D,)),
    "L0aa_wk": ("L0_attn_a_w_k", "T", (D, D)),
    "L0aa_bk": ("L0_attn_a_b_k", None, (D,)),
    "L0aa_wv": ("L0_attn_a_w_v", "T", (D, D)),
    "L0aa_bv": ("L0_attn_a_b_v", None, (D,)),
    "L0aa_wo": ("L0_attn_a_w_o", "T", (D, D)),
    "L0aa_bo": ("L0_attn_a_b_o", None, (D,)),
    "L0f_wr": ("L0_ffn_router_weight", "T", (D, 8)),
    "L0f_g": ("L0_ffn_sh_gate", "T", (D, 8192)),
    "L0f_u": ("L0_ffn_sh_up", "T", (D, 8192)),
    "L0f_d": ("L0_ffn_sh_down", "T", (8192, D)),
    "L0f_rg": ("L0_ffn_r_gate", "T", (D, 1024)),
    "L0f_ru": ("L0_ffn_r_up", "T", (D, 1024)),
    "L0f_rd": ("L0_ffn_r_down", "T", (1024, D)),
    "L0ab_wq": ("L0_attn_b_wq", "T", (D, D)),
    "L0ab_bq": ("L0_attn_b_bq", None, (D,)),
    "L0ab_wk": ("L0_attn_b_wk", "T", (D, 512)),
    "L0ab_bk": ("L0_attn_b_bk", None, (512,)),
    "L0ab_wv": ("L0_attn_b_wv", "T", (D, 512)),
    "L0ab_bv": ("L0_attn_b_bv", None, (512,)),
    "L0ab_wo": ("L0_attn_b_wo", "T", (D, D)),
    "L0ab_bo": ("L0_attn_b_bo", None, (D,)),
    "L1aa_wq": ("L1_attn_a_wq", "T", (D, D)),
    "L1aa_wk": ("L1_attn_a_wk", "T", (D, 512)),
    "L1aa_wv": ("L1_attn_a_wv", "T", (D, 512)),
    "L1aa_wo": ("L1_attn_a_wo", "T", (D, D)),
    "L1aa_gq": ("L1_attn_a_gamma_q", None, (128,)),
    "L1aa_gk": ("L1_attn_a_gamma_k", None, (128,)),
    "L1f_wr": ("L1_ffn_gate_w", "T", (D, 8)),
    "L1f_wd": ("L1_ffn_wd", "L1WD", (D, 4608)),
    "L1f_bd": ("L1_ffn_bd", "L1BD", (4608,)),
    "L1f_wu": ("L1_ffn_wu", "L1WU", (4608, D)),
    "L1f_bu": ("L1_ffn_bu", None, (8, D)),
    "L1f_bsu": ("L1_ffn_bs_u", None, (D,)),
    "L1ab_wq": ("L1_attn_b_Wq", "T", (D, D)),
    "L1ab_bq": ("L1_attn_b_bq", None, (D,)),
    "L1ab_wk": ("L1_attn_b_Wk", "T", (D, D)),
    "L1ab_bk": ("L1_attn_b_bk", None, (D,)),
    "L1ab_wv": ("L1_attn_b_Wv", "T", (D, D)),
    "L1ab_bv": ("L1_attn_b_bv", None, (D,)),
    "L1ab_wo": ("L1_attn_b_Wo", "T", (D, D)),
    "L1ab_bo": ("L1_attn_b_bo", None, (D,)),
}


def build_program(T=2048, upto=1):
    nc = bass.Bass("TRN2", target_bir_lowering=False)
    x = nc.dram_tensor("x", [D, T], F32, kind="ExternalInput")
    out = nc.dram_tensor("out", [D, T], F32, kind="ExternalOutput")
    W = {}
    for name, (_, _, shape) in WEIGHT_SPECS.items():
        W[name] = nc.dram_tensor(name, list(shape), F32, kind="ExternalInput")
    r1 = nc.dram_tensor("r1", [D, T], F32)
    r2 = nc.dram_tensor("r2", [D, T], F32)
    sc = {
        "q": nc.dram_tensor("sc_q", [D, T], BF16),
        "k": nc.dram_tensor("sc_k", [D, T], BF16),
        "v": nc.dram_tensor("sc_v", [T, D], BF16),
        "o": nc.dram_tensor("sc_o", [D, T], BF16),
        "h": nc.dram_tensor("sc_h", [9216, T], BF16),
        "qt": nc.dram_tensor("sc_qt", [T, D], BF16),
        "kt": nc.dram_tensor("sc_kt", [T, 512], BF16),
        "g1": nc.dram_tensor("sc_g1", [D, T], BF16),
        "g2": nc.dram_tensor("sc_g2", [D, T], BF16),
        "g3": nc.dram_tensor("sc_g3", [D, T], BF16),
    }
    sublayers = [sublayer_L0_attn_a, sublayer_L0_ffn, sublayer_L0_attn_b, sublayer_L0_ffn, sublayer_L1_attn_a, sublayer_L1_ffn, sublayer_L1_attn_b, sublayer_L1_ffn][:upto]
    chain = [x, r1, r2, r1, r2, r1, r2, r1, out]
    with ExitStack() as es:
        C = Ctx(nc, es, T)
        for i, f in enumerate(sublayers):
            src = chain[i]
            dst = out if i == len(sublayers) - 1 else chain[i + 1]
            f(C, W, sc, src, dst)
    return nc


def host_weights(inputs):
    m = {}
    for name, (key, tf, _) in WEIGHT_SPECS.items():
        a = np.asarray(inputs[key], dtype=np.float32)
        if tf == "T":
            a = np.ascontiguousarray(a.T)
        elif tf == "L1WD":
            wd = a
            wsd = np.asarray(inputs["L1_ffn_ws_d"], dtype=np.float32)
            a = np.ascontiguousarray(np.concatenate(
                [np.concatenate([wd[e].T for e in range(8)], axis=1), wsd.T], axis=1))
        elif tf == "L1BD":
            bsd = np.asarray(inputs["L1_ffn_bs_d"], dtype=np.float32)
            a = np.ascontiguousarray(np.concatenate([a.reshape(-1), bsd]))
        elif tf == "L1WU":
            wu = a
            wsu = np.asarray(inputs["L1_ffn_ws_u"], dtype=np.float32)
            a = np.ascontiguousarray(np.concatenate(
                [np.concatenate([wu[e].T for e in range(8)], axis=0), wsu.T], axis=0))
        m[name] = a
    return m


def kernel(**inputs):
    T = 2048
    nc = build_program(T=T, upto=8)
    wmap = host_weights(inputs)
    x_all = np.asarray(inputs["L0_attn_a_x"], dtype=np.float32)
    in_maps = []
    for b in range(8):
        d = dict(wmap)
        d["x"] = np.ascontiguousarray(x_all[b].T)
        in_maps.append(d)
    res = run_bass_kernel_spmd(nc, in_maps, core_ids=list(range(8)))
    return np.stack([res.results[b]["out"].T for b in range(8)]).astype(np.float32)


@_scoped
def ph_router_comb(C, ln_sbuf, wr_hbm, c_b):
    nc, T = C.nc, C.T
    u = C.uid()
    wr = C.sb(f"rt_wr{u}", [128, ND, 8], BF16)
    wrs = C.sb(f"rt_ws{u}", [128, ND, 8], F32)
    lg = C.sb(f"rt_lg{u}", [128, 8], F32)
    ex = C.sb(f"rt_ex{u}", [128, 8], F32)
    sg = C.sb(f"rt_sg{u}", [128, 8], F32)
    st = C.sb(f"rt_st{u}", [128, 6], F32)
    ccol = C.sb(f"rt_cc{u}", [128, C.NT], F32)
    crow = C.sb(f"rt_cr{u}", [1, T], F32)
    identf = C.sb(f"rt_id{u}", [128, 128], F32)
    onesf = C.sb(f"rt_on{u}", [1, 128], F32)
    pl = C.psum(f"rt_pl{u}", [128, 512], F32)
    pr = C.psum(f"rt_pr{u}", [128, 512], F32)

    idf_h = C.inline("c_identf", np.eye(128, dtype=np.float32))
    ld0 = C.base("ld")
    with nc.Block() as blk:
        @blk.sync
        def _(e_):
            e_.dma_start(wrs[:, :, :], chunked(wr_hbm)).then_inc(C.sem("ld"), 16)
            e_.dma_start(identf[:, :], idf_h[:, :]).then_inc(C.sem("ld"), 16)
        @blk.gpsimd
        def _(e_):
            e_.memset(onesf[:, :], 1.0)
        @blk.vector
        def _(e_):
            e_.wait_ge(C.sem("ld"), ld0 + 32)
    C.bump("ld", 32)
    with nc.Block() as blk:
        @blk.vector
        def _(e_):
            e_.tensor_copy(wr[:, :, :], wrs[:, :, :])

    for tt in range(C.NT):
        with nc.Block() as blk:
            @blk.tensor
            def _(e_):
                for c in range(ND):
                    e_.matmul(pl[:, 0:8], ln_sbuf[:, c, tt * 128:(tt + 1) * 128],
                              wr[:, c, :], start=(c == 0), stop=(c == ND - 1))
        with nc.Block() as blk:
            @blk.vector
            def _(e_):
                C.fence(e_, e_.reduce_max(st[:, 0:1], pl[:, 0:8], axis=AX.X,
                                          negate=True), "fv")
                C.fence(e_, e_.tensor_scalar(lg[:, :], pl[:, 0:8], st[:, 0:1], None,
                                             ALU.add), "fv")
        with nc.Block() as blk:
            @blk.scalar
            def _(e_):
                C.fence(e_, e_.activation(ex[:, :], lg[:, :], AF.Exp, bias=0.0,
                                          scale=1.0), "fs")
                e_.activation(sg[:, :], lg[:, :], AF.Sign, bias=0.0, scale=1.0)
        with nc.Block() as blk:
            @blk.vector
            def _(e_):
                C.fence(e_, e_.tensor_scalar(sg[:, :], sg[:, :], 1.0, 1.0,
                                             ALU.add, ALU.min), "fv")
                C.fence(e_, e_.tensor_sub(sg[:, :], ex[:, :], sg[:, :]), "fv")
                C.fence(e_, e_.reduce_max(st[:, 1:2], sg[:, :], axis=AX.X), "fv")
                C.fence(e_, e_.tensor_scalar(st[:, 2:3], st[:, 1:2], 1.0, None,
                                             ALU.add), "fv")
                C.fence(e_, e_.reciprocal(st[:, 3:4], st[:, 2:3]), "fv")
                C.fence(e_, e_.tensor_mul(st[:, 4:5], st[:, 1:2], st[:, 3:4]), "fv")
                C.fence(e_, e_.tensor_add(ccol[:, tt:tt + 1], st[:, 3:4], st[:, 4:5]),
                        "fv")
    for tt in range(C.NT):
        with nc.Block() as blk:
            @blk.tensor
            def _(e_):
                e_.transpose(pr[0:1, 0:128], ccol[:, tt:tt + 1], identf[:, :])
        with nc.Block() as blk:
            @blk.vector
            def _(e_):
                e_.tensor_copy(crow[0:1, tt * 128:(tt + 1) * 128], pr[0:1, 0:128])
    for tb in range(C.NB):
        with nc.Block() as blk:
            @blk.tensor
            def _(e_):
                e_.matmul(pr[:, 0:512], onesf[:, :], crow[0:1, tb * 512:(tb + 1) * 512],
                          start=True, stop=True)
        with nc.Block() as blk:
            @blk.vector
            def _(e_):
                e_.tensor_copy(c_b[:, tb * 512:(tb + 1) * 512], pr[:, 0:512])


@_scoped
def ph_gu(C, ln_sbuf, wg_hbm, wu_hbm, h_hbm, *, h_row0=0, c_b=None):
    nc, T = C.nc, C.T
    E = wg_hbm.shape[1]
    NE = E // 128
    u = C.uid()
    wsg = [C.sb(f"gu_ws{u}_{i}", [128, ND, 128], F32) for i in range(2)]
    wsu = [C.sb(f"gu_wt{u}_{i}", [128, ND, 128], F32) for i in range(2)]
    wgb = [C.sb(f"gu_wg{u}_{i}", [128, ND, 128], BF16) for i in range(2)]
    wub = [C.sb(f"gu_wu{u}_{i}", [128, ND, 128], BF16) for i in range(2)]
    gs = [C.sb(f"gu_gs{u}_{i}", [128, 512], BF16) for i in range(2)]
    hb = [C.sb(f"gu_hb{u}_{i}", [128, 512], BF16) for i in range(2)]
    ps = C.psum(f"gu_ps{u}", [128, 2048], F32)

    b_ld = C.base("ld")
    b_ca = C.base("ca")
    b_mm = C.base("mm")
    b_ev = C.base("ev")
    b_x1 = C.base("x1")
    b_s = [C.base("s0"), C.base("s1")]
    nunits = NE * C.NB
    with nc.Block() as blk:
        @blk.sync
        def _(e_):
            for ib in range(NE):
                b = ib % 2
                if ib >= 1:
                    e_.wait_ge(C.sem("ld"), b_ld + 32 * ib)
                if ib >= 2:
                    e_.wait_ge(C.sem("ca"), b_ca + 2 * (ib - 1))
                e_.dma_start(
                    wsg[b][:, :, :],
                    wg_hbm[:, ib * 128:(ib + 1) * 128].rearrange("(c p) e -> p c e", p=128),
                ).then_inc(C.sem("ld"), 16)
                e_.wait_ge(C.sem("ld"), b_ld + 32 * ib + 16)
                if ib >= 2:
                    e_.wait_ge(C.sem("ca"), b_ca + 2 * (ib - 1) + 1)
                e_.dma_start(
                    wsu[b][:, :, :],
                    wu_hbm[:, ib * 128:(ib + 1) * 128].rearrange("(c p) e -> p c e", p=128),
                ).then_inc(C.sem("ld"), 16)
        @blk.vector
        def _(e_):
            for ib in range(NE):
                b = ib % 2
                e_.wait_ge(C.sem("ld"), b_ld + 32 * ib + 16)
                if ib >= 2:
                    e_.wait_ge(C.sem("mm"), b_mm + 2 * C.NB * (ib - 1))
                e_.tensor_copy(wgb[b][:, :, :], wsg[b][:, :, :]).then_inc(C.sem("ca"), 1)
                e_.wait_ge(C.sem("ld"), b_ld + 32 * (ib + 1))
                e_.tensor_copy(wub[b][:, :, :], wsu[b][:, :, :]).then_inc(C.sem("ca"), 1)
                for tb in range(C.NB):
                    i = ib * C.NB + tb
                    p = i % 2
                    e_.wait_ge(C.sem("mm"), b_mm + 2 * i + 2)
                    e_.wait_ge(C.sem("ev"), b_ev + i + 1)
                    if i >= 2:
                        e_.wait_ge(C.sem(f"s{p}"), b_s[p] + 16 * (i // 2))
                    if c_b is None:
                        e_.tensor_mul(hb[p][:, :], gs[p][:, :],
                                      ps[:, p * 1024 + 512:p * 1024 + 1024]).then_inc(
                                          C.sem("x1"), 1)
                    else:
                        C.fence(e_, e_.tensor_mul(hb[p][:, :], gs[p][:, :],
                                                  ps[:, p * 1024 + 512:p * 1024 + 1024]),
                                "fv")
                        e_.tensor_mul(hb[p][:, :], hb[p][:, :],
                                      c_b[:, tb * 512:(tb + 1) * 512]).then_inc(
                                          C.sem("x1"), 1)
        @blk.tensor
        def _(e_):
            for ib in range(NE):
                b = ib % 2
                for tb in range(C.NB):
                    i = ib * C.NB + tb
                    p = i % 2
                    if i >= 2:
                        e_.wait_ge(C.sem("ev"), b_ev + i - 1)
                        e_.wait_ge(C.sem("x1"), b_x1 + i - 1)
                    e_.wait_ge(C.sem("ca"), b_ca + 2 * ib + 1)
                    for c in range(ND):
                        ins = e_.matmul(ps[:, p * 1024:p * 1024 + 512],
                                        wgb[b][:, c, :], ln_sbuf[:, c, tb * 512:(tb + 1) * 512],
                                        start=(c == 0), stop=(c == ND - 1))
                    ins.then_inc(C.sem("mm"), 1)
                    e_.wait_ge(C.sem("ca"), b_ca + 2 * ib + 2)
                    for c in range(ND):
                        ins = e_.matmul(ps[:, p * 1024 + 512:p * 1024 + 1024],
                                        wub[b][:, c, :], ln_sbuf[:, c, tb * 512:(tb + 1) * 512],
                                        start=(c == 0), stop=(c == ND - 1))
                    ins.then_inc(C.sem("mm"), 1)
        @blk.scalar
        def _(e_):
            for ib in range(NE):
                for tb in range(C.NB):
                    i = ib * C.NB + tb
                    p = i % 2
                    e_.wait_ge(C.sem("mm"), b_mm + 2 * i + 1)
                    if i >= 2:
                        e_.wait_ge(C.sem("x1"), b_x1 + i - 1)
                    e_.activation(gs[p][:, :], ps[:, p * 1024:p * 1024 + 512],
                                  AF.Silu, bias=0.0, scale=1.0).then_inc(C.sem("ev"), 1)
                    e_.wait_ge(C.sem("x1"), b_x1 + i + 1)
                    e_.dma_start(
                        h_hbm[h_row0 + ib * 128:h_row0 + (ib + 1) * 128,
                              tb * 512:(tb + 1) * 512],
                        hb[p][:, :]).then_inc(C.sem(f"s{p}"), 16)
    C.bump("ld", 32 * NE)
    C.bump("ca", 2 * NE)
    C.bump("mm", 2 * nunits)
    C.bump("ev", nunits)
    C.bump("x1", nunits)
    C.bump("s0", 16 * ceil_div(nunits, 2))
    C.bump("s1", 16 * (nunits // 2))
    e0, e1 = C.base("s0"), C.base("s1")
    with nc.Block() as blk:
        @blk.sync
        def _(e_):
            e_.wait_ge(C.sem("s0"), e0)
            e_.wait_ge(C.sem("s1"), e1)




@_scoped
def ph_down_add(C, h_hbm, wd_specs, r_src, r_dst, n_i_blocks):
    nc, T = C.nc, C.T
    u = C.uid()
    NIB = n_i_blocks
    wq = C.sb(f"dn_wq{u}", [128, NIB, 512], BF16)
    wstage = [C.sb(f"dn_ws{u}_{i}", [128, 512], F32) for i in range(4)]
    hq = C.sb(f"dn_hq{u}", [128, NIB, 512], BF16)
    yb = [C.sb(f"dn_yb{u}_{i}", [128, 512], F32) for i in range(2)]
    rtile = [C.sb(f"dn_rt{u}_{i}", [128, 512], F32) for i in range(2)]
    ob = [C.sb(f"dn_ob{u}_{i}", [128, 512], F32) for i in range(2)]
    ps = C.psum(f"dn_ps{u}", [128, 4096], F32)

    def wd_block_ap(ib):
        for w_hbm, i0, nb in wd_specs:
            if i0 <= ib < i0 + nb:
                return w_hbm[(ib - i0) * 128:(ib - i0 + 1) * 128, :]
        raise ValueError(ib)

    b_w = [C.base(f"w{i}") for i in range(4)]
    b_ca = C.base("ca")
    b_mm = C.base("mm")
    b_ev = C.base("ev")
    b_x2 = C.base("x2")
    b_hl = C.base("hl")
    b_r = [C.base("r0"), C.base("r1")]
    b_s = [C.base("s0"), C.base("s1")]
    nw = [0, 0, 0, 0]

    for eq in range(4):
        with nc.Block() as blk:
            @blk.sync
            def _(e_):
                for ib in range(NIB):
                    sp = ib % 4
                    ca_abs = eq * NIB + ib
                    if ca_abs >= 4:
                        e_.wait_ge(C.sem("ca"), b_ca + ca_abs - 3)
                    e_.dma_start(wstage[sp][:, :],
                                 wd_block_ap(ib)[:, eq * 512:(eq + 1) * 512]
                                 ).then_inc(C.sem(f"w{sp}"), 16)
            @blk.vector
            def _(e_):
                for ib in range(NIB):
                    sp = ib % 4
                    nw[sp] += 16
                    e_.wait_ge(C.sem(f"w{sp}"), b_w[sp] + nw[sp])
                    e_.tensor_copy(wq[:, ib, :], wstage[sp][:, :]).then_inc(C.sem("ca"), 1)
        C.bump("ca", NIB)

        with nc.Block() as blk:
            @blk.sync
            def _(e_):
                for tb in range(C.NB):
                    tb_abs = eq * C.NB + tb
                    if tb_abs >= 1:
                        e_.wait_ge(C.sem("mm"), b_mm + 4 * tb_abs)
                    e_.dma_start(hq[:, :, :],
                                 chunked(h_hbm[:, tb * 512:(tb + 1) * 512])
                                 ).then_inc(C.sem("hl"), 16)
                    for ec in range(4):
                        i = tb_abs * 4 + ec
                        if i >= 2:
                            e_.wait_ge(C.sem("x2"), b_x2 + i - 1)
                        e_.dma_start(
                            rtile[i % 2][:, :],
                            r_src[eq * 512 + ec * 128:eq * 512 + (ec + 1) * 128,
                                  tb * 512:(tb + 1) * 512]).then_inc(
                                      C.sem(f"r{i % 2}"), 16)
            @blk.tensor
            def _(e_):
                for tb in range(C.NB):
                    tb_abs = eq * C.NB + tb
                    tp = tb_abs % 2
                    e_.wait_ge(C.sem("hl"), b_hl + 16 * (tb_abs + 1))
                    if tb_abs >= 2:
                        e_.wait_ge(C.sem("x2"), b_x2 + 4 * (tb_abs - 1))
                    for ec in range(4):
                        for ib in range(NIB):
                            ins = e_.matmul(
                                ps[:, tp * 2048 + ec * 512:tp * 2048 + (ec + 1) * 512],
                                wq[:, ib, ec * 128:(ec + 1) * 128], hq[:, ib, :],
                                start=(ib == 0), stop=(ib == NIB - 1))
                        ins.then_inc(C.sem("mm"), 1)
            @blk.scalar
            def _(e_):
                for tb in range(C.NB):
                    tb_abs = eq * C.NB + tb
                    for ec in range(4):
                        i = tb_abs * 4 + ec
                        pb = i % 2
                        e_.wait_ge(C.sem("x2"), b_x2 + i + 1)
                        e_.dma_start(
                            r_dst[eq * 512 + ec * 128:eq * 512 + (ec + 1) * 128,
                                  tb * 512:(tb + 1) * 512],
                            ob[pb][:, :]).then_inc(C.sem(f"s{pb}"), 16)
            @blk.vector
            def _(e_):
                for tb in range(C.NB):
                    tb_abs = eq * C.NB + tb
                    tp = tb_abs % 2
                    for ec in range(4):
                        i = tb_abs * 4 + ec
                        pb = i % 2
                        e_.wait_ge(C.sem("mm"), b_mm + 4 * tb_abs + ec + 1)
                        e_.wait_ge(C.sem(f"r{pb}"), b_r[pb] + 16 * (i // 2 + 1))
                        if i >= 2:
                            e_.wait_ge(C.sem(f"s{pb}"), b_s[pb] + 16 * (i // 2))
                        e_.scalar_tensor_tensor(
                            ob[pb][:, :],
                            ps[:, tp * 2048 + ec * 512:tp * 2048 + (ec + 1) * 512],
                            0.0, rtile[pb][:, :], ALU.add, ALU.add).then_inc(
                                C.sem("x2"), 1)
        n = 4 * C.NB
        C.bump("hl", 16 * C.NB)
        C.bump("mm", 4 * C.NB)
        C.bump("x2", n)
    ntot = 16 * C.NB
    for sp in range(4):
        C.bump(f"w{sp}", nw[sp])
    C.bump("r0", 16 * ceil_div(ntot, 2))
    C.bump("r1", 16 * (ntot // 2))
    C.bump("s0", 16 * ceil_div(ntot, 2))
    C.bump("s1", 16 * (ntot // 2))
    e0, e1 = C.base("s0"), C.base("s1")
    with nc.Block() as blk:
        @blk.sync
        def _(e_):
            e_.wait_ge(C.sem("s0"), e0)
            e_.wait_ge(C.sem("s1"), e1)


@_scoped
def ph_down_add2(C, h_hbm, wd_specs, r_src, r_dst, n_i_blocks):
    nc, T = C.nc, C.T
    u = C.uid()
    NIB = n_i_blocks
    assert NIB % 2 == 0
    NH = NIB // 2
    NB = C.NB
    wq = C.sb(f"dn_wq{u}", [128, NIB, 512], BF16)
    wstage = [C.sb(f"dn_ws{u}_{i}", [128, 512], F32) for i in range(4)]
    hq = C.sb(f"dn_hq{u}", [128, NIB, 512], BF16)
    rtile = [C.sb(f"dn_rt{u}_{i}", [128, 512], F32) for i in range(2)]
    ob = [C.sb(f"dn_ob{u}_{i}", [128, 512], F32) for i in range(2)]
    ps = C.psum(f"dn_ps{u}", [128, 4096], F32)

    def wd_block_ap(ib):
        for w_hbm, i0, nb in wd_specs:
            if i0 <= ib < i0 + nb:
                return w_hbm[(ib - i0) * 128:(ib - i0 + 1) * 128, :]
        raise ValueError(ib)

    b_w = [C.base(f"w{i}") for i in range(4)]
    b_ca = C.base("ca")
    b_mm = C.base("mm")
    b_x2 = C.base("x2")
    b_hl = C.base("hl")
    b_l0 = C.base("l0")
    b_r = [C.base("r0"), C.base("r1")]
    b_s = [C.base("s0"), C.base("s1")]

    def mm_idx(eq, tb, half, ec, ibl):
        return ((eq * NB + tb) * 8 + half * 4 + ec) * NH + ibl

    with nc.Block() as blk:
        @blk.sync
        def _(e_):
            nwl = [0, 0, 0, 0]

            def wload(eq, lo, hi_):
                for ib in range(lo, hi_):
                    k = eq * NIB + ib
                    sp = k % 4
                    if k >= 4:
                        e_.wait_ge(C.sem("ca"), b_ca + k - 3)
                    nwl[sp] += 16
                    e_.dma_start(wstage[sp][:, :],
                                 wd_block_ap(ib)[:, eq * 512:(eq + 1) * 512]
                                 ).then_inc(C.sem(f"w{sp}"), 16)
            for eq in range(4):
                for tb in range(NB):
                    tb_abs = eq * NB + tb
                    for half in range(2):
                        if tb_abs >= 1:
                            e_.wait_ge(C.sem("mm"),
                                       b_mm + mm_idx(0, tb_abs - 1, half, 3, NH - 1) + 1)
                        e_.dma_start(
                            hq[:, half * NH:(half + 1) * NH, :],
                            chunked(h_hbm[half * NH * 128:(half + 1) * NH * 128,
                                          tb * 512:(tb + 1) * 512])
                        ).then_inc(C.sem(["hl", "l0"][half]), 16)
                    for ec in range(4):
                        i = tb_abs * 4 + ec
                        if i >= 2:
                            e_.wait_ge(C.sem("x2"), b_x2 + i - 1)
                        e_.dma_start(
                            rtile[i % 2][:, :],
                            r_src[eq * 512 + ec * 128:eq * 512 + (ec + 1) * 128,
                                  tb * 512:(tb + 1) * 512]).then_inc(
                                      C.sem(f"r{i % 2}"), 16)
                        if ec == 1:
                            if eq == 0 and tb == 0:
                                wload(0, 0, NIB)
                            if tb == NB - 1 and eq < 3:
                                wload(eq + 1, 0, NH + 4)
                        if ec == 3 and tb == NB - 1 and eq < 3:
                            wload(eq + 1, NH + 4, NIB)
        @blk.vector
        def _(e_):
            nw = [0, 0, 0, 0]

            def casts(eq, lo, hi_):
                for ib in range(lo, hi_):
                    k = eq * NIB + ib
                    sp = k % 4
                    nw[sp] += 16
                    e_.wait_ge(C.sem(f"w{sp}"), b_w[sp] + nw[sp])
                    if eq > 0:
                        e_.wait_ge(C.sem("mm"),
                                   b_mm + mm_idx(eq - 1, NB - 1, ib // NH, 3, ib % NH) + 1)
                    e_.tensor_copy(wq[:, ib, :], wstage[sp][:, :]).then_inc(
                        C.sem("ca"), 1)

            def stt(eq, tb, ec):
                tb_abs = eq * NB + tb
                i = tb_abs * 4 + ec
                pb = i % 2
                tp = tb_abs % 2
                e_.wait_ge(C.sem("mm"), b_mm + mm_idx(eq, tb, 1, ec, NH - 1) + 1)
                e_.wait_ge(C.sem(f"r{pb}"), b_r[pb] + 16 * (i // 2 + 1))
                if i >= 2:
                    e_.wait_ge(C.sem(f"s{pb}"), b_s[pb] + 16 * (i // 2))
                e_.scalar_tensor_tensor(
                    ob[pb][:, :],
                    ps[:, tp * 2048 + ec * 512:tp * 2048 + (ec + 1) * 512],
                    0.0, rtile[pb][:, :], ALU.add, ALU.add).then_inc(
                        C.sem("x2"), 1)

            casts(0, 0, NIB)
            for eq in range(4):
                for tb in range(NB):
                    if eq < 3 and tb == NB - 1:
                        casts(eq + 1, 0, NH)
                        for ec in range(3):
                            stt(eq, tb, ec)
                        casts(eq + 1, NH, NIB)
                        stt(eq, tb, 3)
                    else:
                        for ec in range(4):
                            stt(eq, tb, ec)
        @blk.tensor
        def _(e_):
            for eq in range(4):
                for tb in range(NB):
                    tb_abs = eq * NB + tb
                    tp = tb_abs % 2
                    if tb_abs >= 2:
                        e_.wait_ge(C.sem("x2"), b_x2 + 4 * (tb_abs - 1))
                    for half in range(2):
                        e_.wait_ge(C.sem(["hl", "l0"][half]),
                                   [b_hl, b_l0][half] + 16 * (tb_abs + 1))
                        for ec in range(4):
                            for ibl in range(NH):
                                ib = half * NH + ibl
                                if tb == 0 and ec == 0:
                                    e_.wait_ge(C.sem("ca"),
                                               b_ca + eq * NIB + ib + 1)
                                e_.matmul(
                                    ps[:, tp * 2048 + ec * 512:tp * 2048 + (ec + 1) * 512],
                                    wq[:, ib, ec * 128:(ec + 1) * 128],
                                    hq[:, ib, :],
                                    start=(half == 0 and ibl == 0),
                                    stop=(half == 1 and ibl == NH - 1),
                                    skip_group_check=True).then_inc(C.sem("mm"), 1)
        @blk.scalar
        def _(e_):
            for eq in range(4):
                for tb in range(NB):
                    tb_abs = eq * NB + tb
                    for ec in range(4):
                        i = tb_abs * 4 + ec
                        pb = i % 2
                        e_.wait_ge(C.sem("x2"), b_x2 + i + 1)
                        e_.dma_start(
                            r_dst[eq * 512 + ec * 128:eq * 512 + (ec + 1) * 128,
                                  tb * 512:(tb + 1) * 512],
                            ob[pb][:, :]).then_inc(C.sem(f"s{pb}"), 16)

    ntot = 16 * NB
    C.bump("mm", 4 * NB * 8 * NH)
    C.bump("ca", 4 * NIB)
    C.bump("hl", 16 * 4 * NB)
    C.bump("l0", 16 * 4 * NB)
    for sp in range(4):
        C.bump(f"w{sp}", 16 * NIB)
    C.bump("x2", ntot)
    C.bump("r0", 16 * ceil_div(ntot, 2))
    C.bump("r1", 16 * (ntot // 2))
    C.bump("s0", 16 * ceil_div(ntot, 2))
    C.bump("s1", 16 * (ntot // 2))
    e0, e1 = C.base("s0"), C.base("s1")
    with nc.Block() as blk:
        @blk.sync
        def _(e_):
            e_.wait_ge(C.sem("s0"), e0)
            e_.wait_ge(C.sem("s1"), e1)


def sublayer_L0_ffn(C, W, sc, r_src, r_dst):
    C.push_scope()
    su = C.uid()
    ln = C.sb(f"sf_ln{su}", [128, ND, C.T], BF16)
    c_b = C.sb(f"sf_cb{su}", [128, C.T], F32)
    ph_layernorm(C, r_src, ln)
    ph_router_comb(C, ln, W["L0f_wr"], c_b)
    ph_gu(C, ln, W["L0f_g"], W["L0f_u"], sc["h"], h_row0=0)
    ph_gu(C, ln, W["L0f_rg"], W["L0f_ru"], sc["h"], h_row0=8192, c_b=c_b)
    C.pop_scope()
    ph_down_add2(C, sc["h"], [(W["L0f_d"], 0, 64), (W["L0f_rd"], 64, 8)],
                 r_src, r_dst, 72)


def _l0b_rope_tables(scale_q):
    Dh = 128
    inv_freq = 1.0 / 10000.0 ** (np.arange(0, Dh, 2, dtype=np.float32) / Dh)
    t = np.arange(16, dtype=np.float32)
    freqs = np.einsum("i,j->ij", t, inv_freq)
    emb = np.concatenate([freqs, freqs], axis=-1)
    cos = np.cos(emb)
    sin = np.sin(emb)
    sn = np.concatenate([-sin[:, 0:64], sin[:, 64:128]], axis=-1)
    return (cos * scale_q).astype(np.float32), (sn * scale_q).astype(np.float32)


def _make_rope_w_hook(C, tag0, n_heads, scale):
    tag = f"{tag0}_{C.uid()}"
    cos_np, sn_np = _l0b_rope_tables(scale)
    cos_t = np.tile(cos_np[None, :, :], (128, 1, 1)).astype(ml_dtypes.bfloat16)
    sn_t = np.tile(sn_np[None, :, :], (128, 1, 1)).astype(ml_dtypes.bfloat16)
    cos_h = C.inline(f"c_cos{tag0}_{n_heads}_{scale}", cos_t)
    sn_h = C.inline(f"c_sn{tag0}_{n_heads}_{scale}", sn_t)
    cos_sb = C.sb(f"rp_c{tag}", [128, 16, 128], BF16)
    sn_sb = C.sb(f"rp_s{tag}", [128, 16, 128], BF16)
    t1 = C.sb(f"rp_t1{tag}", [128, ND, 128], BF16)
    t2 = C.sb(f"rp_t2{tag}", [128, ND, 128], BF16)
    nc = C.nc
    ld0 = C.base("ld")
    with nc.Block() as blk:
        @blk.sync
        def _(e_):
            e_.dma_start(cos_sb[:, :, :], cos_h[:, :, :]).then_inc(C.sem("ld"), 16)
            e_.dma_start(sn_sb[:, :, :], sn_h[:, :, :]).then_inc(C.sem("ld"), 16)
        @blk.vector
        def _(e_):
            e_.wait_ge(C.sem("ld"), ld0 + 32)
    C.bump("ld", 32)

    def hook(e_, wb, ec):
        h = ec
        for c in range(ND):
            e_.tensor_mul(t1[:, c, :], wb[:, c, :], cos_sb[:, h, :])
            e_.tensor_mul(t2[:, c, 0:64], wb[:, c, 64:128], sn_sb[:, h, 0:64])
            ins = e_.tensor_mul(t2[:, c, 64:128], wb[:, c, 0:64], sn_sb[:, h, 64:128])
        C.fence(e_, ins, "fv")
        return e_.tensor_add(wb[:, :, :], t1[:, :, :], t2[:, :, :])

    return hook


@_scoped
def ph_rope_bias_row(C, bias_hbm, n_heads, scale, brow_out, ones_out):
    nc = C.nc
    u = C.uid()
    E = n_heads * 128
    cos_np, sn_np = _l0b_rope_tables(scale)
    cosr = C.inline(f"c_bcr{u}", cos_np[:n_heads].reshape(1, E))
    snr = C.inline(f"c_bsr{u}", sn_np[:n_heads].reshape(1, E))
    bf = C.sb(f"rb_b{u}", [1, E], F32)
    cr = C.sb(f"rb_c{u}", [1, E], F32)
    sr = C.sb(f"rb_s{u}", [1, E], F32)
    t1 = C.sb(f"rb_t1{u}", [1, E], F32)
    t2 = C.sb(f"rb_t2{u}", [1, E], F32)
    ld0 = C.base("ld")
    with nc.Block() as blk:
        @blk.sync
        def _(e_):
            e_.dma_start(bf[0:1, :], bias_hbm.rearrange("(o e) -> o e", o=1)).then_inc(
                C.sem("ld"), 16)
            e_.dma_start(cr[0:1, :], cosr[:, :]).then_inc(C.sem("ld"), 16)
            e_.dma_start(sr[0:1, :], snr[:, :]).then_inc(C.sem("ld"), 16)
        @blk.gpsimd
        def _(e_):
            e_.memset(ones_out[:, :], 1.0)
        @blk.vector
        def _(e_):
            e_.wait_ge(C.sem("ld"), ld0 + 48)
    C.bump("ld", 48)
    with nc.Block() as blk:
        @blk.vector
        def _(e_):
            C.fence(e_, e_.tensor_mul(t1[0:1, :], bf[0:1, :], cr[0:1, :]), "fv")
            for h in range(n_heads):
                lo = h * 128
                e_.tensor_mul(t2[0:1, lo:lo + 64], bf[0:1, lo + 64:lo + 128],
                              sr[0:1, lo:lo + 64])
                ins = e_.tensor_mul(t2[0:1, lo + 64:lo + 128], bf[0:1, lo:lo + 64],
                                    sr[0:1, lo + 64:lo + 128])
            C.fence(e_, ins, "fv")
            e_.tensor_add(brow_out[0:1, :], t1[0:1, :], t2[0:1, :])


def sublayer_L0_attn_b(C, W, sc, r_src, r_dst):
    C.push_scope()
    su = C.uid()
    ln = C.sb(f"s3_ln{su}", [128, ND, C.T], BF16)
    bqr = C.sb(f"s3_bqr{su}", [1, D], BF16)
    bkr = C.sb(f"s3_bkr{su}", [1, 512], BF16)
    onr = C.sb(f"s3_onr{su}", [1, 512], BF16)
    ph_layernorm(C, r_src, ln)
    ph_rope_bias_row(C, W["L0ab_bq"], 16, INV_SQRT_DK, bqr, onr)
    ph_rope_bias_row(C, W["L0ab_bk"], 4, 1.0, bkr, onr)
    C.push_scope()
    hook_q = _make_rope_w_hook(C, "q3", 16, INV_SQRT_DK)
    ph_proj(C, ln, W["L0ab_wq"], sc["q"], w_hook=hook_q, bias_row=(bqr, onr))
    C.pop_scope()
    C.push_scope()
    hook_k = _make_rope_w_hook(C, "k3", 4, 1.0)
    ph_proj(C, ln, W["L0ab_wk"], sc["k"], w_hook=hook_k, bias_row=(bkr, onr))
    C.pop_scope()
    ph_proj_tm(C, ln, W["L0ab_wv"], sc["v"], bias_hbm=W["L0ab_bv"])
    C.pop_scope()
    ph_attention4(C, sc["q"], sc["k"], sc["v"], sc["o"],
                  n_heads=16, n_kv=4, mask_kind="trilbias")
    ph_out_proj_add(C, sc["o"], W["L0ab_wo"], W["L0ab_bo"], r_src, r_dst)


@_scoped
def ph_rope_norm(C, tm_hbm, fm_hbm, n_heads, *, w_b=None):
    nc, T = C.nc, C.T
    u = C.uid()
    H = n_heads
    qld = [C.sb(f"rn_q{u}_{i}", [128, H, 128], BF16) for i in range(2)]
    t1 = C.sb(f"rn_t1{u}", [128, H, 128], BF16)
    t2 = C.sb(f"rn_t2{u}", [128, H, 128], BF16)
    q1 = C.sb(f"rn_q1{u}", [128, H, 128], BF16)
    q2 = C.sb(f"rn_q2{u}", [128, H, 128], BF16)
    sqd = C.sb(f"rn_sq{u}", [128, 128], F32)
    nrm = C.sb(f"rn_nr{u}", [128, H], F32)
    rr = C.sb(f"rn_rr{u}", [128, H], F32)
    cos_sb = C.sb(f"rn_c{u}", [128, C.NT, 128], F32)
    sin_sb = C.sb(f"rn_s{u}", [128, C.NT, 128], F32)
    ob = [C.sb(f"rn_ob{u}_{i}", [128, H, 128], BF16) for i in range(2)]
    pT = C.psum(f"rn_pt{u}", [128, 2048], BF16)
    ident = C.sb(f"rn_id{u}", [128, 128], BF16)

    Dh = 128
    inv_freq = 1.0 / 10000.0 ** (np.arange(0, Dh, 2, dtype=np.float32) / Dh)
    ang = np.arange(T, dtype=np.float32)[:, None] * inv_freq[None, :]
    cos_full = np.concatenate([np.cos(ang), np.cos(ang)], axis=-1)
    sin_full = np.concatenate([np.sin(ang), np.sin(ang)], axis=-1)
    sn_full = np.concatenate([-sin_full[:, 0:64], sin_full[:, 64:128]], axis=-1)
    cos_h = C.inline(f"c_l1cos{T}", cos_full.reshape(C.NT, 128, 128).transpose(1, 0, 2)
                     .copy())
    sin_h = C.inline(f"c_l1sin{T}", sn_full.reshape(C.NT, 128, 128).transpose(1, 0, 2)
                     .copy())
    id_h = C.inline("c_ident", np.eye(128, dtype=ml_dtypes.bfloat16))

    eps_ap = C.constcol(1e-6)
    ld0 = C.base("ld")
    with nc.Block() as blk:
        @blk.sync
        def _(e_):
            e_.dma_start(cos_sb[:, :, :], cos_h[:, :, :]).then_inc(C.sem("ld"), 16)
            e_.dma_start(sin_sb[:, :, :], sin_h[:, :, :]).then_inc(C.sem("ld"), 16)
            e_.dma_start(ident[:, :], id_h[:, :]).then_inc(C.sem("ld"), 16)
            e_.dma_start(qld[0][:, :, :],
                         tm_hbm[0:128, :].rearrange("(c p) e -> p c e", p=128)
                         if False else
                         tm_hbm[0:128, 0:H * 128].rearrange("t (h d) -> t h d", d=128)
                         ).then_inc(C.sem("q0"), 16)
        @blk.vector
        def _(e_):
            e_.wait_ge(C.sem("ld"), ld0 + 48)
        @blk.tensor
        def _(e_):
            e_.wait_ge(C.sem("ld"), ld0 + 48)
    C.bump("ld", 48)
    C.bump("q0", 16)

    b_q = [C.base("q0") - 16, C.base("q1")]
    b_x1 = C.base("x1")
    b_ev = C.base("ev")
    b_mm = C.base("mm")
    b_s = [C.base("s0"), C.base("s1")]

    with nc.Block() as blk:
        @blk.sync
        def _(e_):
            for tt in range(1, C.NT):
                if tt >= 2:
                    e_.wait_ge(C.sem("x1"), b_x1 + 3 * (tt - 2) + 1)
                e_.dma_start(qld[tt % 2][:, :, :],
                             tm_hbm[tt * 128:(tt + 1) * 128, 0:H * 128]
                             .rearrange("t (h d) -> t h d", d=128)
                             ).then_inc(C.sem(["q0", "q1"][tt % 2]), 16)

        @blk.vector
        def _(e_):
            for tt in range(C.NT):
                p = tt % 2
                e_.wait_ge(C.sem(["q0", "q1"][p]), b_q[p] + 16 * (tt // 2 + 1))
                ins = None
                for h in range(H):
                    e_.tensor_mul(t1[:, h, :], qld[p][:, h, :], cos_sb[:, tt, :])
                    e_.tensor_mul(t2[:, h, 0:64], qld[p][:, h, 64:128],
                                  sin_sb[:, tt, 0:64])
                    ins = e_.tensor_mul(t2[:, h, 64:128], qld[p][:, h, 0:64],
                                        sin_sb[:, tt, 64:128])
                C.fence(e_, ins, "fv")
                e_.tensor_add(q1[:, :, :], t1[:, :, :], t2[:, :, :]).then_inc(
                    C.sem("x1"), 1)
                e_.wait_ge(C.sem("x1"), b_x1 + 3 * tt + 1)
                ins = None
                for h in range(H):
                    ins = e_.scalar_tensor_tensor(t1[:, h, :], q1[:, h, :], 1.0,
                                                  q1[:, h, :], ALU.bypass, ALU.mult,
                                                  accum_out=nrm[:, h:h + 1])
                ins.then_inc(C.sem("x1"), 1)
                e_.wait_ge(C.sem("x1"), b_x1 + 3 * tt + 2)
                e_.wait_ge(C.sem("ev"), b_ev + 2 * tt + 1)
                C.fence(e_, e_.reciprocal(rr[:, :], nrm[:, :]), "fv")
                if tt >= 2:
                    e_.wait_ge(C.sem("mm"), b_mm + tt - 1)
                ins = None
                for h in range(H):
                    if w_b is None:
                        ins = e_.tensor_scalar(q2[:, h, :], q1[:, h, :], rr[:, h:h + 1],
                                               None, ALU.mult)
                    else:
                        C.fence(e_, e_.tensor_scalar(q2[:, h, :], q1[:, h, :],
                                                     rr[:, h:h + 1], None, ALU.mult),
                                "fv")
                        ins = e_.tensor_mul(q2[:, h, :], q2[:, h, :], w_b[:, :])
                ins.then_inc(C.sem("x1"), 1)

        @blk.scalar
        def _(e_):
            for tt in range(C.NT):
                e_.wait_ge(C.sem("x1"), b_x1 + 3 * tt + 2)
                ins = e_.activation(nrm[:, :], nrm[:, :], AF.Sqrt, bias=0.0, scale=1.0)
                C.fence(e_, ins, "fs")
                e_.activation(nrm[:, :], nrm[:, :], AF.Identity,
                              bias=eps_ap[:, :], scale=1.0).then_inc(C.sem("ev"), 1)
                e_.wait_ge(C.sem("mm"), b_mm + tt + 1)
                if tt >= 2:
                    e_.wait_ge(C.sem(["s0", "s1"][tt % 2]),
                               b_s[tt % 2] + 16 * (tt // 2))
                ins = None
                for h in range(H):
                    ins = e_.activation(ob[tt % 2][:, h, :], pT[:, h * 128:(h + 1) * 128],
                                        AF.Identity, bias=0.0, scale=1.0)
                ins.then_inc(C.sem("ev"), 1)
                e_.wait_ge(C.sem("ev"), b_ev + 2 * tt + 2)
                e_.dma_start(
                    fm_hbm[0:H * 128, tt * 128:(tt + 1) * 128]
                    .rearrange("(h d) t -> d h t", d=128),
                    ob[tt % 2][:, :, :]).then_inc(C.sem(["s0", "s1"][tt % 2]), 16)

        @blk.tensor
        def _(e_):
            for tt in range(C.NT):
                e_.wait_ge(C.sem("x1"), b_x1 + 3 * tt + 3)
                if tt >= 1:
                    e_.wait_ge(C.sem("ev"), b_ev + 2 * tt)
                ins = None
                for h in range(H):
                    ins = e_.transpose(pT[:, h * 128:(h + 1) * 128], q2[:, h, :],
                                       ident[:, :])
                ins.then_inc(C.sem("mm"), 1)

    C.bump("q0", 16 * (C.NT // 2 - 1) if C.NT >= 2 else 0)
    C.bump("q1", 16 * ceil_div(C.NT - 1, 2))
    C.bump("x1", 3 * C.NT)
    C.bump("ev", 2 * C.NT)
    C.bump("mm", C.NT)
    C.bump("s0", 16 * ceil_div(C.NT, 2))
    C.bump("s1", 16 * (C.NT // 2))
    e0, e1 = C.base("s0"), C.base("s1")
    with nc.Block() as blk:
        @blk.sync
        def _(e_):
            e_.wait_ge(C.sem("s0"), e0)
            e_.wait_ge(C.sem("s1"), e1)


@_scoped
def ph_gamma_tile(C, gq_hbm, gk_hbm, w_b, scale):
    nc = C.nc
    u = C.uid()
    rows = C.sb(f"gm_r{u}", [1, 3, 128], F32)
    onesf = C.sb(f"gm_on{u}", [1, 128], F32)
    pr = C.psum(f"gm_pr{u}", [128, 512], F32)
    ld0 = C.base("ld")
    with nc.Block() as blk:
        @blk.sync
        def _(e_):
            e_.dma_start(rows[0:1, 0, :], gq_hbm.rearrange("(o e) -> o e", o=1)
                         ).then_inc(C.sem("ld"), 16)
            e_.dma_start(rows[0:1, 1, :], gk_hbm.rearrange("(o e) -> o e", o=1)
                         ).then_inc(C.sem("ld"), 16)
        @blk.gpsimd
        def _(e_):
            e_.memset(onesf[:, :], 1.0)
        @blk.vector
        def _(e_):
            e_.wait_ge(C.sem("ld"), ld0 + 32)
    C.bump("ld", 32)
    with nc.Block() as blk:
        @blk.vector
        def _(e_):
            C.fence(e_, e_.tensor_mul(rows[0:1, 2, :], rows[0:1, 0, :],
                                      rows[0:1, 1, :]), "fv")
            e_.tensor_scalar(rows[0:1, 2, :], rows[0:1, 2, :], scale, None, ALU.mult)
    with nc.Block() as blk:
        @blk.tensor
        def _(e_):
            e_.matmul(pr[:, 0:128], onesf[:, :], rows[0:1, 2, :], start=True, stop=True)
    with nc.Block() as blk:
        @blk.vector
        def _(e_):
            e_.tensor_copy(w_b[:, :], pr[:, 0:128])


def sublayer_L1_attn_a(C, W, sc, r_src, r_dst):
    C.push_scope()
    su = C.uid()
    ln = C.sb(f"s5_ln{su}", [128, ND, C.T], BF16)
    w_b = C.sb(f"s5_wb{su}", [128, 128], F32)
    ph_layernorm(C, r_src, ln)
    ph_gamma_tile(C, W["L1aa_gq"], W["L1aa_gk"], w_b, INV_SQRT_DK)
    ph_proj_tm(C, ln, W["L1aa_wq"], sc["qt"])
    ph_proj_tm(C, ln, W["L1aa_wk"], sc["kt"])
    ph_proj_tm(C, ln, W["L1aa_wv"], sc["v"])
    C.pop_scope()
    C.push_scope()
    ph_rope_norm(C, sc["qt"], sc["q"], 16)
    C.pop_scope()
    C.push_scope()
    su = C.uid()
    w_b2 = C.sb(f"s5_wc{su}", [128, 128], F32)
    ph_gamma_tile(C, W["L1aa_gq"], W["L1aa_gk"], w_b2, INV_SQRT_DK)
    ph_rope_norm(C, sc["kt"], sc["k"], 4, w_b=w_b2)
    C.pop_scope()
    ph_attention4(C, sc["q"], sc["k"], sc["v"], sc["o"],
                  n_heads=16, n_kv=4, mask_kind="window", window_chunks=8)
    ph_out_proj_add(C, sc["o"], W["L1aa_wo"], None, r_src, r_dst)


@_scoped
def ph_router_l1(C, ln_sbuf, wr_hbm, crows, c_em_bf):
    nc, T = C.nc, C.T
    u = C.uid()
    wr = C.sb(f"r1_wr{u}", [128, ND, 8], BF16)
    wrs = C.sb(f"r1_ws{u}", [128, ND, 8], F32)
    lg = C.sb(f"r1_lg{u}", [128, 8], F32)
    ex = C.sb(f"r1_ex{u}", [128, 8], F32)
    sg = C.sb(f"r1_sg{u}", [128, 8], F32)
    msk = C.sb(f"r1_mk{u}", [128, 8], F32)
    cc = C.sb(f"r1_cc{u}", [128, 8], F32)
    st = C.sb(f"r1_st{u}", [128, 6], F32)
    cem = C.sb(f"r1_ce{u}", [8, T], F32)
    stg = C.sb(f"r1_tg{u}", [8, 128], F32)
    identf = C.sb(f"r1_id{u}", [128, 128], F32)
    pl = C.psum(f"r1_pl{u}", [128, 512], F32)
    pr = C.psum(f"r1_pr{u}", [128, 512], F32)

    idf_h = C.inline("c_identf", np.eye(128, dtype=np.float32))
    ld0 = C.base("ld")
    with nc.Block() as blk:
        @blk.sync
        def _(e_):
            e_.dma_start(wrs[:, :, :], chunked(wr_hbm)).then_inc(C.sem("ld"), 16)
            e_.dma_start(identf[:, :], idf_h[:, :]).then_inc(C.sem("ld"), 16)
        @blk.vector
        def _(e_):
            e_.wait_ge(C.sem("ld"), ld0 + 32)
    C.bump("ld", 32)
    with nc.Block() as blk:
        @blk.vector
        def _(e_):
            e_.tensor_copy(wr[:, :, :], wrs[:, :, :])

    for tt in range(C.NT):
        with nc.Block() as blk:
            @blk.tensor
            def _(e_):
                for c in range(ND):
                    e_.matmul(pl[:, 0:8], ln_sbuf[:, c, tt * 128:(tt + 1) * 128],
                              wr[:, c, :], start=(c == 0), stop=(c == ND - 1))
        with nc.Block() as blk:
            @blk.vector
            def _(e_):
                C.fence(e_, e_.reduce_max(st[:, 0:1], pl[:, 0:8], axis=AX.X,
                                          negate=True), "fv")
                C.fence(e_, e_.tensor_scalar(lg[:, :], pl[:, 0:8], st[:, 0:1], None,
                                             ALU.add), "fv")
        with nc.Block() as blk:
            @blk.scalar
            def _(e_):
                C.fence(e_, e_.activation(ex[:, :], lg[:, :], AF.Exp, bias=0.0,
                                          scale=1.0), "fs")
                e_.activation(sg[:, :], lg[:, :], AF.Sign, bias=0.0, scale=1.0)
        with nc.Block() as blk:
            @blk.vector
            def _(e_):
                C.fence(e_, e_.tensor_scalar(sg[:, :], sg[:, :], 1.0, 1.0,
                                             ALU.add, ALU.min), "fv")
                C.fence(e_, e_.tensor_sub(sg[:, :], ex[:, :], sg[:, :]), "fv")
                C.fence(e_, e_.reduce_max(st[:, 1:2], sg[:, :], axis=AX.X), "fv")
                C.fence(e_, e_.tensor_scalar(st[:, 2:3], st[:, 1:2], 1.0, None,
                                             ALU.add), "fv")
                C.fence(e_, e_.reciprocal(st[:, 3:4], st[:, 2:3]), "fv")
                C.fence(e_, e_.tensor_scalar(msk[:, :], ex[:, :], st[:, 1:2], None,
                                             ALU.subtract), "fv")
        with nc.Block() as blk:
            @blk.scalar
            def _(e_):
                e_.activation(msk[:, :], msk[:, :], AF.Sign, bias=0.0, scale=1.0)
        with nc.Block() as blk:
            @blk.vector
            def _(e_):
                C.fence(e_, e_.tensor_scalar(msk[:, :], msk[:, :], 1.0, 1.0,
                                             ALU.add, ALU.min), "fv")
                C.fence(e_, e_.tensor_mul(cc[:, :], ex[:, :], msk[:, :]), "fv")
                C.fence(e_, e_.tensor_scalar(cc[:, :], cc[:, :], st[:, 3:4], None,
                                             ALU.mult), "fv")
        with nc.Block() as blk:
            @blk.tensor
            def _(e_):
                e_.transpose(pr[0:8, 0:128], cc[:, :], identf[:, :])
        with nc.Block() as blk:
            @blk.vector
            def _(e_):
                e_.tensor_copy(cem[:, tt * 128:(tt + 1) * 128], pr[0:8, 0:128])
    st0 = C.base("st")
    with nc.Block() as blk:
        @blk.vector
        def _(e_):
            e_.tensor_copy(c_em_bf[:, :], cem[:, :])
    with nc.Block() as blk:
        @blk.sync
        def _(e_):
            e_.dma_start(crows[0:1, :, :], c_em_bf[:, :]).then_inc(C.sem("st"), 16)
        @blk.scalar
        def _(e_):
            e_.wait_ge(C.sem("st"), st0 + 16)
    C.bump("st", 16)


@_scoped
def ph_expert_up_add(C, h_hbm, wu_hbm, bu_hbm, bsu_hbm, c_em_bf, crows,
                     r_src, r_dst, n_blocks, n_routed_blocks):
    nc, T = C.nc, C.T
    u = C.uid()
    NIB = n_blocks
    n_th = 2 if T >= 2048 else 1
    hw = T // n_th
    nhb = hw // 512
    hq = C.sb(f"up_hq{u}", [128, NIB, hw], BF16)
    wstage = [C.sb(f"up_ws{u}_{i}", [128, NIB, 128], F32) for i in range(2)]
    wuc = [C.sb(f"up_wc{u}_{i}", [128, NIB, 128], BF16) for i in range(2)]
    bu_s = C.sb(f"up_bu{u}", [8, D], F32)
    bu_b = C.sb(f"up_bb{u}", [8, D], BF16)
    bsu_c = C.sb(f"up_bs{u}", [128, ND], F32)
    cb = C.sb(f"up_cb{u}", [128, hw], F32)
    onesf = C.sb(f"up_on{u}", [1, 128], BF16)
    yb = [C.sb(f"up_yb{u}_{i}", [128, 512], F32) for i in range(2)]
    rtile = [C.sb(f"up_rt{u}_{i}", [128, 512], F32) for i in range(2)]
    ob = [C.sb(f"up_ob{u}_{i}", [128, 512], F32) for i in range(2)]
    ps = C.psum(f"up_ps{u}", [128, 2048], F32)
    pc = C.psum(f"up_pc{u}", [128, 1024], F32)

    ld0 = C.base("ld")
    with nc.Block() as blk:
        @blk.sync
        def _(e_):
            e_.dma_start(bu_s[:, :], bu_hbm[:, :]).then_inc(C.sem("ld"), 16)
            e_.dma_start(bsu_c[:, 0:ND], bsu_hbm.rearrange("(c p) -> p c", p=128),
                         allow_slow_non_contiguous=True).then_inc(C.sem("ld"), 16)
        @blk.gpsimd
        def _(e_):
            e_.memset(onesf[:, :], 1.0)
        @blk.vector
        def _(e_):
            e_.wait_ge(C.sem("ld"), ld0 + 32)
    C.bump("ld", 32)
    with nc.Block() as blk:
        @blk.vector
        def _(e_):
            e_.tensor_copy(bu_b[:, :], bu_s[:, :])

    b_ld = C.base("ld")
    b_l1 = C.base("l1")
    b_mm = C.base("mm")
    b_ev = C.base("ev")
    b_ca = C.base("ca")
    b_x2 = C.base("x2")
    b_hl = C.base("hl")
    b_r = [C.base("r0"), C.base("r1")]
    b_s = [C.base("s0"), C.base("s1")]
    iu = 0
    nld = 0

    def wu_col(k):
        ec = k % ND
        return wu_hbm[:, ec * 128:(ec + 1) * 128].rearrange("(c p) e -> p c e", p=128)

    with nc.Block() as blk:
        @blk.sync
        def _(e_):
            e_.dma_start(wstage[0][:, :, :], wu_col(0)).then_inc(C.sem("ld"), 16)
    nld += 16
    nl1 = 0

    for th in range(n_th):
        t0 = th * hw
        with nc.Block() as blk:
            @blk.sync
            def _(e_):
                e_.dma_start(hq[:, :, :],
                             chunked(h_hbm[0:NIB * 128, t0:t0 + hw])).then_inc(
                                 C.sem("hl"), 16)
            @blk.vector
            def _(e_):
                e_.wait_ge(C.sem("hl"), b_hl + 16 * (th + 1))
        C.bump("hl", 16)
        for e in range(n_routed_blocks // 4):
            with nc.Block() as blk:
                @blk.tensor
                def _(e_):
                    for hb in range(nhb):
                        e_.matmul(pc[:, hb % 2 * 512:(hb % 2 + 1) * 512], onesf[:, :],
                                  crows[0:1, e, t0 + hb * 512:t0 + (hb + 1) * 512],
                                  start=True, stop=True)
                        assert nhb <= 2
            with nc.Block() as blk:
                @blk.vector
                def _(e_):
                    for hb in range(nhb):
                        C.fence(e_, e_.tensor_copy(cb[:, hb * 512:(hb + 1) * 512],
                                                   pc[:, hb % 2 * 512:(hb % 2 + 1) * 512]),
                                "fv")
                    for b4 in range(4):
                        C.fence(e_, e_.tensor_mul(hq[:, e * 4 + b4, :],
                                                  hq[:, e * 4 + b4, :],
                                                  cb[:, :]), "fv")
        for ec in range(ND):
            k = th * ND + ec
            b = k % 2
            with nc.Block() as blk:
                @blk.sync
                def _(e_):
                    if k + 1 < n_th * ND:
                        e_.dma_start(wstage[(k + 1) % 2][:, :, :],
                                     wu_col(k + 1)).then_inc(
                                         C.sem(["ld", "l1"][(k + 1) % 2]), 16)
                    for tb in range(nhb):
                        i = iu + tb
                        if i >= 2:
                            e_.wait_ge(C.sem("x2"), b_x2 + i - 1)
                        e_.dma_start(
                            rtile[i % 2][:, :],
                            r_src[ec * 128:(ec + 1) * 128,
                                  t0 + tb * 512:t0 + (tb + 1) * 512]).then_inc(
                                      C.sem(f"r{i % 2}"), 16)
                @blk.vector
                def _(e_):
                    e_.wait_ge(C.sem(["ld", "l1"][k % 2]),
                               [b_ld, b_l1][k % 2] + 16 * (k // 2 + 1))
                    e_.tensor_copy(wuc[b][:, :, :], wstage[b][:, :, :]).then_inc(
                        C.sem("ca"), 1)
                    for tb in range(nhb):
                        i = iu + tb
                        pb = i % 2
                        e_.wait_ge(C.sem("ev"), b_ev + i + 1)
                        e_.wait_ge(C.sem(f"r{pb}"), b_r[pb] + 16 * (i // 2 + 1))
                        if i >= 2:
                            e_.wait_ge(C.sem(f"s{pb}"), b_s[pb] + 16 * (i // 2))
                        e_.tensor_add(ob[pb][:, :], yb[pb][:, :], rtile[pb][:, :]
                                      ).then_inc(C.sem("x2"), 1)
                @blk.tensor
                def _(e_):
                    e_.wait_ge(C.sem("ca"), b_ca + k + 1)
                    for tb in range(nhb):
                        i = iu + tb
                        pb = i % 2
                        for ib in range(NIB):
                            e_.matmul(ps[:, pb * 512:(pb + 1) * 512],
                                      wuc[b][:, ib, :],
                                      hq[:, ib, tb * 512:(tb + 1) * 512],
                                      start=(ib == 0), stop=False)
                        e_.matmul(ps[:, pb * 512:(pb + 1) * 512],
                                  bu_b[:, ec * 128:(ec + 1) * 128],
                                  c_em_bf[:, t0 + tb * 512:t0 + (tb + 1) * 512],
                                  start=False, stop=True).then_inc(C.sem("mm"), 1)
                @blk.scalar
                def _(e_):
                    for tb in range(nhb):
                        i = iu + tb
                        pb = i % 2
                        e_.wait_ge(C.sem("mm"), b_mm + i + 1)
                        e_.activation(yb[pb][:, :], ps[:, pb * 512:(pb + 1) * 512],
                                      AF.Identity, bias=bsu_c[:, ec:ec + 1],
                                      scale=1.0).then_inc(C.sem("ev"), 1)
                        e_.wait_ge(C.sem("x2"), b_x2 + i + 1)
                        e_.dma_start(
                            r_dst[ec * 128:(ec + 1) * 128,
                                  t0 + tb * 512:t0 + (tb + 1) * 512],
                            ob[pb][:, :]).then_inc(C.sem(f"s{pb}"), 16)
            iu += nhb
            C.bump("mm", nhb)
            C.bump("ev", nhb)
            C.bump("ca", 1)
            C.bump("x2", nhb)
            if k + 1 < n_th * ND:
                if (k + 1) % 2 == 0:
                    nld += 16
                else:
                    nl1 += 16
    C.bump("ld", nld)
    C.bump("l1", nl1)
    ntot = iu
    C.bump("r0", 16 * ceil_div(ntot, 2))
    C.bump("r1", 16 * (ntot // 2))
    C.bump("s0", 16 * ceil_div(ntot, 2))
    C.bump("s1", 16 * (ntot // 2))
    e0, e1 = C.base("s0"), C.base("s1")
    with nc.Block() as blk:
        @blk.sync
        def _(e_):
            e_.wait_ge(C.sem("s0"), e0)
            e_.wait_ge(C.sem("s1"), e1)


def sublayer_L1_ffn(C, W, sc, r_src, r_dst):
    C.push_scope()
    su = C.uid()
    crows = C.sb(f"s6_cr{su}", [1, 8, C.T], BF16)
    c_em_bf = C.sb(f"s6_ce{su}", [8, C.T], BF16)
    C.push_scope()
    ln = C.sb(f"s6_ln{su}", [128, ND, C.T], BF16)
    ph_layernorm(C, r_src, ln)
    ph_router_l1(C, ln, W["L1f_wr"], crows, c_em_bf)
    ph_proj(C, ln, W["L1f_wd"], sc["h"], bias_hbm=W["L1f_bd"], func=AF.Gelu)
    C.pop_scope()
    ph_expert_up_add(C, sc["h"], W["L1f_wu"], W["L1f_bu"], W["L1f_bsu"],
                     c_em_bf, crows, r_src, r_dst, 36, 32)
    C.pop_scope()


@_scoped
def ph_phi(C, src_hbm, dst_hbm, *, scale=1.0):
    nc, T = C.nc, C.T
    u = C.uid()
    E = src_hbm.shape[0]
    NE = E // 128
    xin = [C.sb(f"ph_x{u}_{i}", [128, 512], BF16) for i in range(2)]
    m1 = C.sb(f"ph_m{u}", [128, 512], BF16)
    e1 = C.sb(f"ph_e{u}", [128, 512], BF16)
    r1 = C.sb(f"ph_r{u}", [128, 512], BF16)
    out = [C.sb(f"ph_o{u}_{i}", [128, 512], BF16) for i in range(2)]
    lns = C.constcol(math.log(scale))
    b_l = [C.base("l0"), C.base("l1")]
    b_x1 = C.base("x1")
    b_ev = C.base("ev")
    b_s = [C.base("s0"), C.base("s1")]
    n = NE * C.NB
    with nc.Block() as blk:
        @blk.sync
        def _(e_):
            for i in range(n):
                ec, tb = divmod(i, C.NB)
                if i >= 2:
                    e_.wait_ge(C.sem("x1"), b_x1 + 2 * (i - 1))
                e_.dma_start(xin[i % 2][:, :],
                             src_hbm[ec * 128:(ec + 1) * 128, tb * 512:(tb + 1) * 512]
                             ).then_inc(C.sem(["l0", "l1"][i % 2]), 16)
        @blk.vector
        def _(e_):
            for i in range(n):
                p = i % 2
                e_.wait_ge(C.sem(["l0", "l1"][p]), b_l[p] + 16 * (i // 2 + 1))
                e_.tensor_scalar(m1[:, :], xin[p][:, :], 0.0, None, ALU.min
                                 ).then_inc(C.sem("x1"), 1)
                e_.wait_ge(C.sem("ev"), b_ev + 2 * i + 2)
                if i >= 2:
                    e_.wait_ge(C.sem(f"s{p}"), b_s[p] + 16 * (i // 2))
                e_.tensor_add(out[p][:, :], e1[:, :], r1[:, :]).then_inc(C.sem("x1"), 1)
        @blk.scalar
        def _(e_):
            for i in range(n):
                ec, tb = divmod(i, C.NB)
                p = i % 2
                e_.wait_ge(C.sem("x1"), b_x1 + 2 * i + 1)
                C.fence(e_, e_.activation(e1[:, :], m1[:, :], AF.Exp,
                                          bias=lns[0:1, :] if False else lns,
                                          scale=1.0), "fs")
                e_.activation(r1[:, :], xin[p][:, :], AF.Relu, bias=0.0,
                              scale=scale).then_inc(C.sem("ev"), 2)
                e_.wait_ge(C.sem("x1"), b_x1 + 2 * i + 2)
                e_.dma_start(dst_hbm[ec * 128:(ec + 1) * 128, tb * 512:(tb + 1) * 512],
                             out[p][:, :]).then_inc(C.sem(f"s{p}"), 16)
    C.bump("l0", 16 * ceil_div(n, 2))
    C.bump("l1", 16 * (n // 2))
    C.bump("x1", 2 * n)
    C.bump("ev", 2 * n)
    C.bump("s0", 16 * ceil_div(n, 2))
    C.bump("s1", 16 * (n // 2))
    e0, e1v = C.base("s0"), C.base("s1")
    with nc.Block() as blk:
        @blk.sync
        def _(e_):
            e_.wait_ge(C.sem("s0"), e0)
            e_.wait_ge(C.sem("s1"), e1v)


@_scoped
def ph_pkv_scan(C, phik_hbm, v_hbm, phiq_hbm, ctx_hbm):
    nc, T = C.nc, C.T
    u = C.uid()
    ka = [C.sb(f"sc_k{u}_{i}", [128, T], BF16) for i in range(2)]
    va = [C.sb(f"sc_v{u}_{i}", [128, T], BF16) for i in range(2)]
    qa = [C.sb(f"sc_q{u}_{i}", [128, T], BF16) for i in range(2)]
    pkv = C.sb(f"sc_p{u}", [128, T], F32)
    kv = C.sb(f"sc_c{u}", [128, T], F32)
    ctx = [C.sb(f"sc_o{u}_{i}", [128, T], BF16) for i in range(2)]
    b_l = [C.base("l0"), C.base("l1")]
    b_x1 = C.base("x1")
    b_s = [C.base("s0"), C.base("s1")]
    with nc.Block() as blk:
        @blk.sync
        def _(e_):
            for c in range(ND):
                p = c % 2
                if c >= 2:
                    e_.wait_ge(C.sem("x1"), b_x1 + 4 * (c - 1))
                e_.dma_start(ka[p][:, :], phik_hbm[c * 128:(c + 1) * 128, :]
                             ).then_inc(C.sem(["l0", "l1"][p]), 16)
                e_.dma_start(va[p][:, :], v_hbm[c * 128:(c + 1) * 128, :]
                             ).then_inc(C.sem(["l0", "l1"][p]), 16)
                e_.dma_start(qa[p][:, :], phiq_hbm[c * 128:(c + 1) * 128, :]
                             ).then_inc(C.sem(["l0", "l1"][p]), 16)
        @blk.vector
        def _(e_):
            for c in range(ND):
                p = c % 2
                e_.wait_ge(C.sem(["l0", "l1"][p]), b_l[p] + 48 * (c // 2 + 1))
                C.fence(e_, e_.tensor_mul(pkv[:, :], ka[p][:, :], va[p][:, :]), "fv")
                C.fence(e_, e_.tensor_tensor_scan(kv[:, :], pkv[:, :], pkv[:, :],
                                                  0.0, ALU.add, ALU.bypass), "fv")
                if c >= 2:
                    e_.wait_ge(C.sem(f"s{p}"), b_s[p] + 16 * (c // 2))
                e_.tensor_mul(ctx[p][:, :], qa[p][:, :], kv[:, :]).then_inc(
                    C.sem("x1"), 4)
        @blk.scalar
        def _(e_):
            for c in range(ND):
                p = c % 2
                e_.wait_ge(C.sem("x1"), b_x1 + 4 * c + 4)
                e_.dma_start(ctx_hbm[c * 128:(c + 1) * 128, :], ctx[p][:, :]
                             ).then_inc(C.sem(f"s{p}"), 16)
    C.bump("l0", 48 * ceil_div(ND, 2))
    C.bump("l1", 48 * (ND // 2))
    C.bump("x1", 4 * ND)
    C.bump("s0", 16 * ceil_div(ND, 2))
    C.bump("s1", 16 * (ND // 2))
    e0, e1 = C.base("s0"), C.base("s1")
    with nc.Block() as blk:
        @blk.sync
        def _(e_):
            e_.wait_ge(C.sem("s0"), e0)
            e_.wait_ge(C.sem("s1"), e1)


def sublayer_L1_attn_b(C, W, sc, r_src, r_dst):
    C.push_scope()
    su = C.uid()
    ln = C.sb(f"s7_ln{su}", [128, ND, C.T], BF16)
    ph_layernorm(C, r_src, ln)
    ph_proj(C, ln, W["L1ab_wq"], sc["q"], bias_hbm=W["L1ab_bq"])
    ph_proj(C, ln, W["L1ab_wk"], sc["k"], bias_hbm=W["L1ab_bk"])
    ph_proj(C, ln, W["L1ab_wv"], sc["g3"], bias_hbm=W["L1ab_bv"])
    C.pop_scope()
    ph_phi(C, sc["q"], sc["g1"], scale=INV_SQRT_DK)
    ph_phi(C, sc["k"], sc["g2"])
    ph_pkv_scan(C, sc["g2"], sc["g3"], sc["g1"], sc["o"])
    ph_out_proj_add(C, sc["o"], W["L1ab_wo"], W["L1ab_bo"], r_src, r_dst)



@_scoped
def ph_attention2(C, q_hbm, k_hbm, v_hbm, o_hbm, *, n_heads=16, n_kv=16,
                  mask_kind="causal", window_chunks=8):
    nc, T = C.nc, C.T
    NT = C.NT
    u = C.uid()
    EK = n_kv * 128
    GR = n_heads // n_kv

    ksb = C.sb(f"a2_k{u}", [128, n_kv, T], BF16)
    vsb = C.sb(f"a2_v{u}", [128, NT, EK], BF16)
    qsb = [C.sb(f"a2_q{u}_{i}", [128, n_heads, 512], BF16) for i in range(2)]
    pt = C.sb(f"a2_ptb{u}", [128, NT, 512], BF16)
    osb = [C.sb(f"a2_o{u}_{i}", [128, 512], BF16) for i in range(2)]
    mcol = C.sb(f"a2_mc{u}", [128, 4], F32)
    tmpm = C.sb(f"a2_tm{u}", [128, 1], F32)
    nmb_r = C.sb(f"a2_nb{u}", [1, 2], F32)
    mbk = C.sb(f"a2_mb{u}", [128, 1], F32)
    frow = C.sb(f"a2_fr{u}", [1, 512], F32)
    f_b = C.sb(f"a2_fb{u}", [128, 512], F32)
    onesc = C.sb(f"a2_on{u}", [128, 1], BF16)
    onesf = C.sb(f"a2_of{u}", [1, 128], F32)
    identf = C.sb(f"a2_if{u}", [128, 128], F32)
    zdiag = C.sb(f"a2_zd{u}", [128, 128], BF16)
    zwin = C.sb(f"a2_zw{u}", [128, 128], BF16)
    ediag = C.sb(f"a2_ed{u}", [128, 128], F32)

    pS = C.psum(f"a2_ps{u}", [128, 2048], F32)
    pST = C.psum(f"a2_pq{u}", [128, 1024], F32)
    pR = C.psum(f"a2_pr{u}", [128, 512], F32)
    pO = C.psum(f"a2_po{u}", [128, 512], F32)

    E_CONST = float(np.exp(np.float32(1.0)))
    idf_h = C.inline("c_identf", np.eye(128, dtype=np.float32))
    kq = np.arange(128)
    zdiag_h = C.inline("c_zdiag", (kq[:, None] <= kq[None, :]).astype(ml_dtypes.bfloat16))
    zwin_h = C.inline("c_zwin", (kq[:, None] >= kq[None, :]).astype(ml_dtypes.bfloat16))
    ediag_h = C.inline("c_ediag", np.where(kq[:, None] <= kq[None, :],
                                           np.float32(E_CONST),
                                           np.float32(1.0)).astype(np.float32))

    ld0 = C.base("ld")
    with nc.Block() as blk:
        @blk.sync
        def _(e_):
            e_.dma_start(ksb[:, :, :], chunked(k_hbm[0:EK, :])).then_inc(C.sem("ld"), 16)
            e_.dma_start(vsb[:, :, :],
                         v_hbm[:, 0:EK].rearrange("(c p) t -> p c t", p=128)
                         ).then_inc(C.sem("ld"), 16)
            e_.dma_start(identf[:, :], idf_h[:, :]).then_inc(C.sem("ld"), 16)
            e_.dma_start(zdiag[:, :], zdiag_h[:, :]).then_inc(C.sem("ld"), 16)
            e_.dma_start(zwin[:, :], zwin_h[:, :]).then_inc(C.sem("ld"), 16)
            e_.dma_start(ediag[:, :], ediag_h[:, :]).then_inc(C.sem("ld"), 16)
        @blk.gpsimd
        def _(e_):
            e_.memset(onesc[:, :], 1.0)
            e_.memset(onesf[:, :], 1.0)
        @blk.vector
        def _(e_):
            e_.wait_ge(C.sem("ld"), ld0 + 16 * 6)
        @blk.tensor
        def _(e_):
            e_.wait_ge(C.sem("ld"), ld0 + 16 * 6)
        @blk.scalar
        def _(e_):
            e_.wait_ge(C.sem("ld"), ld0 + 16 * 6)
    C.bump("ld", 16 * 6)

    units = [(qb, h) for qb in range(C.NB) for h in range(n_heads)]
    nu = len(units)

    def qt_kparams(qt_abs):
        k_lo = max(0, qt_abs - window_chunks) if mask_kind == "window" else 0
        k_hi = NT if mask_kind == "trilbias" else qt_abs + 1
        return k_lo, k_hi

    def blk_kparams(qb):
        hi = NT if mask_kind == "trilbias" else 4 * (qb + 1)
        lo = max(0, 4 * qb - window_chunks) if mask_kind == "window" else 0
        return lo, hi


    def tile_mask_ops(qb, kc):
        ops = []
        qt_above = [qt for qt in range(4) if kc > qb * 4 + qt]
        qt_below = [qt for qt in range(4) if kc < qb * 4 + qt]
        if mask_kind in ("causal", "window"):
            if qt_above:
                ops.append(("zs", min(qt_above) * 128, (max(qt_above) + 1) * 128))
            if kc - qb * 4 in range(4):
                ops.append(("tt", zdiag, (kc - qb * 4) * 128))
            if mask_kind == "window":
                far = [qt for qt in range(4) if kc < qb * 4 + qt - window_chunks]
                if far:
                    ops.append(("zs", min(far) * 128, (max(far) + 1) * 128))
                if (kc + window_chunks - qb * 4) in range(4):
                    ops.append(("tt", zwin, (kc + window_chunks - qb * 4) * 128))
        else:
            if qt_below:
                ops.append(("es", min(qt_below) * 128, (max(qt_below) + 1) * 128))
            if kc - qb * 4 in range(4):
                ops.append(("tt", ediag, (kc - qb * 4) * 128))
        return ops

    sgroups = []
    nkbs = []
    pools = []
    for (qb, h) in units:
        lo, hi = blk_kparams(qb)
        nkbs.append(hi - lo)
        gs = []
        for qt in range(4):
            qt_abs = qb * 4 + qt
            klo, khi = qt_kparams(qt_abs)
            for g in range(klo, khi, 4):
                gs.append((qt, g, min(4, khi - g)))
        sgroups.append(gs)
        pu = [len(tile_mask_ops(qb, lo + j)) for j in range(hi - lo)]
        pools.append(pu)

    mm_cum = [0]
    x1_cum = [0]
    x2_cum = [0]
    ev_cum = [0]
    pl_cum = [0]
    for ui in range(nu):
        mm_cum.append(mm_cum[-1] + len(sgroups[ui]) + nkbs[ui] + 5)
        x1_cum.append(x1_cum[-1] + 6)
        x2_cum.append(x2_cum[-1] + len(sgroups[ui]))
        ev_cum.append(ev_cum[-1] + nkbs[ui])
        pl_cum.append(pl_cum[-1] + sum(1 for p in pools[ui] if p > 0))

    b_mm = C.base("mm")
    b_x1 = C.base("x1")
    b_x2 = C.base("x2")
    b_ev = C.base("ev")
    b_pl = C.base("hl")
    b_q = [C.base("q0"), C.base("q1")]
    b_s = [C.base("s0"), C.base("s1")]

    with nc.Block() as blk:
        @blk.sync
        def _(e_):
            for qb in range(C.NB):
                if qb >= 2:
                    done = len([x for x in units if x[0] <= qb - 2])
                    e_.wait_ge(C.sem("mm"), b_mm + mm_cum[done])
                e_.dma_start(qsb[qb % 2][:, :, :],
                             chunked(q_hbm[:, qb * 512:(qb + 1) * 512])).then_inc(
                                 C.sem(f"q{qb % 2}"), 16)

        @blk.tensor
        def _(e_):
            gidx = 0
            for ui, (qb, h) in enumerate(units):
                lo, hi = blk_kparams(qb)
                nkb = hi - lo
                kv = h // GR
                MM0 = b_mm + mm_cum[ui]
                X10 = b_x1 + x1_cum[ui]
                EV0 = b_ev + ev_cum[ui]
                nS = len(sgroups[ui])
                e_.wait_ge(C.sem(f"q{qb % 2}"), b_q[qb % 2] + 16 * (qb // 2 + 1))
                for gl, (qt, g, kw) in enumerate(sgroups[ui]):
                    if gidx >= 4:
                        e_.wait_ge(C.sem("x2"), b_x2 + gidx - 3)
                    bank = gidx % 4
                    e_.matmul(pS[:, bank * 512:bank * 512 + kw * 128],
                              qsb[qb % 2][:, h, qt * 128:(qt + 1) * 128],
                              ksb[:, kv, g * 128:(g + kw) * 128],
                              start=True, stop=True).then_inc(C.sem("mm"), 1)
                    gidx += 1
                e_.wait_ge(C.sem("x1"), X10 + 1)
                e_.transpose(pR[0:1, 0:128], tmpm[:, 0:1], identf[:, :]
                             ).then_inc(C.sem("mm"), 1)
                e_.wait_ge(C.sem("x1"), X10 + 2)
                e_.matmul(pO[:, 0:1], onesf[:, :], nmb_r[0:1, 0:1],
                          start=True, stop=True).then_inc(C.sem("mm"), 1)
                for j in range(nkb):
                    pb = j % 2
                    if j == 0:
                        if ui >= 1:
                            pass
                    if j >= 2:
                        e_.wait_ge(C.sem("ev"), EV0 + j - 1)
                    e_.matmul(pST[:, pb * 512:(pb + 1) * 512],
                              ksb[:, kv, (lo + j) * 128:(lo + j + 1) * 128],
                              qsb[qb % 2][:, h, 0:512],
                              start=True, stop=True).then_inc(C.sem("mm"), 1)
                e_.wait_ge(C.sem("ev"), EV0 + nkb)
                e_.wait_ge(C.sem("hl"), b_pl + pl_cum[ui + 1])
                ins = None
                for j in range(nkb):
                    ins = e_.matmul(pR[0:1, 0:512], onesc[:, :], pt[:, j, :],
                                    start=(j == 0), stop=(j == nkb - 1))
                ins.then_inc(C.sem("mm"), 1)
                e_.wait_ge(C.sem("x1"), X10 + 4)
                e_.matmul(pR[:, 0:512], onesf[:, :], frow[0:1, 0:512],
                          start=True, stop=True).then_inc(C.sem("mm"), 1)
                e_.wait_ge(C.sem("x1"), X10)
                ins = None
                for j in range(nkb):
                    ins = e_.matmul(pO[:, 0:512],
                                    vsb[:, lo + j, kv * 128:(kv + 1) * 128],
                                    pt[:, j, :],
                                    start=(j == 0), stop=(j == nkb - 1))
                ins.then_inc(C.sem("mm"), 1)

        @blk.vector
        def _(e_):
            gidx = 0
            for ui, (qb, h) in enumerate(units):
                lo, hi = blk_kparams(qb)
                nkb = hi - lo
                MM0 = b_mm + mm_cum[ui]
                X10 = b_x1 + x1_cum[ui]
                nS = len(sgroups[ui])
                for gl, (qt, g, kw) in enumerate(sgroups[ui]):
                    bank = gidx % 4
                    e_.wait_ge(C.sem("mm"), MM0 + gl + 1)
                    if gl < 4:
                        pass
                    first_of_qt = (gl == 0) or (sgroups[ui][gl - 1][0] != qt)
                    if first_of_qt:
                        C.fence(e_, e_.reduce_max(
                            mcol[:, qt:qt + 1], pS[:, bank * 512:bank * 512 + kw * 128],
                            axis=AX.X, negate=True), "fv")
                    else:
                        C.fence(e_, e_.reduce_max(
                            tmpm[:, 0:1], pS[:, bank * 512:bank * 512 + kw * 128],
                            axis=AX.X, negate=True), "fv")
                        C.fence(e_, e_.tensor_tensor(
                            mcol[:, qt:qt + 1], mcol[:, qt:qt + 1], tmpm[:, 0:1],
                            ALU.min), "fv")
                    nc.vector.engine_nop().then_inc(C.sem("x2"), 1)
                    gidx += 1
                C.fence(e_, e_.tensor_reduce(tmpm[:, 0:1], mcol[:, 0:4], AX.X,
                                             ALU.min), "fv")
                nc.vector.engine_nop().then_inc(C.sem("x1"), 1)
                e_.wait_ge(C.sem("mm"), MM0 + nS + 1)
                C.fence(e_, e_.tensor_reduce(nmb_r[0:1, 0:1], pR[0:1, 0:128], AX.X,
                                             ALU.min), "fv")
                nc.vector.engine_nop().then_inc(C.sem("x1"), 1)
                e_.wait_ge(C.sem("mm"), MM0 + nS + 2)
                e_.tensor_copy(mbk[:, 0:1], pO[:, 0:1]).then_inc(C.sem("x1"), 1)
                e_.wait_ge(C.sem("mm"), MM0 + nS + 2 + nkb + 1)
                C.fence(e_, e_.reciprocal(frow[0:1, 0:512], pR[0:1, 0:512]), "fv")
                nc.vector.engine_nop().then_inc(C.sem("x1"), 1)
                e_.wait_ge(C.sem("mm"), MM0 + nS + 2 + nkb + 2)
                e_.tensor_copy(f_b[:, :], pR[:, 0:512]).then_inc(C.sem("x1"), 1)
                e_.wait_ge(C.sem("mm"), MM0 + nS + 2 + nkb + 3)
                e_.wait_ge(C.sem("x1"), X10 + 5)
                if ui >= 2:
                    e_.wait_ge(C.sem(f"s{ui % 2}"), b_s[ui % 2] + 16 * (ui // 2))
                e_.tensor_tensor(osb[ui % 2][:, :], pO[:, 0:512], f_b[:, :],
                                 ALU.mult).then_inc(C.sem("x1"), 1)

        @blk.scalar
        def _(e_):
            for ui, (qb, h) in enumerate(units):
                lo, hi = blk_kparams(qb)
                nkb = hi - lo
                MM0 = b_mm + mm_cum[ui]
                X10 = b_x1 + x1_cum[ui]
                nS = len(sgroups[ui])
                for j in range(nkb):
                    pb = j % 2
                    e_.wait_ge(C.sem("mm"), MM0 + nS + 2 + j + 1)
                    if j == 0:
                        e_.wait_ge(C.sem("x1"), X10 + 3)
                        if ui >= 1:
                            e_.wait_ge(C.sem("mm"), MM0)
                    e_.activation(pt[:, j, :], pST[:, pb * 512:(pb + 1) * 512],
                                  AF.Exp, bias=mbk[:, 0:1], scale=1.0).then_inc(
                                      C.sem("ev"), 1)
                e_.wait_ge(C.sem("x1"), X10 + 6)
                e_.dma_start(o_hbm[h * 128:(h + 1) * 128, qb * 512:(qb + 1) * 512],
                             osb[ui % 2][:, :]).then_inc(C.sem(f"s{ui % 2}"), 16)

        @blk.gpsimd
        def _(e_):
            for ui, (qb, h) in enumerate(units):
                lo, hi = blk_kparams(qb)
                nkb = hi - lo
                EV0 = b_ev + ev_cum[ui]
                for j in range(nkb):
                    kc = lo + j
                    ops = tile_mask_ops(qb, kc)
                    if not ops:
                        continue
                    e_.wait_ge(C.sem("ev"), EV0 + j + 1)
                    ins = None
                    for op in ops:
                        if op[0] == "zs":
                            ins = e_.memset(pt[:, j, op[1]:op[2]], 0.0)
                        elif op[0] == "es":
                            ins = e_.tensor_scalar(pt[:, j, op[1]:op[2]],
                                                   pt[:, j, op[1]:op[2]],
                                                   E_CONST, None, ALU.mult)
                        else:
                            col = op[2]
                            ins = e_.tensor_mul(pt[:, j, col:col + 128],
                                                pt[:, j, col:col + 128], op[1][:, :])
                        C.fence(e_, ins, "fp")
                    nc.gpsimd.engine_nop().then_inc(C.sem("hl"), 1)

    C.bump("q0", 16 * ceil_div(C.NB, 2))
    C.bump("q1", 16 * (C.NB // 2))
    C.bump("mm", mm_cum[-1])
    C.bump("x1", x1_cum[-1])
    C.bump("x2", x2_cum[-1])
    C.bump("ev", ev_cum[-1])
    C.bump("hl", pl_cum[-1])
    C.bump("s0", 16 * ceil_div(nu, 2))
    C.bump("s1", 16 * (nu // 2))
    e0, e1 = C.base("s0"), C.base("s1")
    with nc.Block() as blk:
        @blk.sync
        def _(e_):
            e_.wait_ge(C.sem("s0"), e0)
            e_.wait_ge(C.sem("s1"), e1)


@_scoped
def ph_attention3(C, q_hbm, k_hbm, v_hbm, o_hbm, *, n_heads=16, n_kv=16,
                  mask_kind="causal", window_chunks=8):
    nc, T = C.nc, C.T
    NT = C.NT
    u = C.uid()
    EK = n_kv * 128
    GR = n_heads // n_kv

    ksb = C.sb(f"a3_k{u}", [128, n_kv, T], BF16)
    vsb = C.sb(f"a3_v{u}", [128, NT, EK], BF16)
    qsb = [C.sb(f"a3_q{u}_{i}", [128, n_heads, 512], BF16) for i in range(2)]
    pt = C.sb(f"a3_ptb{u}", [128, NT, 512], BF16)
    osb = [C.sb(f"a3_o{u}_{i}", [128, 512], BF16) for i in range(2)]
    mst = C.sb(f"a3_ms{u}", [128, 16], F32)
    tmpm = C.sb(f"a3_tm{u}", [128, 1], F32)
    nmb_r = C.sb(f"a3_nb{u}", [1, 2], F32)
    mbk = C.sb(f"a3_mb{u}", [128, 1], F32)
    frow = C.sb(f"a3_fr{u}", [1, 512], F32)
    f_b = C.sb(f"a3_fb{u}", [128, 512], F32)
    onesc = C.sb(f"a3_on{u}", [128, 1], BF16)
    onesf = C.sb(f"a3_of{u}", [1, 128], F32)
    identf = C.sb(f"a3_if{u}", [128, 128], F32)
    zdiag = C.sb(f"a3_zd{u}", [128, 128], BF16)
    zwin = C.sb(f"a3_zw{u}", [128, 128], BF16)
    ediag = C.sb(f"a3_ed{u}", [128, 128], F32)

    pS = C.psum(f"a3_ps{u}", [128, 2048], F32)
    pST = C.psum(f"a3_pq{u}", [128, 1024], F32)
    pR = C.psum(f"a3_pr{u}", [128, 512], F32)
    pO = C.psum(f"a3_po{u}", [128, 512], F32)

    E_CONST = float(np.exp(np.float32(1.0)))
    idf_h = C.inline("c_identf", np.eye(128, dtype=np.float32))
    kq = np.arange(128)
    zdiag_h = C.inline("c_zdiag", (kq[:, None] <= kq[None, :]).astype(ml_dtypes.bfloat16))
    zwin_h = C.inline("c_zwin", (kq[:, None] >= kq[None, :]).astype(ml_dtypes.bfloat16))
    ediag_h = C.inline("c_ediagf", np.where(kq[:, None] <= kq[None, :],
                                            np.float32(E_CONST),
                                            np.float32(1.0)).astype(np.float32))

    ld0 = C.base("ld")
    with nc.Block() as blk:
        @blk.sync
        def _(e_):
            e_.dma_start(ksb[:, :, :], chunked(k_hbm[0:EK, :])).then_inc(C.sem("ld"), 16)
            e_.dma_start(vsb[:, :, :],
                         v_hbm[:, 0:EK].rearrange("(c p) t -> p c t", p=128)
                         ).then_inc(C.sem("ld"), 16)
            e_.dma_start(identf[:, :], idf_h[:, :]).then_inc(C.sem("ld"), 16)
            e_.dma_start(zdiag[:, :], zdiag_h[:, :]).then_inc(C.sem("ld"), 16)
            e_.dma_start(zwin[:, :], zwin_h[:, :]).then_inc(C.sem("ld"), 16)
            e_.dma_start(ediag[:, :], ediag_h[:, :]).then_inc(C.sem("ld"), 16)
        @blk.gpsimd
        def _(e_):
            e_.memset(onesc[:, :], 1.0)
            e_.memset(onesf[:, :], 1.0)
        @blk.vector
        def _(e_):
            e_.wait_ge(C.sem("ld"), ld0 + 16 * 6)
        @blk.tensor
        def _(e_):
            e_.wait_ge(C.sem("ld"), ld0 + 16 * 6)
        @blk.scalar
        def _(e_):
            e_.wait_ge(C.sem("ld"), ld0 + 16 * 6)
    C.bump("ld", 16 * 6)

    units = [(qb, h) for qb in range(C.NB) for h in range(n_heads)]
    nu = len(units)

    def qt_kparams(qt_abs):
        k_lo = max(0, qt_abs - window_chunks) if mask_kind == "window" else 0
        k_hi = NT if mask_kind == "trilbias" else qt_abs + 1
        return k_lo, k_hi

    def blk_kparams(qb):
        hi = NT if mask_kind == "trilbias" else 4 * (qb + 1)
        lo = max(0, 4 * qb - window_chunks) if mask_kind == "window" else 0
        return lo, hi

    def valid_qslice(qb, kc):
        if mask_kind == "trilbias":
            return 0, 512
        qts = [qt for qt in range(4)
               if (qb * 4 + qt >= kc) and
               (mask_kind != "window" or kc >= qb * 4 + qt - window_chunks)]
        return min(qts) * 128, (max(qts) + 1) * 128

    def dve_patterns(qb, kc):
        ops = []
        if kc - qb * 4 in range(4):
            qt = kc - qb * 4
            ops.append((ediag if mask_kind == "trilbias" else zdiag, qt * 128))
        if mask_kind == "window" and (kc + window_chunks - qb * 4) in range(4):
            ops.append((zwin, (kc + window_chunks - qb * 4) * 128))
        if mask_kind == "trilbias":
            qt_below = [qt for qt in range(4) if kc < qb * 4 + qt]
            if qt_below:
                ops.append(("escale", min(qt_below) * 128, (max(qt_below) + 1) * 128))
        else:
            qt_above = [qt for qt in range(4) if kc > qb * 4 + qt]
            if qt_above:
                ops.append(("zero", min(qt_above) * 128, (max(qt_above) + 1) * 128))
            if mask_kind == "window":
                far = [qt for qt in range(4) if kc < qb * 4 + qt - window_chunks]
                if far:
                    ops.append(("zero", min(far) * 128, (max(far) + 1) * 128))
        return ops

    sgroups = []
    nkbs = []
    los = []
    for (qb, h) in units:
        lo, hi = blk_kparams(qb)
        nkbs.append(hi - lo)
        los.append(lo)
        gs = []
        for qt in range(4):
            qt_abs = qb * 4 + qt
            klo, khi = qt_kparams(qt_abs)
            for g in range(klo, khi, 4):
                gs.append((qt, g, min(4, khi - g)))
        sgroups.append(gs)

    tab = {}

    def emit(dry):
        cnt = {"mm": 0, "x1": 0, "x2": 0, "ev": 0, "dp": 0}

        def inc(ins, role, key=None):
            cnt[role] += 1
            if dry:
                if key is not None:
                    tab[key] = cnt[role]
            else:
                ins.then_inc(C.sem(role), 1)
            return cnt[role]

        def wv(e_, role, key, base):
            if not dry:
                e_.wait_ge(C.sem(role), base[role] + tab[key])

        return cnt, inc, wv

    b = {r: C.base(r) for r in ("mm", "x1", "x2", "ev", "dp")}
    b_q = [C.base("q0"), C.base("q1")]
    b_s = [C.base("s0"), C.base("s1")]

    def run_pe(e_, dry):
        cnt, inc, wvf = emit(dry)
        gidx = 0
        for it in range(nu + 1):
            if it < nu:
                ui = it
                qb, h = units[ui]
                kv = h // GR
                if not dry:
                    e_.wait_ge(C.sem(f"q{qb % 2}"), b_q[qb % 2] + 16 * (qb // 2 + 1))
                for gl, (qt, g, kw) in enumerate(sgroups[ui]):
                    if not dry and gidx >= 4:
                        e_.wait_ge(C.sem("x2"), b["x2"] + gidx - 3)
                    bank = gidx % 4
                    ins = None if dry else e_.matmul(
                        pS[:, bank * 512:bank * 512 + kw * 128],
                        qsb[qb % 2][:, h, qt * 128:(qt + 1) * 128],
                        ksb[:, kv, g * 128:(g + kw) * 128],
                        start=True, stop=True)
                    inc(ins, "mm", ("smm", ui, gl))
                    gidx += 1
            if it >= 1:
                ui = it - 1
                qb, h = units[ui]
                lo, hi = blk_kparams(qb)
                nkb = hi - lo
                kv = h // GR
                if not dry:
                    e_.wait_ge(C.sem("x1"), b["x1"] + tab[("colmin", ui)])
                    if ui >= 1:
                        e_.wait_ge(C.sem("x1"), b["x1"] + tab[("fbev", ui - 1)])
                ins = None if dry else e_.transpose(pR[0:1, 0:128], tmpm[:, 0:1],
                                                    identf[:, :])
                inc(ins, "mm", ("tpose", ui))
                if not dry:
                    e_.wait_ge(C.sem("x1"), b["x1"] + tab[("rowmin", ui)])
                ins = None if dry else e_.matmul(pO[:, 0:1], onesf[:, :],
                                                 nmb_r[0:1, 0:1], start=True, stop=True)
                inc(ins, "mm", ("mbbc", ui))
                for j in range(nkb):
                    pb = j % 2
                    if not dry and j < 2 and ui >= 1:
                        nkp = nkbs[ui - 1]
                        e_.wait_ge(C.sem("ev"), b["ev"] + tab[("exp", ui - 1, nkp - 1)])
                    if not dry and j >= 2:
                        e_.wait_ge(C.sem("ev"), b["ev"] + tab[("exp", ui, j - 2)])
                    ins = None if dry else e_.matmul(
                        pST[:, pb * 512:(pb + 1) * 512],
                        ksb[:, kv, (lo + j) * 128:(lo + j + 1) * 128],
                        qsb[qb % 2][:, h, 0:512],
                        start=True, stop=True)
                    inc(ins, "mm", ("stmm", ui, j))
                if not dry:
                    e_.wait_ge(C.sem("ev"), b["ev"] + tab[("exp", ui, nkb - 1)])
                    e_.wait_ge(C.sem("dp"), b["dp"] + tab[("pat", ui)])
                ins = None
                for j in range(nkb):
                    if not dry:
                        ins = e_.matmul(pR[0:1, 0:512], onesc[:, :], pt[:, j, :],
                                        start=(j == 0), stop=(j == nkb - 1))
                inc(ins, "mm", ("summ", ui))
                if not dry:
                    e_.wait_ge(C.sem("x1"), b["x1"] + tab[("recip", ui)])
                ins = None if dry else e_.matmul(pR[:, 0:512], onesf[:, :],
                                                 frow[0:1, 0:512], start=True, stop=True)
                inc(ins, "mm", ("fbbc", ui))
                if not dry:
                    if ui >= 1:
                        e_.wait_ge(C.sem("x1"), b["x1"] + tab[("oev", ui - 1)])
                ins = None
                for j in range(nkb):
                    if not dry:
                        ins = e_.matmul(pO[:, 0:512],
                                        vsb[:, lo + j, kv * 128:(kv + 1) * 128],
                                        pt[:, j, :],
                                        start=(j == 0), stop=(j == nkb - 1))
                inc(ins, "mm", ("pvmm", ui))
        return cnt

    def run_dve(e_, dry):
        cnt, inc, wvf = emit(dry)
        gidx = 0
        for it in range(nu + 1):
            if it < nu:
                ui = it
                qb, h = units[ui]
                nS = len(sgroups[ui])
                for gl, (qt, g, kw) in enumerate(sgroups[ui]):
                    bank = gidx % 4
                    if not dry:
                        e_.wait_ge(C.sem("mm"), b["mm"] + tab[("smm", ui, gl)])
                    ins = None if dry else e_.reduce_max(
                        mst[:, gl:gl + 1], pS[:, bank * 512:bank * 512 + kw * 128],
                        axis=AX.X, negate=True)
                    inc(ins, "x2", ("sred", ui, gl))
                    gidx += 1
                if not dry:
                    e_.wait_ge(C.sem("x2"), b["x2"] + tab[("sred", ui, nS - 1)])
                    if ui >= 1:
                        e_.wait_ge(C.sem("mm"), b["mm"] + tab[("tpose", ui - 1)])
                    C.fence(e_, e_.tensor_reduce(tmpm[:, 0:1], mst[:, 0:nS], AX.X,
                                                 ALU.min), "fv")
                    ins = nc.vector.engine_nop()
                else:
                    ins = None
                inc(ins, "x1", ("colmin", ui))
            if it >= 1:
                ui = it - 1
                qb, h = units[ui]
                lo, hi = blk_kparams(qb)
                nkb = hi - lo
                nS = len(sgroups[ui])
                if not dry:
                    e_.wait_ge(C.sem("mm"), b["mm"] + tab[("tpose", ui)])
                    C.fence(e_, e_.tensor_reduce(nmb_r[0:1, 0:1], pR[0:1, 0:128], AX.X,
                                                 ALU.min), "fv")
                    ins = nc.vector.engine_nop()
                else:
                    ins = None
                inc(ins, "x1", ("rowmin", ui))
                if not dry:
                    e_.wait_ge(C.sem("mm"), b["mm"] + tab[("mbbc", ui)])
                    if ui >= 1:
                        nkp = nkbs[ui - 1]
                        e_.wait_ge(C.sem("ev"), b["ev"] + tab[("exp", ui - 1, nkp - 1)])
                ins = None if dry else e_.tensor_copy(mbk[:, 0:1], pO[:, 0:1])
                inc(ins, "x1", ("mbk", ui))
                npat = 0
                for j in range(nkb):
                    ops = dve_patterns(qb, lo + j)
                    if not ops:
                        continue
                    if not dry:
                        e_.wait_ge(C.sem("ev"), b["ev"] + tab[("exp", ui, j)])
                    for op in ops:
                        if op[0] == "escale":
                            ins = None if dry else e_.tensor_scalar(
                                pt[:, j, op[1]:op[2]], pt[:, j, op[1]:op[2]],
                                E_CONST, None, ALU.mult)
                        elif op[0] == "zero":
                            ins = None if dry else e_.memset(pt[:, j, op[1]:op[2]], 0.0)
                        else:
                            col = op[1]
                            ins = None if dry else e_.tensor_mul(
                                pt[:, j, col:col + 128], pt[:, j, col:col + 128],
                                op[0][:, :])
                        if not dry:
                            C.fence(e_, ins, "fv")
                        npat += 1
                ins = None if dry else nc.vector.engine_nop()
                inc(ins, "dp", ("pat", ui))
                if not dry:
                    e_.wait_ge(C.sem("mm"), b["mm"] + tab[("summ", ui)])
                    C.fence(e_, e_.reciprocal(frow[0:1, 0:512], pR[0:1, 0:512]), "fv")
                    ins = nc.vector.engine_nop()
                else:
                    ins = None
                inc(ins, "x1", ("recip", ui))
                if not dry:
                    e_.wait_ge(C.sem("mm"), b["mm"] + tab[("fbbc", ui)])
                ins = None if dry else e_.tensor_copy(f_b[:, :], pR[:, 0:512])
                inc(ins, "x1", ("fbev", ui))
                if not dry:
                    e_.wait_ge(C.sem("mm"), b["mm"] + tab[("pvmm", ui)])
                    e_.wait_ge(C.sem("x1"), b["x1"] + tab[("fbev", ui)])
                    if ui >= 2:
                        e_.wait_ge(C.sem(f"s{ui % 2}"), b_s[ui % 2] + 16 * (ui // 2))
                ins = None if dry else e_.tensor_tensor(
                    osb[ui % 2][:, :], pO[:, 0:512], f_b[:, :], ALU.mult)
                inc(ins, "x1", ("oev", ui))
        return cnt

    def run_act(e_, dry):
        cnt, inc, wvf = emit(dry)
        for ui in range(nu):
            qb, h = units[ui]
            lo, hi = blk_kparams(qb)
            nkb = hi - lo
            for j in range(nkb):
                pb = j % 2
                if not dry:
                    e_.wait_ge(C.sem("mm"), b["mm"] + tab[("stmm", ui, j)])
                    if j == 0:
                        e_.wait_ge(C.sem("x1"), b["x1"] + tab[("mbk", ui)])
                        if ui >= 1:
                            e_.wait_ge(C.sem("mm"), b["mm"] + tab[("pvmm", ui - 1)])
                ins = None if dry else e_.activation(
                    pt[:, j, :], pST[:, pb * 512:(pb + 1) * 512], AF.Exp,
                    bias=mbk[:, 0:1], scale=1.0)
                inc(ins, "ev", ("exp", ui, j))
            if not dry:
                e_.wait_ge(C.sem("x1"), b["x1"] + tab[("oev", ui)])
                e_.dma_start(o_hbm[h * 128:(h + 1) * 128, qb * 512:(qb + 1) * 512],
                             osb[ui % 2][:, :]).then_inc(C.sem(f"s{ui % 2}"), 16)
        return cnt

    cpe = run_pe(None, True)
    cdv = run_dve(None, True)
    cac = run_act(None, True)

    with nc.Block() as blk:
        @blk.sync
        def _(e_):
            for qb in range(C.NB):
                if qb >= 2:
                    done = len([x for x in units if x[0] <= qb - 2])
                    e_.wait_ge(C.sem("mm"), b["mm"] + tab[("pvmm", done - 1)])
                e_.dma_start(qsb[qb % 2][:, :, :],
                             chunked(q_hbm[:, qb * 512:(qb + 1) * 512])).then_inc(
                                 C.sem(f"q{qb % 2}"), 16)
        @blk.tensor
        def _(e_):
            run_pe(e_, False)
        @blk.vector
        def _(e_):
            run_dve(e_, False)
        @blk.scalar
        def _(e_):
            run_act(e_, False)

    C.bump("q0", 16 * ceil_div(C.NB, 2))
    C.bump("q1", 16 * (C.NB // 2))
    C.bump("mm", cpe["mm"])
    C.bump("x1", cdv["x1"])
    C.bump("x2", cdv["x2"])
    C.bump("dp", cdv["dp"])
    C.bump("ev", cac["ev"])
    C.bump("s0", 16 * ceil_div(nu, 2))
    C.bump("s1", 16 * (nu // 2))
    e0, e1 = C.base("s0"), C.base("s1")
    with nc.Block() as blk:
        @blk.sync
        def _(e_):
            e_.wait_ge(C.sem("s0"), e0)
            e_.wait_ge(C.sem("s1"), e1)


@_scoped
def ph_attention4(C, q_hbm, k_hbm, v_hbm, o_hbm, *, n_heads=16, n_kv=16,
                  mask_kind="causal", window_chunks=8):
    nc, T = C.nc, C.T
    NT = C.NT
    u = C.uid()
    EK = n_kv * 128
    GR = n_heads // n_kv

    def qt_kparams(qt_abs):
        k_lo = max(0, qt_abs - window_chunks) if mask_kind == "window" else 0
        k_hi = NT if mask_kind == "trilbias" else qt_abs + 1
        return k_lo, k_hi

    def blk_kparams(qb):
        hi = NT if mask_kind == "trilbias" else 4 * (qb + 1)
        lo = max(0, 4 * qb - window_chunks) if mask_kind == "window" else 0
        return lo, hi

    punits = []
    head_cols = []
    col = 0
    for h in range(n_heads):
        c0 = col
        for qt_abs in range(NT):
            klo, khi = qt_kparams(qt_abs)
            for g in range(klo, khi, 4):
                punits.append((h, qt_abs, g, min(4, khi - g), col))
                col += 1
        head_cols.append((c0, col - c0))
    ncols = col
    punits.sort(key=lambda p_: (p_[1], p_[0], p_[2]))

    ksb = C.sb(f"a4_k{u}", [128, n_kv, T], BF16)
    vsb = C.sb(f"a4_v{u}", [128, NT, EK], BF16)
    qsb = [C.sb(f"a4_q{u}_{i}", [128, n_heads, 512], BF16) for i in range(2)]
    pt_p = [C.sb(f"a4_ptb{u}_{i}", [128, 4, 512], BF16) for i in range(2)]

    def pts(t):
        return pt_p[t % 2][:, (t // 2) % 4, :]

    osb = [C.sb(f"a4_o{u}_{i}", [128, 512], BF16) for i in range(4)]
    mst = C.sb(f"a4_ms{u}", [128, ncols], F32)
    mtmp = C.sb(f"a4_mt{u}", [128, n_heads], F32)
    mcol = C.sb(f"a4_mc{u}", [n_heads, 1], F32)
    mrow = C.sb(f"a4_mr{u}", [1, n_heads], F32)
    mbk = C.sb(f"a4_mb{u}", [128, n_heads], F32)
    frow = C.sb(f"a4_fr{u}", [1, 1024], F32)
    f_b = C.sb(f"a4_fb{u}", [128, 1024], F32)
    onesc = C.sb(f"a4_on{u}", [128, 1], BF16)
    onesf = C.sb(f"a4_of{u}", [1, 128], F32)
    identf = C.sb(f"a4_if{u}", [128, 128], F32)
    zdiag = C.sb(f"a4_zd{u}", [128, 128], BF16)
    zwin = C.sb(f"a4_zw{u}", [128, 128], BF16)
    ediag = C.sb(f"a4_ed{u}", [128, 128], F32)

    pS = C.psum(f"a4_ps{u}", [128, 2048], F32)
    pR = C.psum(f"a4_pr{u}", [128, 1024], F32)
    pO = C.psum(f"a4_po{u}", [128, 1024], F32)

    E_CONST = float(np.exp(np.float32(1.0)))
    idf_h = C.inline("c_identf", np.eye(128, dtype=np.float32))
    kq = np.arange(128)
    zdiag_h = C.inline("c_zdiag", (kq[:, None] <= kq[None, :]).astype(ml_dtypes.bfloat16))
    zwin_h = C.inline("c_zwin", (kq[:, None] >= kq[None, :]).astype(ml_dtypes.bfloat16))
    ediag_h = C.inline("c_ediagf", np.where(kq[:, None] <= kq[None, :],
                                            np.float32(E_CONST),
                                            np.float32(1.0)).astype(np.float32))

    ld0 = C.base("ld")
    with nc.Block() as blk:
        @blk.sync
        def _(e_):
            e_.dma_start(ksb[:, :, :], chunked(k_hbm[0:EK, :])).then_inc(C.sem("ld"), 16)
            e_.dma_start(vsb[:, :, :],
                         v_hbm[:, 0:EK].rearrange("(c p) t -> p c t", p=128)
                         ).then_inc(C.sem("ld"), 16)
            e_.dma_start(identf[:, :], idf_h[:, :]).then_inc(C.sem("ld"), 16)
            e_.dma_start(zdiag[:, :], zdiag_h[:, :]).then_inc(C.sem("ld"), 16)
            e_.dma_start(zwin[:, :], zwin_h[:, :]).then_inc(C.sem("ld"), 16)
            e_.dma_start(ediag[:, :], ediag_h[:, :]).then_inc(C.sem("ld"), 16)
            e_.dma_start(qsb[0][:, :, :], chunked(q_hbm[:, 0:512])).then_inc(
                C.sem("q0"), 16)
        @blk.gpsimd
        def _(e_):
            e_.memset(onesc[:, :], 1.0)
            e_.memset(onesf[:, :], 1.0)
        @blk.vector
        def _(e_):
            e_.wait_ge(C.sem("ld"), ld0 + 16 * 6)
        @blk.tensor
        def _(e_):
            e_.wait_ge(C.sem("ld"), ld0 + 16 * 6)
        @blk.scalar
        def _(e_):
            e_.wait_ge(C.sem("ld"), ld0 + 16 * 6)
    C.bump("ld", 16 * 6)
    C.bump("q0", 16)
    b_q = [C.base("q0") - 16, C.base("q1")]

    b_mm = C.base("mm")
    b_x2 = C.base("x2")
    with nc.Block() as blk:
        @blk.sync
        def _(e_):
            for qb in range(1, C.NB):
                if qb >= 2:
                    need = len([1 for p_ in punits if p_[1] // 4 <= qb - 2])
                    e_.wait_ge(C.sem("mm"), b_mm + need)
                e_.dma_start(qsb[qb % 2][:, :, :],
                             chunked(q_hbm[:, qb * 512:(qb + 1) * 512])).then_inc(
                                 C.sem(["q0", "q1"][qb % 2]), 16)
        @blk.tensor
        def _(e_):
            for pi, (h, qt_abs, g, kw, cl) in enumerate(punits):
                qb = qt_abs // 4
                kv = h // GR
                e_.wait_ge(C.sem(["q0", "q1"][qb % 2]), b_q[qb % 2] + 16 * (qb // 2 + 1))
                if pi >= 4:
                    e_.wait_ge(C.sem("x2"), b_x2 + pi - 3)
                bank = pi % 4
                e_.matmul(pS[:, bank * 512:bank * 512 + kw * 128],
                          qsb[qb % 2][:, h, (qt_abs % 4) * 128:(qt_abs % 4 + 1) * 128],
                          ksb[:, kv, g * 128:(g + kw) * 128],
                          start=True, stop=True).then_inc(C.sem("mm"), 1)
        @blk.vector
        def _(e_):
            for pi, (h, qt_abs, g, kw, cl) in enumerate(punits):
                bank = pi % 4
                e_.wait_ge(C.sem("mm"), b_mm + pi + 1)
                e_.reduce_max(mst[:, cl:cl + 1], pS[:, bank * 512:bank * 512 + kw * 128],
                              axis=AX.X, negate=True).then_inc(C.sem("x2"), 1)
    C.bump("mm", len(punits))
    C.bump("x2", len(punits))
    nql = [0, 0]
    for qb in range(1, C.NB):
        nql[qb % 2] += 16
    C.bump("q0", nql[0])
    C.bump("q1", nql[1])

    with nc.Block() as blk:
        @blk.vector
        def _(e_):
            for h in range(n_heads):
                c0, cn = head_cols[h]
                e_.tensor_reduce(mtmp[:, h:h + 1], mst[:, c0:c0 + cn], AX.X, ALU.min)
    with nc.Block() as blk:
        @blk.tensor
        def _(e_):
            e_.transpose(pR[0:n_heads, 0:128], mtmp[:, 0:n_heads], identf[:, :])
    with nc.Block() as blk:
        @blk.vector
        def _(e_):
            e_.tensor_reduce(mcol[0:n_heads, 0:1], pR[0:n_heads, 0:128], AX.X, ALU.min)
    with nc.Block() as blk:
        @blk.tensor
        def _(e_):
            e_.transpose(pR[0:1, 0:n_heads], mcol[0:n_heads, 0:1],
                         identf[0:n_heads, 0:n_heads])
    with nc.Block() as blk:
        @blk.vector
        def _(e_):
            e_.tensor_copy(mrow[0:1, 0:n_heads], pR[0:1, 0:n_heads])
    with nc.Block() as blk:
        @blk.tensor
        def _(e_):
            e_.matmul(pR[:, 0:n_heads], onesf[:, :], mrow[0:1, 0:n_heads],
                      start=True, stop=True)
    with nc.Block() as blk:
        @blk.vector
        def _(e_):
            e_.tensor_copy(mbk[:, :], pR[:, 0:n_heads])

    units = [(qb, h) for qb in range(C.NB) for h in range(n_heads)]
    nu = len(units)
    tiles = []
    first_t = {}
    last_t = {}
    for ui, (qb, h) in enumerate(units):
        lo, hi = blk_kparams(qb)
        first_t[ui] = len(tiles)
        for j in range(hi - lo):
            tiles.append((ui, qb, h, j, lo + j, hi - lo))
        last_t[ui] = len(tiles) - 1
    ntiles = len(tiles)
    NS = 8
    LAG = 3
    DELAY = 6

    def dve_patterns(qb, kc):
        ops = []
        if kc - qb * 4 in range(4):
            qt = kc - qb * 4
            ops.append((ediag if mask_kind == "trilbias" else zdiag, qt * 128))
        if mask_kind == "window" and (kc + window_chunks - qb * 4) in range(4):
            ops.append((zwin, (kc + window_chunks - qb * 4) * 128))
        if mask_kind == "trilbias":
            qt_below = [qt for qt in range(4) if kc < qb * 4 + qt]
            if qt_below:
                ops.append(("escale", min(qt_below) * 128, (max(qt_below) + 1) * 128))
        else:
            qt_above = [qt for qt in range(4) if kc > qb * 4 + qt]
            if qt_above:
                ops.append(("zero", min(qt_above) * 128, (max(qt_above) + 1) * 128))
            if mask_kind == "window":
                far = [qt for qt in range(4) if kc < qb * 4 + qt - window_chunks]
                if far:
                    ops.append(("zero", min(far) * 128, (max(far) + 1) * 128))
        return ops

    tab = {}

    def mk(dry):
        cnt = {"mm": 0, "x1": 0, "ev": 0, "dp": 0}

        def inc(ins, role, key=None):
            cnt[role] += 1
            if dry:
                if key is not None:
                    tab[key] = cnt[role]
            else:
                ins.then_inc(C.sem(role), 1)
            return cnt[role]
        return cnt, inc

    b2 = {r: C.base(r) for r in ("mm", "x1", "ev", "dp")}
    b_q2 = [C.base("q0"), C.base("q1")]
    SROLE = ["s0", "s1", "r0", "r1"]
    b_s = [C.base(r) for r in SROLE]

    def run_pe(e_, dry):
        cnt, inc = mk(dry)
        for step in range(ntiles + LAG + 1):
            if step < ntiles:
                ui, qb, h, j, kc, nkb = tiles[step]
                kv = h // GR
                if not dry:
                    if step == first_t[ui] and ui == qb * n_heads:
                        e_.wait_ge(C.sem(["q0", "q1"][qb % 2]),
                                   b_q2[qb % 2] + 16 * (qb // 2 + 1))
                    if step >= 4:
                        e_.wait_ge(C.sem("ev"), b2["ev"] + tab[("exp", step - 4)])
                ins = None if dry else e_.matmul(
                    pS[:, (step % 4) * 512:(step % 4 + 1) * 512],
                    ksb[:, kv, kc * 128:(kc + 1) * 128],
                    qsb[qb % 2][:, h, 0:512],
                    start=True, stop=True)
                inc(ins, "mm", ("stmm", step))
            tf = step - LAG - 1
            if 0 <= tf < ntiles and tf == last_t[tiles[tf][0]]:
                uf = tiles[tf][0]
                p = uf % 2
                if not dry:
                    e_.wait_ge(C.sem("x1"), b2["x1"] + tab[("recip", uf)])
                ins = None if dry else e_.matmul(
                    pR[:, p * 512:(p + 1) * 512], onesf[:, :],
                    frow[0:1, p * 512:p * 512 + 512], start=True, stop=True)
                inc(ins, "mm", ("fbbc", uf))
            t2 = step - LAG
            if 0 <= t2 < ntiles:
                ui, qb, h, j, kc, nkb = tiles[t2]
                kv = h // GR
                p = ui % 2
                if not dry:
                    e_.wait_ge(C.sem("ev"), b2["ev"] + tab[("exp", t2)])
                    e_.wait_ge(C.sem("dp"), b2["dp"] + tab[("pat", t2)])
                    if j == 0 and ui >= 2:
                        e_.wait_ge(C.sem("x1"), b2["x1"] + tab[("fbev", ui - 2)])
                        e_.wait_ge(C.sem("x1"), b2["x1"] + tab[("oev", ui - 2)])
                ins = None if dry else e_.matmul(
                    pR[0:1, p * 512:p * 512 + 512], onesc[:, :], pts(t2),
                    start=(j == 0), stop=(j == nkb - 1), skip_group_check=True)
                inc(ins, "mm", ("summa", t2))
                ins = None if dry else e_.matmul(
                    pO[:, p * 512:(p + 1) * 512],
                    vsb[:, kc, kv * 128:(kv + 1) * 128], pts(t2),
                    start=(j == 0), stop=(j == nkb - 1), skip_group_check=True)
                inc(ins, "mm", ("pvmm", t2))
        return cnt

    def run_act(e_, dry):
        cnt, inc = mk(dry)
        for step in range(ntiles + DELAY + 1):
            if step < ntiles:
                ui, qb, h, j, kc, nkb = tiles[step]
                if not dry:
                    e_.wait_ge(C.sem("mm"), b2["mm"] + tab[("stmm", step)])
                    if step >= NS:
                        e_.wait_ge(C.sem("mm"), b2["mm"] + tab[("pvmm", step - NS)])
                ins = None if dry else e_.activation(
                    pts(step), pS[:, (step % 4) * 512:(step % 4 + 1) * 512],
                    AF.Exp, bias=mbk[:, h:h + 1], scale=1.0)
                inc(ins, "ev", ("exp", step))
            t5 = step - DELAY
            if 0 <= t5 < ntiles and t5 == last_t[tiles[t5][0]]:
                u5 = tiles[t5][0]
                qb5, h5 = units[u5]
                if not dry:
                    e_.wait_ge(C.sem("x1"), b2["x1"] + tab[("oev", u5)])
                    e_.dma_start(
                        o_hbm[h5 * 128:(h5 + 1) * 128, qb5 * 512:(qb5 + 1) * 512],
                        osb[u5 % 4][:, :]).then_inc(C.sem(SROLE[u5 % 4]), 16)
        return cnt

    def run_dve(e_, dry):
        cnt, inc = mk(dry)
        for t in range(ntiles):
            ui, qb, h, j, kc, nkb = tiles[t]
            p = ui % 2
            ops = dve_patterns(qb, kc)
            if ops and not dry:
                e_.wait_ge(C.sem("ev"), b2["ev"] + tab[("exp", t)])
            for op in ops:
                if op[0] == "escale":
                    ins = None if dry else e_.tensor_scalar(
                        pts(t)[:, op[1]:op[2]], pts(t)[:, op[1]:op[2]],
                        E_CONST, None, ALU.mult)
                elif op[0] == "zero":
                    ins = None if dry else e_.memset(pts(t)[:, op[1]:op[2]], 0.0)
                else:
                    col = op[1]
                    ins = None if dry else e_.tensor_mul(
                        pts(t)[:, col:col + 128], pts(t)[:, col:col + 128],
                        op[0][:, :])
                inc(ins, "dp")
            if dry:
                tab[("pat", t)] = cnt["dp"]
            if t == last_t[ui]:
                if not dry:
                    e_.wait_ge(C.sem("mm"), b2["mm"] + tab[("summa", t)])
                    if ui >= 2:
                        e_.wait_ge(C.sem("mm"), b2["mm"] + tab[("fbbc", ui - 2)])
                ins = None if dry else e_.reciprocal(
                    frow[0:1, p * 512:p * 512 + 512], pR[0:1, p * 512:p * 512 + 512])
                inc(ins, "x1", ("recip", ui))
                if not dry:
                    e_.wait_ge(C.sem("mm"), b2["mm"] + tab[("fbbc", ui)])
                ins = None if dry else e_.tensor_copy(
                    f_b[:, p * 512:(p + 1) * 512], pR[:, p * 512:(p + 1) * 512])
                inc(ins, "x1", ("fbev", ui))
                if not dry:
                    e_.wait_ge(C.sem("mm"), b2["mm"] + tab[("pvmm", t)])
                    e_.wait_ge(C.sem("x1"), b2["x1"] + tab[("fbev", ui)])
                    if ui >= 4:
                        e_.wait_ge(C.sem(SROLE[ui % 4]),
                                   b_s[ui % 4] + 16 * (ui // 4))
                ins = None if dry else e_.tensor_tensor(
                    osb[ui % 4][:, :], pO[:, p * 512:(p + 1) * 512],
                    f_b[:, p * 512:(p + 1) * 512], ALU.mult)
                inc(ins, "x1", ("oev", ui))
        return cnt

    cpe = run_pe(None, True)
    cac = run_act(None, True)
    cdv = run_dve(None, True)

    with nc.Block() as blk:
        @blk.sync
        def _(e_):
            for qb in range(C.NB):
                if qb < 2:
                    e_.wait_ge(C.sem(["q0", "q1"][qb % 2]), b_q2[qb % 2])
                else:
                    done_u = (qb - 2 + 1) * n_heads - 1
                    e_.wait_ge(C.sem("mm"), b2["mm"] + tab[("pvmm", last_t[done_u])])
                e_.dma_start(qsb[qb % 2][:, :, :],
                             chunked(q_hbm[:, qb * 512:(qb + 1) * 512])).then_inc(
                                 C.sem(f"q{qb % 2}"), 16)
        @blk.tensor
        def _(e_):
            run_pe(e_, False)
        @blk.scalar
        def _(e_):
            run_act(e_, False)
        @blk.vector
        def _(e_):
            run_dve(e_, False)

    C.bump("q0", 16 * ceil_div(C.NB, 2))
    C.bump("q1", 16 * (C.NB // 2))
    C.bump("mm", cpe["mm"])
    C.bump("x1", cdv["x1"])
    C.bump("dp", cdv["dp"])
    C.bump("ev", cac["ev"])
    for sl in range(4):
        C.bump(SROLE[sl], 16 * len([1 for u_ in range(nu) if u_ % 4 == sl]))
    ends = [C.base(r) for r in SROLE]
    with nc.Block() as blk:
        @blk.sync
        def _(e_):
            for sl in range(4):
                e_.wait_ge(C.sem(SROLE[sl]), ends[sl])
